# version 09
# baseline (speedup 1.0000x reference)
import numpy as np
import ml_dtypes
import concourse.bass as bass
import concourse.mybir as mybir
from concourse.bass_utils import run_bass_kernel_spmd
from contextlib import ExitStack

F32 = mybir.dt.float32
BF16 = mybir.dt.bfloat16
I32 = mybir.dt.int32
AF = mybir.ActivationFunctionType
ALU = mybir.AluOpType

BATCH, SEQ, HID = 4, 4096, 4096
H, PDIM, NDIM, G, KCONV, CS = 128, 64, 128, 8, 4, 256
INTER = H * PDIM
CONVCH = INTER + 2 * G * NDIM
PROJ = INTER + CONVCH + H
EPS = 1e-5

TOK = 2048
HALO = 128
TOKH = TOK + HALO
NCH = TOK // CS
NKT = HID // 128
NITK = INTER // 128

GATE_T0 = 0
HS_T0 = INTER // 128
B_T0 = HS_T0 + 64
C_T0 = B_T0 + 8
DT_T0 = C_T0 + 8

DEBUG_TENSORS = {"hstok", "gate", "bch", "btok", "cch", "hstok_p", "btok_p",
                 "x1s", "h2t", "dtdbg", "sdbg", "gdbg", "dbg2"}


class Ctx:
    def __init__(self, nc, cur, sems):
        self.nc = nc
        self.cur = cur
        self.sems = sems
        self.cnt = {k: 0 for k in sems}

    def eng(self, name):
        return getattr(self.nc, name) if self.cur == name else None

    def inc(self, instr, sem, k):
        self.cnt[sem] += k
        if instr is not None:
            instr.then_inc(self.sems[sem], k)
        return self.cnt[sem]

    def wait(self, engname, sem, val):
        if val <= 0:
            return
        e = self.eng(engname)
        if e is not None:
            e.wait_ge(self.sems[sem], val)

    def v(self, sem):
        return self.cnt[sem]


class BankPool:
    def __init__(self, ctx, banks):
        self.ctx = ctx
        self.banks = banks
        self.free_at = [None] * len(banks)
        self.rr = 0

    def acquire(self, producer_eng, which=None):
        if which is None:
            which = self.rr % len(self.banks)
            self.rr += 1
        fa = self.free_at[which]
        if fa is not None:
            self.ctx.wait(producer_eng, fa[0], fa[1])
            self.free_at[which] = None
        return which

    def release(self, which, sem, val):
        self.free_at[which] = (sem, val)


def build_program(debug=False, stages=4):
    nc = bass.Bass("TRN2", target_bir_lowering=False, debug=False)

    dram = {}

    def din(name, shape, dtype):
        dram[name] = nc.dram_tensor(name, shape, dtype, kind="ExternalInput").ap()

    def dint(name, shape, dtype):
        if debug and name in DEBUG_TENSORS:
            dram[name] = nc.dram_tensor(name, shape, dtype, kind="ExternalOutput").ap()
        else:
            dram[name] = nc.dram_tensor(name, shape, dtype).ap()

    din("x_main", [TOKH, HID], F32)
    din("x_pref", [TOKH, HID], F32)
    din("w1t", [HID, PROJ], F32)
    din("w2t", [INTER, HID], F32)
    din("qw", [HID, HID], I32)
    din("scz", [128, 32, 32], F32)
    din("zrz", [128, 32, 32], F32)
    din("convw", [128, 80, 4], F32)
    din("convb", [128, 80], F32)
    din("dtb", [H, 1], F32)
    din("alog", [H, 1], F32)
    din("dvec", [H, 1], F32)
    din("nw", [128, 64], F32)
    din("fbias", [1, HID], F32)
    din("smask", [128, 1], F32)
    out_hbm = nc.dram_tensor("out", [TOK, HID], F32, kind="ExternalOutput").ap()
    dram["out"] = out_hbm

    dint("hstok", [TOK, INTER], BF16)
    dint("gate", [TOK, INTER], BF16)
    dint("bch", [G * NDIM, TOK], BF16)
    dint("btok", [TOK, G * NDIM], BF16)
    dint("cch", [G * NDIM, TOK], BF16)
    dint("hstok_p", [TOK, INTER], BF16)
    dint("btok_p", [TOK, G * NDIM], BF16)
    dint("cumhbm", [H, CS], F32)
    dint("x1s", [TOK, HID], F32)
    dint("h2t", [HID, TOK], BF16)
    dint("w2b16", [INTER, HID], BF16)
    dint("w1b16", [HID, PROJ], BF16)
    dint("qwb16", [HID, HID], BF16)
    if debug:
        dint("dtdbg", [H, TOKH], F32)
        dint("sdbg", [NDIM, INTER], BF16)
        dint("gdbg", [TOK, INTER], BF16)
        dint("dbg2", [768, INTER], F32)

    es = ExitStack()
    sb = lambda n, s, d: es.enter_context(nc.sbuf_tensor(n, s, d))
    psb = lambda n, s, d: es.enter_context(nc.psum_tensor(n, s, d))

    ident16 = sb("ident16", [128, 128], BF16)
    ident32 = sb("ident32", [128, 128], F32)
    tril16 = sb("tril16", [128, 128], BF16)
    mneg32 = sb("mneg32", [128, 128], F32)
    onesrow = sb("onesrow", [1, 128], F32)
    rowst = sb("rowst", [1, 512], F32)
    conww = sb("conww", [128, 80, 4], F32)
    conbb = sb("conbb", [128, 80], F32)
    acol = sb("acol", [128, 1], F32)
    dtbcol = sb("dtbcol", [128, 1], F32)
    dvcol = sb("dvcol", [128, 1], F32)
    smaskc = sb("smaskc", [128, 1], F32)
    epscol = sb("epscol", [128, 1], F32)
    onecol = sb("onecol", [128, 1], F32)
    drowp = sb("drowp", [1, 128], F32)
    nws = sb("nws", [128, 64], F32)
    colb = sb("colb", [128, 32], F32)
    dtbuf = sb("dtbuf", [128, TOKH], F32)
    S = sb("S", [128, INTER], BF16)


    ARENA_BYTES = 180224
    arena = sb("arena", [128, ARENA_BYTES], mybir.dt.uint8)
    abase = nc.lookup_mloc(arena).addr

    def at(name, shape, dtype, rel):
        t = nc.alloc_sbuf_tensor_at(name, shape, dtype, offset=abase + rel)
        return t

    biga = at("biga", [128, 69632], BF16, 0)
    gTacc = at("gTacc", [128, NITK, 512], BF16, 0)
    h2th = at("h2th", [128, NKT, 1024], BF16, 0)
    hsr = [at(f"hsr{i}", [128, INTER], BF16, 65536 + 16384 * i) for i in range(2)]
    x1q = [at(f"x1q{i}", [128, HID], F32, 65536 + 16384 * i) for i in range(2)]
    qwb = [at(f"qwb{i}", [128, HID], BF16, 65536 + 8192 * i) for i in range(2)]
    gater = [at(f"gater{i}", [128, 1024], BF16, 98304 + 2048 * i) for i in range(2)]
    g0b = at("g0b", [128, INTER], BF16, 102400)
    wT = at("wT", [128, NKT, 512], BF16, 98304)
    cumr = at("cumr", [1, 1024], F32, 118784)
    bchr = [at(f"bchr{i}", [128, CS], BF16, 122880 + 512 * i) for i in range(2)]
    cchr = [at(f"cchr{i}", [128, CS], BF16, 123904 + 512 * i) for i in range(2)]
    btokr = [at(f"btokr{i}", [128, G * NDIM], BF16, 124928 + 2048 * i) for i in range(2)]
    xpc = [at(f"xpc{i}", [128, 512], F32, 129024 + 2048 * i) for i in range(2)]
    wdq = at("wdq", [128, HID], BF16, 131072)
    xbuf = at("xbuf", [128, HID], F32, 139264)
    hbuf = at("hbuf", [128, HID], BF16, 155648)
    xbcst = at("xbcst", [128, TOKH], BF16, 139264)
    convo = at("convo", [128, TOK], F32, 143616)
    spst = at("spst", [128, TOK], BF16, 151808)
    spstB = at("spstB", [128, TOK], BF16, 155904)
    tst = at("tst", [128, 8, 128], BF16, 160000)
    dbc = at("dbc", [128, INTER], BF16, 139264)
    sczb = at("sczb", [128, 32, 32], F32, 139264)
    zrzb = at("zrzb", [128, 32, 32], F32, 143360)
    fbb = at("fbb", [128, HID], BF16, 147456)
    outst = [at(f"outst{i}", [128, 512], F32, 155648 + 2048 * i) for i in range(2)]
    x1p = [at(f"x1p{i}", [128, 512], F32, 159744 + 2048 * i) for i in range(2)]
    tst2 = at("tst2", [128, 16, 128], BF16, 147456)
    oq = [at(f"oq{i}", [128, 512], F32, 118784 + 2048 * i) for i in range(2)]
    w1b = [at(f"w1b{i}", [128, NKT, 128], BF16, 163840 + 8192 * i) for i in range(2)]
    dxb = at("dxb", [128, INTER], BF16, 163840)
    w2p = [at(f"w2p{i}", [128, 8, 512], BF16, 163840 + 8192 * i) for i in range(2)]
    dtsp = at("dtsp", [128, CS], F32, 155648)
    dtt1 = at("dtt1", [128, CS], F32, 156672)
    dtt2 = at("dtt2", [128, CS], F32, 157696)
    cumc = at("cumc", [128, CS], F32, 158720)
    wch = at("wch", [128, CS], F32, 159744)
    sbch = at("sbch", [128, CS], F32, 160768)
    cumt = at("cumt", [128, 2, 128], F32, 161792)
    ectt = at("ectt", [128, 2, 128], F32, 162816)
    wtok = at("wtok", [128, 2, 128], F32, 129024)
    sbtok = at("sbtok", [128, 2, 128], F32, 130048)
    scraw = at("scraw", [128, CS + 128], BF16, 131072)
    decb = [at(f"decb{i}", [128, CS + 128], BF16, 131840 + 768 * i) for i in range(2)]
    schb = [at(f"schb{i}", [128, CS + 128], BF16, 133376 + 768 * i) for i in range(2)]
    yzbuf = at("yzbuf", [128, 512], BF16, 134912)
    cdecc = at("cdecc", [128, 128], F32, 135936)
    cdrow = at("cdrow", [1, 128], F32, 136448)
    rstd8 = at("rstd8", [128, 8], F32, 136960)
    bwb = [at(f"bwb{i}", [128, 2, 16, 128], BF16, 98304 + 8192 * i) for i in range(2)]
    w2st = [at(f"w2st{i}", [128, 1024], F32, 131072 + 4096 * i) for i in range(2)]
    w2cv = [at(f"w2cv{i}", [128, 1024], BF16, 118784 + 2048 * i) for i in range(2)]

    psum = [psb(f"pp{i}", [128, 512], F32) for i in range(8)]

    sem_names = ["xc", "xl", "w1a", "w1b", "spt", "sps", "sp2", "hs0", "hs1",
                 "bt0", "bt1", "bc0", "bc1", "gt0", "gt1", "cmw", "cmr",
                 "w2a", "w2b", "xp0", "xp1", "x1w", "x1r", "h2w", "h2r",
                 "qw0", "qw1", "fbr", "oq0", "oq1", "x1q0", "x1q1", "dout", "dou2", "dbg", "wc0", "wc1", "wcw", "wcx", "cv1", "cv2", "cv3",
                 "te", "ve", "se", "ge"]
    sems = {n: es.enter_context(nc.semaphore(n)) for n in sem_names}

    def program(ctx: Ctx):
        te_ = ctx.eng("tensor")
        ve_ = ctx.eng("vector")
        se_ = ctx.eng("scalar")
        ge_ = ctx.eng("gpsimd")
        sy_ = ctx.eng("sync")
        pool = BankPool(ctx, psum)

        def MM(out, lhsT, rhs, start, stop, inc=None, k=1, tp=None):
            i = None
            if te_ is not None:
                i = te_.matmul(out, lhsT, rhs, start=start, stop=stop,
                               skip_group_check=True, tile_position=tp)
            if inc:
                ctx.inc(i, inc, k)

        def TR(out, in_, ident, inc=None, k=1):
            i = None
            if te_ is not None:
                i = te_.transpose(out, in_, ident)
            if inc:
                ctx.inc(i, inc, k)

        def V(fn, *a, inc=None, k=1, **kw):
            i = fn(ve_, *a, **kw) if ve_ is not None else None
            if inc:
                ctx.inc(i, inc, k)

        def SC(fn, *a, inc=None, k=1, **kw):
            i = fn(se_, *a, **kw) if se_ is not None else None
            if inc:
                ctx.inc(i, inc, k)

        def GP(fn, *a, inc=None, k=1, **kw):
            i = fn(ge_, *a, **kw) if ge_ is not None else None
            if inc:
                ctx.inc(i, inc, k)

        def DMA(eng, out, in_, sem, engobj=None):
            e = ctx.eng(eng)
            i = e.dma_start(out=out, in_=in_) if e is not None else None
            return ctx.inc(i, sem, 16)

        g = ge_
        i1 = g.memset(ident16[:, :], 0.0) if g else None
        ctx.inc(i1, "ge", 1)
        ctx.wait("gpsimd", "ge", ctx.v("ge"))
        i1 = g.affine_select(out=ident16[:, :], in_=ident16[:, :],
                             compare_op=ALU.not_equal, fill=1.0, base=0,
                             pattern=[[-1, 128]], channel_multiplier=1) if g else None
        ctx.inc(i1, "ge", 1)
        i1 = g.memset(ident32[:, :], 0.0) if g else None
        ctx.inc(i1, "ge", 1)
        ctx.wait("gpsimd", "ge", ctx.v("ge"))
        i1 = g.affine_select(out=ident32[:, :], in_=ident32[:, :],
                             compare_op=ALU.not_equal, fill=1.0, base=0,
                             pattern=[[-1, 128]], channel_multiplier=1) if g else None
        ctx.inc(i1, "ge", 1)
        i1 = g.memset(tril16[:, :], 1.0) if g else None
        ctx.inc(i1, "ge", 1)
        ctx.wait("gpsimd", "ge", ctx.v("ge"))
        i1 = g.affine_select(out=tril16[:, :], in_=tril16[:, :],
                             compare_op=ALU.is_ge, fill=0.0, base=0,
                             pattern=[[1, 128]], channel_multiplier=-1) if g else None
        ctx.inc(i1, "ge", 1)
        i1 = g.memset(mneg32[:, :], 0.0) if g else None
        ctx.inc(i1, "ge", 1)
        ctx.wait("gpsimd", "ge", ctx.v("ge"))
        i1 = g.affine_select(out=mneg32[:, :], in_=mneg32[:, :],
                             compare_op=ALU.is_ge, fill=-1e30, base=0,
                             pattern=[[1, 128]], channel_multiplier=-1) if g else None
        ctx.inc(i1, "ge", 1)
        GE_CONST = ctx.v("ge")

        V(lambda e: e.memset(onesrow[:, :], 1.0), inc="ve")
        V(lambda e: e.memset(epscol[:, :], EPS), inc="ve")
        V(lambda e: e.memset(onecol[:, :], 1.0), inc="ve")
        VE_ONES = ctx.v("ve")

        for (dst, src) in [
            (conww[:, :, :], dram["convw"]),
            (conbb[:, :], dram["convb"]),
            (dtbcol[:, :], dram["dtb"]),
            (acol[:, :], dram["alog"]),
            (dvcol[:, :], dram["dvec"]),
            (smaskc[:, :], dram["smask"]),
            (nws[:, :], dram["nw"]),
        ]:
            DMA("sync", dst, src, "xc")
        DX_CONST = ctx.v("xc")

        ctx.wait("scalar", "xc", DX_CONST)
        SC(lambda e: e.activation(acol[:, :], acol[:, :], AF.Exp), inc="se")
        ctx.wait("scalar", "se", ctx.v("se"))
        SC(lambda e: e.mul(acol[:, :], acol[:, :], -1.0), inc="se")
        SE_ACOL = ctx.v("se")

        ctx.wait("tensor", "xc", DX_CONST)
        ctx.wait("tensor", "ve", VE_ONES)
        ctx.wait("tensor", "ge", GE_CONST)
        bk = pool.acquire("tensor")
        TR(psum[bk][0:1, 0:128], dvcol[:, 0:1], ident32[:, :], inc="te")
        ctx.wait("vector", "te", ctx.v("te"))
        V(lambda e: e.tensor_copy(drowp[0:1, 0:128], psum[bk][0:1, 0:128]), inc="ve")
        pool.release(bk, "ve", ctx.v("ve"))
        VE_DROW = ctx.v("ve")

        CV1 = 0

        def emit_convert_piece(i):
            if i < 64:
                i1 = ge_.dma_start(out=dram["w2b16"][128 * i:128 * (i + 1), :],
                                   in_=dram["w2t"][128 * i:128 * (i + 1), :]) if ge_ else None
                ctx.inc(i1, "cv2", 16)
            else:
                j = i - 64
                i1 = ge_.dma_start(out=dram["qwb16"][128 * j:128 * (j + 1), :],
                                   in_=dram["qw"][128 * j:128 * (j + 1), :]) if ge_ else None
                ctx.inc(i1, "cv3", 16)

        def build_dbc():
            for jj in range(16):
                bkd = pool.acquire("tensor")
                ctx.wait("tensor", "ve", VE_DROW)
                rhs = drowp[0:1, 8 * jj:8 * jj + 8].unsqueeze(2).broadcast_to([1, 8, 64])
                MM(psum[bkd][:, 0:512].rearrange("p (a b) -> p a b", a=8),
                   onesrow[0:1, :], rhs, True, True, inc="te")
                ctx.wait("vector", "te", ctx.v("te"))
                V(lambda e, jj=jj, bkd=bkd: e.tensor_copy(
                    dbc[:, 512 * jj:512 * (jj + 1)], psum[bkd][:, 0:512]), inc="ve")
                pool.release(bkd, "ve", ctx.v("ve"))

        def build_fbb():
            for jj in range(8):
                ctx.wait("sync", "te", ctx.v("te"))
                DMA("sync", rowst[0:1, 0:512],
                    dram["fbias"][0:1, 512 * jj:512 * (jj + 1)], "fbr")
                bkd = pool.acquire("tensor")
                ctx.wait("tensor", "fbr", ctx.v("fbr"))
                MM(psum[bkd][:, 0:512], onesrow[0:1, :], rowst[0:1, 0:512],
                   True, True, inc="te")
                ctx.wait("vector", "te", ctx.v("te"))
                V(lambda e, jj=jj, bkd=bkd: e.tensor_copy(
                    fbb[:, 512 * jj:512 * (jj + 1)], psum[bkd][:, 0:512]), inc="ve")
                pool.release(bkd, "ve", ctx.v("ve"))

        def barrier():
            snap = {s: ctx.v(s) for s in sem_names}
            for en in ("sync", "tensor", "vector", "scalar", "gpsimd"):
                for s, v in snap.items():
                    ctx.wait(en, s, v)

        def stageA_ln(xdram):
            HT = biga[:, 0:NKT * TOKH].rearrange("p (kt t) -> p kt t", kt=NKT)
            xload_v = [None] * 17
            xln_done = [0] * 17
            hbuf_free = [0] * 17

            for tt in range(17):
                b = tt % 2
                if tt >= 1:
                    ctx.wait("sync", "se", xln_done[tt - 1])
                xload_v[tt] = DMA("sync", xbuf[:, :],
                                  xdram[128 * tt:128 * (tt + 1), :], "xl")
                ctx.wait("vector", "xl", xload_v[tt])
                ctx.wait("scalar", "xl", xload_v[tt])
                if tt >= 1:
                    ctx.wait("scalar", "te", hbuf_free[tt - 1])
                c0 = 4 * b
                V(lambda e: e.tensor_reduce(colb[:, c0:c0 + 1], xbuf[:, :],
                                            mybir.AxisListType.X, ALU.add), inc="ve")
                SC(lambda e: e.activation(hbuf[:, :], xbuf[:, :], AF.Square,
                                          accum_out=colb[:, c0 + 1:c0 + 2]), inc="se")
                ctx.wait("vector", "se", ctx.v("se"))
                ctx.wait("vector", "ve", ctx.v("ve"))
                V(lambda e: e.tensor_scalar(colb[:, c0:c0 + 1], colb[:, c0:c0 + 1],
                                            1.0 / HID, None, ALU.mult), inc="ve")
                ctx.wait("vector", "ve", ctx.v("ve"))
                V(lambda e: e.scalar_tensor_tensor(colb[:, c0 + 1:c0 + 2],
                                                   colb[:, c0 + 1:c0 + 2], 1.0 / HID,
                                                   colb[:, c0:c0 + 1], ALU.mult,
                                                   ALU.bypass), inc="ve")
                ctx.wait("vector", "ve", ctx.v("ve"))
                V(lambda e: e.tensor_tensor(colb[:, c0 + 2:c0 + 3], colb[:, c0:c0 + 1],
                                            colb[:, c0:c0 + 1], ALU.mult), inc="ve")
                ctx.wait("vector", "ve", ctx.v("ve"))
                V(lambda e: e.tensor_tensor(colb[:, c0 + 1:c0 + 2], colb[:, c0 + 1:c0 + 2],
                                            colb[:, c0 + 2:c0 + 3], ALU.subtract), inc="ve")
                VVAR = ctx.inc(None, "ve", 0)
                ctx.wait("scalar", "ve", ctx.v("ve"))
                SC(lambda e: e.activation(colb[:, c0 + 1:c0 + 2], colb[:, c0 + 1:c0 + 2],
                                          AF.Ln, bias=epscol[:, 0:1]), inc="se")
                ctx.wait("scalar", "se", ctx.v("se"))
                SC(lambda e: e.activation(colb[:, c0 + 1:c0 + 2], colb[:, c0 + 1:c0 + 2],
                                          AF.Exp, scale=-0.5), inc="se")
                ctx.wait("vector", "se", ctx.v("se"))
                V(lambda e: e.tensor_scalar(colb[:, c0 + 2:c0 + 3], colb[:, c0:c0 + 1],
                                            colb[:, c0 + 1:c0 + 2], -1.0,
                                            ALU.mult, ALU.mult), inc="ve")
                ctx.wait("scalar", "ve", ctx.v("ve"))
                SC(lambda e: e.activation(hbuf[:, :], xbuf[:, :], AF.Identity,
                                          bias=colb[:, c0 + 2:c0 + 3],
                                          scale=colb[:, c0 + 1:c0 + 2]), inc="se")
                xln_done[tt] = ctx.v("se")
                ctx.wait("tensor", "se", ctx.v("se"))
                for rr in range(4):
                    bk2 = pool.acquire("tensor")
                    pb = psum[bk2][:, :].bitcast(BF16)
                    for kk in range(8):
                        kt = rr * 8 + kk
                        TR(pb[:, 128 * kk:128 * (kk + 1)],
                           hbuf[:, 128 * kt:128 * (kt + 1)], ident16[:, :],
                           inc="te" if kk == 7 else None)
                    ctx.wait("vector", "te", ctx.v("te"))
                    V(lambda e: e.tensor_copy(
                        HT[:, 8 * rr:8 * rr + 8, 128 * tt:128 * (tt + 1)],
                        pb[:, 0:1024].rearrange("p (a b) -> p a b", a=8)), inc="ve")
                    pool.release(bk2, "ve", ctx.v("ve"))
                hbuf_free[tt] = ctx.v("te")
            return ctx.v("ve")

        def stageB_proj(is_main, ht_ready=0):
            HT = biga[:, 0:NKT * TOKH].rearrange("p (kt t) -> p kt t", kt=NKT)
            ctx.wait("tensor", "ve", ht_ready)
            w1dram = dram["w1t"].rearrange("(kt kl) c -> kl kt c", kl=128)
            hs_d = dram["hstok"] if is_main else dram["hstok_p"]
            bt_d = dram["btok"] if is_main else dram["btok_p"]

            if is_main:
                tiles = ([("hs", HS_T0 + i, i) for i in range(64)]
                         + [("b", B_T0 + i, i) for i in range(8)]
                         + [("c", C_T0 + i, i) for i in range(8)]
                         + [("dt", DT_T0, 0)]
                         + [("gate", GATE_T0 + i, i) for i in range(64)])
            else:
                tiles = ([("hs", HS_T0 + i, i) for i in range(64)]
                         + [("b", B_T0 + i, i) for i in range(8)]
                         + [("dt", DT_T0, 0)])

            w1sem = ["w1a", "w1b"]
            w1_used_at = [0, 0]
            w1_ready = [0, 0]
            for i in range(min(2, len(tiles))):
                kind, ct, rel = tiles[i]
                ctx.wait("gpsimd", "te", w1_used_at[i])
                w1_ready[i] = DMA("gpsimd", w1b[i][:, :, :],
                                  w1dram[:, :, 128 * ct:128 * (ct + 1)], w1sem[i])

            spstp = [spst, spstB]
            xbcst_free = [("ve", 0)]
            convo_free = [("se", 0)]
            spst_free = [[("te", 0), ("sps", 0)], [("te", 0), ("sps", 0)]]
            tstf = [0]
            pending = [None]

            def make_tail(kind, rel, par, silu_se):
                def tail():
                    sp = spstp[par]
                    ctx.wait("tensor", "se", silu_se)
                    if kind == "hs":
                        dstd = hs_d.rearrange("(tt tl) c -> tl tt c", tl=128)[:, :, 128 * rel:128 * (rel + 1)]
                    elif kind == "b":
                        dstd = bt_d.rearrange("(tt tl) c -> tl tt c", tl=128)[:, :, 128 * rel:128 * (rel + 1)]
                    else:
                        dstd = dram["gate"].rearrange("(tt tl) c -> tl tt c", tl=128)[:, :, 128 * rel:128 * (rel + 1)]
                    for rr in range(2):
                        bk3 = pool.acquire("tensor")
                        pb = psum[bk3][:, :].bitcast(BF16)
                        for kk in range(8):
                            ttt = rr * 8 + kk
                            TR(pb[:, 128 * kk:128 * (kk + 1)],
                               sp[:, 128 * ttt:128 * (ttt + 1)], ident16[:, :],
                               inc="te" if kk == 7 else None)
                        if rr == 1:
                            spst_free[par][0] = ("te", ctx.v("te"))
                        ctx.wait("vector", "te", ctx.v("te"))
                        ctx.wait("vector", "spt", tstf[0])
                        V(lambda e, pb=pb: e.tensor_copy(
                            tst[:, :, :],
                            pb[:, 0:1024].rearrange("p (a b) -> p a b", a=8)), inc="ve")
                        pool.release(bk3, "ve", ctx.v("ve"))
                        ctx.wait("sync", "ve", ctx.v("ve"))
                        tstf[0] = DMA("sync", dstd[:, 8 * rr:8 * rr + 8, :],
                                      tst[:, :, :], "spt")
                return tail

            cvn = [0]
            for idx, (kind, ct, rel) in enumerate(tiles):
                if not is_main:
                    for _ in range(2):
                        if cvn[0] < 96:
                            emit_convert_piece(cvn[0])
                            cvn[0] += 1
                wb = idx % 2
                par = idx % 2
                ctx.wait("tensor", w1sem[wb], w1_ready[wb])
                conv = kind in ("hs", "b", "c")
                if kind in ("gate", "dt"):
                    tbs = [(128 + 512 * i, 512) for i in range(4)]
                else:
                    tbs = [(0, 512), (512, 512), (1024, 512), (1536, 512), (2048, 128)]
                unit_banks = []
                for (t0, tw) in tbs:
                    bk2 = pool.acquire("tensor")
                    for kt in range(NKT):
                        MM(psum[bk2][:, 0:tw], w1b[wb][:, kt, :], HT[:, kt, t0:t0 + tw],
                           kt == 0, kt == NKT - 1,
                           inc="te" if kt == NKT - 1 else None)
                    unit_banks.append((bk2, t0, tw, ctx.v("te")))
                w1_used_at[wb] = ctx.v("te")
                if idx + 2 < len(tiles):
                    kind2, ct2, _ = tiles[idx + 2]
                    ctx.wait("gpsimd", "te", w1_used_at[wb])
                    w1_ready[wb] = DMA("gpsimd", w1b[wb][:, :, :],
                                       w1dram[:, :, 128 * ct2:128 * (ct2 + 1)], w1sem[wb])
                if pending[0] is not None:
                    pending[0]()
                    pending[0] = None

                if kind == "dt":
                    for (bk2, t0, tw, tev) in unit_banks:
                        ctx.wait("vector", "te", tev)
                        V(lambda e, bk2=bk2, t0=t0, tw=tw:
                          e.tensor_copy(dtbuf[:, t0:t0 + tw], psum[bk2][:, 0:tw]),
                          inc="ve")
                        pool.release(bk2, "ve", ctx.v("ve"))
                    continue

                if kind == "gate":
                    ctx.wait("scalar", spst_free[par][0][0], spst_free[par][0][1])
                    ctx.wait("scalar", spst_free[par][1][0], spst_free[par][1][1])
                else:
                    ctx.wait("scalar", xbcst_free[0][0], xbcst_free[0][1])
                for (bk2, t0, tw, tev) in unit_banks:
                    ctx.wait("scalar", "te", tev)
                    if kind == "gate":
                        SC(lambda e, bk2=bk2, t0=t0, tw=tw, par=par:
                           e.activation(spstp[par][:, t0 - 128:t0 - 128 + tw],
                                        psum[bk2][:, 0:tw], AF.Silu), inc="se")
                    else:
                        SC(lambda e, bk2=bk2, t0=t0, tw=tw:
                           e.activation(xbcst[:, t0:t0 + tw], psum[bk2][:, 0:tw],
                                        AF.Copy), inc="se")
                    pool.release(bk2, "se", ctx.v("se"))
                EVAC_SE = ctx.v("se")

                if conv:
                    ctx.wait("vector", "se", EVAC_SE)
                    ctx.wait("vector", convo_free[0][0], convo_free[0][1])
                    cti = (ct - HS_T0)
                    V(lambda e: e.tensor_scalar(convo[:, :], xbcst[:, 125:125 + TOK],
                                                conww[:, cti, 0:1], None, ALU.mult),
                      inc="ve")
                    for kk in range(1, 4):
                        ctx.wait("vector", "ve", ctx.v("ve"))
                        V(lambda e, kk=kk: e.scalar_tensor_tensor(
                            convo[:, :], xbcst[:, 125 + kk:125 + kk + TOK],
                            conww[:, cti, kk:kk + 1], convo[:, :], ALU.mult, ALU.add),
                          inc="ve")
                    xbcst_free[0] = ("ve", ctx.v("ve"))
                    ctx.wait("scalar", "ve", ctx.v("ve"))
                    ctx.wait("scalar", spst_free[par][0][0], spst_free[par][0][1])
                    ctx.wait("scalar", spst_free[par][1][0], spst_free[par][1][1])
                    SC(lambda e, par=par: e.activation(spstp[par][:, :], convo[:, :],
                                                       AF.Silu,
                                                       bias=conbb[:, cti:cti + 1]),
                       inc="se")
                    convo_free[0] = ("se", ctx.v("se"))
                SILU_SE = ctx.v("se")

                if kind in ("c", "b"):
                    ctx.wait("sync", "se", SILU_SE)
                    chd = dram["cch"] if kind == "c" else dram["bch"]
                    spsem = ["sps", "sp2"][par]
                    dv = DMA("sync", chd[128 * rel:128 * (rel + 1), :],
                             spstp[par][:, :], spsem)
                    spst_free[par][1] = (spsem, dv)
                    if kind == "c":
                        continue
                pending[0] = make_tail(kind, rel, par, SILU_SE)
            if pending[0] is not None:
                pending[0]()
                pending[0] = None
            return

        def dt_pipeline(c, is_main):
            t0 = 128 + c * CS
            src = dtbuf[:, t0:t0 + CS]
            ctx.wait("scalar", "ve", dt_ready[0])
            SC(lambda e: e.activation(dtt1[:, :], src, AF.Abs, bias=dtbcol[:, 0:1]), inc="se")
            ctx.wait("scalar", "se", ctx.v("se"))
            SC(lambda e: e.activation(dtt1[:, :], dtt1[:, :], AF.Exp, scale=-1.0), inc="se")
            ctx.wait("scalar", "se", ctx.v("se"))
            SC(lambda e: e.activation(dtt1[:, :], dtt1[:, :], AF.Ln, bias=onecol[:, 0:1]), inc="se")
            ctx.wait("scalar", "se", ctx.v("se"))
            SC(lambda e: e.activation(dtt2[:, :], src, AF.Relu, bias=dtbcol[:, 0:1]), inc="se")
            ctx.wait("vector", "se", ctx.v("se"))
            V(lambda e: e.tensor_tensor(dtsp[:, :], dtt1[:, :], dtt2[:, :], ALU.add), inc="ve")
            ctx.wait("vector", "ve", ctx.v("ve"))
            ctx.wait("vector", "se", SE_ACOL)
            V(lambda e: e.tensor_scalar(dtt1[:, :], dtsp[:, :], acol[:, 0:1], None,
                                        ALU.mult), inc="ve")
            ctx.wait("vector", "ve", ctx.v("ve"))
            V(lambda e: e.tensor_tensor_scan(cumc[:, :], dtt1[:, :], dtt1[:, :], 0.0,
                                             ALU.add, ALU.bypass), inc="ve")
            VE_CUM = ctx.v("ve")
            ctx.wait("scalar", "ve", VE_CUM)
            SC(lambda e: e.activation(dtt2[:, :], cumc[:, :], AF.Exp, scale=-1.0,
                                      bias=cumc[:, CS - 1:CS]), inc="se")
            ctx.wait("vector", "se", ctx.v("se"))
            V(lambda e: e.tensor_tensor(wch[:, :], dtsp[:, :], dtt2[:, :], ALU.mult), inc="ve")
            VE_W = ctx.v("ve")
            if is_main:
                SC(lambda e: e.activation(dtt1[:, :], dtsp[:, :], AF.Ln), inc="se")
                ctx.wait("vector", "se", ctx.v("se"))
                V(lambda e: e.tensor_tensor(sbch[:, :], dtt1[:, :], cumc[:, :],
                                            ALU.subtract), inc="ve")
            VE_SB = ctx.v("ve")
            ctx.wait("tensor", "ve", ctx.v("ve"))
            trs = [(cumc, cumt), (wch, wtok)] + ([(sbch, sbtok)] if is_main else [])
            for (srcb, dstb) in trs:
                bk = pool.acquire("tensor")
                for jt in range(2):
                    TR(psum[bk][:, 128 * jt:128 * (jt + 1)],
                       srcb[:, 128 * jt:128 * (jt + 1)], ident32[:, :],
                       inc="te" if jt == 1 else None)
                ctx.wait("vector", "te", ctx.v("te"))
                V(lambda e, dstb=dstb, bk=bk: e.tensor_copy(
                    dstb[:, :, :].rearrange("p a b -> p (a b)"), psum[bk][:, 0:256]),
                  inc="ve")
                pool.release(bk, "ve", ctx.v("ve"))
            VE_TOK = ctx.v("ve")
            if is_main:
                ctx.wait("scalar", "ve", VE_TOK)
                SC(lambda e: e.activation(
                    ectt[:, :, :].rearrange("p a b -> p (a b)"),
                    cumt[:, :, :].rearrange("p a b -> p (a b)"), AF.Exp), inc="se")
            bk = pool.acquire("tensor")
            ctx.wait("tensor", "ve", VE_CUM)
            TR(psum[bk][0:1, 0:128], cumc[:, CS - 1:CS], ident32[:, :], inc="te")
            ctx.wait("scalar", "te", ctx.v("te"))
            SC(lambda e, bk=bk: e.activation(cdrow[0:1, :], psum[bk][0:1, 0:128],
                                             AF.Exp), inc="se")
            pool.release(bk, "se", ctx.v("se"))
            bk = pool.acquire("tensor")
            ctx.wait("tensor", "se", ctx.v("se"))
            MM(psum[bk][:, 0:128], onesrow[0:1, :], cdrow[0:1, :], True, True, inc="te")
            ctx.wait("vector", "te", ctx.v("te"))
            V(lambda e, bk=bk: e.tensor_copy(cdecc[:, :], psum[bk][:, 0:128]), inc="ve")
            pool.release(bk, "ve", ctx.v("ve"))
            if is_main:
                ctx.wait("sync", "cmr", cum_rows_read[0])
                ctx.wait("sync", "ve", VE_CUM)
                DMA("sync", dram["cumhbm"][:, :], cumc[:, :], "cmw")
            return ctx.v("ve")

        dt_ready = [0]
        cum_rows_read = [0]

        w2c_state = {"ld": [0, 0], "cv": [0, 0], "st": [0, 0]}

        def w2conv_load(i):
            kt, cq = i // 4, i % 4
            par = i % 2
            ldsem = ["wc0", "wc1"][par]
            ctx.wait("sync", "se", w2c_state["cv"][par])
            w2c_state["ld"][par] = DMA(
                "sync", w2st[par][:, :],
                dram["w2t"][128 * kt:128 * (kt + 1), 1024 * cq:1024 * (cq + 1)], ldsem)

        def w2conv_piece(i):
            kt, cq = i // 4, i % 4
            par = i % 2
            stsem = ["wcw", "wcx"][par]
            ldsem = ["wc0", "wc1"][par]
            ctx.wait("scalar", ldsem, w2c_state["ld"][par])
            ctx.wait("scalar", stsem, w2c_state["st"][par])
            SC(lambda e, par=par: e.activation(w2cv[par][:, :], w2st[par][:, :],
                                               AF.Identity), inc="se")
            w2c_state["cv"][par] = ctx.v("se")
            ctx.wait("scalar", "se", ctx.v("se"))
            w2c_state["st"][par] = DMA(
                "scalar", dram["w2b16"][128 * kt:128 * (kt + 1), 1024 * cq:1024 * (cq + 1)],
                w2cv[par][:, :], stsem)

        def ssd_prefix():
            V(lambda e: e.memset(S[:, :], 0.0), inc="ve")
            for c in range(NCH):
                barrier()
                t0 = c * CS
                v_hs = [DMA("sync", hsr[it][:, :],
                            dram["hstok_p"][t0 + 128 * it:t0 + 128 * (it + 1), :],
                            ["hs0", "hs1"][it]) for it in range(2)]
                v_bt = [DMA("sync", btokr[jt][:, :],
                            dram["btok_p"][t0 + 128 * jt:t0 + 128 * (jt + 1), :],
                            ["bt0", "bt1"][jt]) for jt in range(2)]
                dt_pipeline(c, is_main=False)
                ssd_states_and_update(c, v_hs, v_bt, is_main=False)
            barrier()
            ctx.wait("vector", "xc", DX_CONST)
            V(lambda e: e.tensor_scalar(S[:, :], S[:, :], smaskc[:, 0:1], None,
                                        ALU.mult), inc="ve")

        def ssd_states_and_update(c, v_hs, v_bt, is_main):
            VE_PRE = ctx.v("ve")
            bw_used = [0, 0]
            bw_ready = [0, 0]

            def emit_bw(gg):
                par = gg % 2
                for jt in range(2):
                    ctx.wait("vector", "ve", VE_PRE)
                    ctx.wait("vector", ["bt0", "bt1"][jt], v_bt[jt])
                    ctx.wait("vector", "te", bw_used[par])
                    ctx.wait("vector", "ve", ctx.v("ve"))
                    V(lambda e, par=par, gg=gg, jt=jt: e.tensor_tensor(
                        bwb[par][:, jt, :, :],
                        btokr[jt][:, 128 * gg:128 * (gg + 1)].unsqueeze(1)
                        .broadcast_to([128, 16, 128]),
                        wtok[:, jt, 16 * gg:16 * (gg + 1)].unsqueeze(2)
                        .broadcast_to([128, 16, 128]), ALU.mult), inc="ve")
                bw_ready[par] = ctx.v("ve")

            emit_bw(0)
            for gg in range(G):
                par = gg % 2
                st_banks = [pool.acquire("tensor") for _ in range(2)]
                ctx.wait("tensor", "ve", bw_ready[par])
                if gg == 0:
                    ctx.wait("tensor", "hs0", v_hs[0])
                    ctx.wait("tensor", "hs1", v_hs[1])
                for jt in range(2):
                    for hslot in range(16):
                        hh = 16 * gg + hslot
                        stb = st_banks[hslot // 8]
                        MM(psum[stb][:, 64 * (hslot % 8):64 * (hslot % 8 + 1)],
                           bwb[par][:, jt, hslot, :],
                           hsr[jt][:, 64 * hh:64 * (hh + 1)],
                           (jt == 0 and hslot % 8 == 0), (jt == 1 and hslot % 8 == 7),
                           inc="te" if (hslot % 8 == 7) else None)
                bw_used[par] = ctx.v("te")
                TE_ST = ctx.v("te")
                if gg + 1 < G:
                    emit_bw(gg + 1)
                ctx.wait("vector", "te", TE_ST)
                ctx.wait("vector", "ve", ctx.v("ve"))
                V(lambda e, gg=gg: e.tensor_tensor(
                    S[:, 1024 * gg:1024 * (gg + 1)].rearrange("p (a b) -> p a b", a=16),
                    S[:, 1024 * gg:1024 * (gg + 1)].rearrange("p (a b) -> p a b", a=16),
                    cdecc[:, 16 * gg:16 * (gg + 1)].unsqueeze(2)
                    .broadcast_to([128, 16, 64]), ALU.mult), inc="ve")
                for hg in range(2):
                    stb = st_banks[hg]
                    ctx.wait("vector", "ve", ctx.v("ve"))
                    V(lambda e, gg=gg, hg=hg, stb=stb: e.tensor_tensor(
                        S[:, 1024 * gg + 512 * hg:1024 * gg + 512 * (hg + 1)],
                        S[:, 1024 * gg + 512 * hg:1024 * gg + 512 * (hg + 1)],
                        psum[stb][:, 0:512], ALU.add), inc="ve")
                    pool.release(stb, "ve", ctx.v("ve"))

        def ln2_tile(src_f32, dst_bf16, c0):
            V(lambda e: e.tensor_reduce(colb[:, c0:c0 + 1], src_f32,
                                        mybir.AxisListType.X, ALU.add), inc="ve")
            SC(lambda e: e.activation(dst_bf16, src_f32, AF.Square,
                                      accum_out=colb[:, c0 + 1:c0 + 2]), inc="se")
            ctx.wait("vector", "se", ctx.v("se"))
            ctx.wait("vector", "ve", ctx.v("ve"))
            V(lambda e: e.tensor_scalar(colb[:, c0:c0 + 1], colb[:, c0:c0 + 1],
                                        1.0 / HID, None, ALU.mult), inc="ve")
            ctx.wait("vector", "ve", ctx.v("ve"))
            V(lambda e: e.scalar_tensor_tensor(colb[:, c0 + 1:c0 + 2],
                                               colb[:, c0 + 1:c0 + 2], 1.0 / HID,
                                               colb[:, c0:c0 + 1], ALU.mult,
                                               ALU.bypass), inc="ve")
            ctx.wait("vector", "ve", ctx.v("ve"))
            V(lambda e: e.tensor_tensor(colb[:, c0 + 2:c0 + 3], colb[:, c0:c0 + 1],
                                        colb[:, c0:c0 + 1], ALU.mult), inc="ve")
            ctx.wait("vector", "ve", ctx.v("ve"))
            V(lambda e: e.tensor_tensor(colb[:, c0 + 1:c0 + 2], colb[:, c0 + 1:c0 + 2],
                                        colb[:, c0 + 2:c0 + 3], ALU.subtract), inc="ve")
            ctx.wait("scalar", "ve", ctx.v("ve"))
            SC(lambda e: e.activation(colb[:, c0 + 1:c0 + 2], colb[:, c0 + 1:c0 + 2],
                                      AF.Ln, bias=epscol[:, 0:1]), inc="se")
            ctx.wait("scalar", "se", ctx.v("se"))
            SC(lambda e: e.activation(colb[:, c0 + 1:c0 + 2], colb[:, c0 + 1:c0 + 2],
                                      AF.Exp, scale=-0.5), inc="se")
            ctx.wait("vector", "se", ctx.v("se"))
            V(lambda e: e.tensor_scalar(colb[:, c0 + 2:c0 + 3], colb[:, c0:c0 + 1],
                                        colb[:, c0 + 1:c0 + 2], -1.0,
                                        ALU.mult, ALU.mult), inc="ve")
            ctx.wait("scalar", "ve", ctx.v("ve"))
            SC(lambda e: e.activation(dst_bf16, src_f32, AF.Identity,
                                      bias=colb[:, c0 + 2:c0 + 3],
                                      scale=colb[:, c0 + 1:c0 + 2]), inc="se")
            return ctx.v("se")

        def out_quarter(q):
            barrier()
            w2dram = dram["w2b16"].rearrange("(kt kl) c -> kl kt c", kl=128)
            w2sem = ["w2a", "w2b"]
            w2_ready = [0, 0]
            w2_used = [0, 0]
            xpsem = ["xp0", "xp1"]
            xp_ready = [0, 0]
            xp_used = [0, 0]
            oqsem = ["oq0", "oq1"]
            oq_st = [0, 0]
            oq_used = [0, 0]

            def load_w2(slot, hb, kt8):
                ctx.wait("sync", "te", w2_used[slot])
                w2_ready[slot] = DMA(
                    "sync", w2p[slot][:, :, :],
                    w2dram[:, 8 * kt8:8 * kt8 + 8, 512 * hb:512 * (hb + 1)],
                    w2sem[slot])

            xunits = [(hb, tq) for hb in range(8) for tq in range(4)]

            def load_xp(slot, hb, tq):
                ctx.wait("sync", "ve", xp_used[slot])
                xp_ready[slot] = DMA(
                    "sync", xpc[slot][:, :],
                    dram["x_main"][HALO + 512 * q + 128 * tq:
                                   HALO + 512 * q + 128 * (tq + 1),
                                   512 * hb:512 * (hb + 1)], xpsem[slot])

            units = [(hb, kt8) for hb in range(8) for kt8 in range(8)]
            load_w2(0, 0, 0)
            load_w2(1, 0, 1)
            for i in range(2):
                load_xp(i, *xunits[i])
            for ui, (hb, kt8) in enumerate(units):
                slot = ui % 2
                ctx.wait("tensor", w2sem[slot], w2_ready[slot])
                if kt8 == 0:
                    hb_banks = [pool.acquire("tensor") for _ in range(4)]
                for tq in range(4):
                    for kl in range(8):
                        kt = 8 * kt8 + kl
                        MM(psum[hb_banks[tq]][:, 0:512],
                           gTacc[:, kt, 128 * tq:128 * (tq + 1)],
                           w2p[slot][:, kl, :], kt == 0, kt == 63,
                           inc="te" if (kl == 7 and tq == 3) else None)
                w2_used[slot] = ctx.v("te")
                if ui + 2 < len(units):
                    load_w2(slot, *units[ui + 2])
                if kt8 == 7:
                    TE_HB = ctx.v("te")
                    for tq in range(4):
                        ci = hb * 4 + tq
                        xslot = ci % 2
                        opar = ci % 2
                        ctx.wait("vector", "te", TE_HB)
                        ctx.wait("vector", xpsem[xslot], xp_ready[xslot])
                        ctx.wait("vector", oqsem[opar], oq_used[opar])
                        V(lambda e, tq=tq, xslot=xslot, opar=opar: e.tensor_tensor(
                            oq[opar][:, :], psum[hb_banks[tq]][:, 0:512],
                            xpc[xslot][:, :], ALU.add), inc="ve")
                        xp_used[xslot] = ctx.v("ve")
                        pool.release(hb_banks[tq], "ve", ctx.v("ve"))
                        if ci + 2 < len(xunits):
                            load_xp(xslot, *xunits[ci + 2])
                        ctx.wait("sync", "ve", ctx.v("ve"))
                        oq_used[opar] = DMA(
                            "sync",
                            dram["x1s"][512 * q + 128 * tq:512 * q + 128 * (tq + 1),
                                        512 * hb:512 * (hb + 1)],
                            oq[opar][:, :], oqsem[opar])
            ctx.wait("sync", oqsem[0], oq_used[0])
            ctx.wait("sync", oqsem[1], oq_used[1])
            x1qsem = ["x1q0", "x1q1"]
            x1q_ready = [0, 0]
            for tq in range(4):
                tok0 = 512 * q + 128 * tq
                par = tq % 2
                if tq >= 2:
                    ctx.wait("sync", "se", x1q_done[par])
                x1q_ready[par] = DMA("sync", x1q[par][:, :],
                                     dram["x1s"][tok0:tok0 + 128, :], x1qsem[par])
                ctx.wait("vector", x1qsem[par], x1q_ready[par])
                ctx.wait("scalar", x1qsem[par], x1q_ready[par])
                if tq >= 1:
                    ctx.wait("scalar", "te", h2_tpose_done[0])
                se_h2 = ln2_tile(x1q[par][:, :], hbuf[:, :], 8)
                x1q_done[par] = se_h2
                ctx.wait("tensor", "se", se_h2)
                for half in range(2):
                    for rr in range(2):
                        bk2 = pool.acquire("tensor")
                        pb = psum[bk2][:, :].bitcast(BF16)
                        for kk in range(8):
                            kt = 16 * half + rr * 8 + kk
                            TR(pb[:, 128 * kk:128 * (kk + 1)],
                               hbuf[:, 128 * kt:128 * (kt + 1)], ident16[:, :],
                               inc="te" if kk == 7 else None)
                        ctx.wait("vector", "te", ctx.v("te"))
                        ctx.wait("vector", "h2w", h2w_done[0])
                        V(lambda e, rr=rr, half=half, pb=pb: e.tensor_copy(
                            tst2[:, 8 * rr:8 * rr + 8, :],
                            pb[:, 0:1024].rearrange("p (a b) -> p a b", a=8)), inc="ve")
                        pool.release(bk2, "ve", ctx.v("ve"))
                        if rr == 1:
                            ctx.wait("sync", "ve", ctx.v("ve"))
                            h2w_done[0] = DMA(
                                "sync",
                                dram["h2t"].rearrange("(kh kl) t -> kl kh t", kl=128)
                                [:, 16 * half:16 * half + 16, tok0:tok0 + 128],
                                tst2[:, :, :], "h2w")
                h2_tpose_done[0] = ctx.v("te")

        x1q_done = [0, 0]
        h2_tpose_done = [0]
        h2w_done = [0]

        def ssd_main():
            barrier()
            build_dbc()
            if debug:
                import os as _os
                if _os.environ.get("DBG_STOP") == "dbc":
                    ctx.wait("gpsimd", "ve", ctx.v("ve"))
                    DMA("gpsimd", dram["dbg2"][0:128, :], dbc[:, :], "dbg")
                    return
            for c in range(NCH):
                barrier()
                t0 = c * CS
                v_hs = [DMA("sync", hsr[it][:, :],
                            dram["hstok"][t0 + 128 * it:t0 + 128 * (it + 1), :],
                            ["hs0", "hs1"][it]) for it in range(2)]
                v_bt = [DMA("sync", btokr[jt][:, :],
                            dram["btok"][t0 + 128 * jt:t0 + 128 * (jt + 1), :],
                            ["bt0", "bt1"][jt]) for jt in range(2)]
                dt_pipeline(c, is_main=True)
                VE_DT = ctx.v("ve")
                SE_DT = ctx.v("se")
                cmw_v = ctx.v("cmw")
                for it in range(2):
                    ctx.wait("vector", ["hs0", "hs1"][it], v_hs[it])
                    ctx.wait("vector", "te", dxb_free[0])
                    V(lambda e, it=it: e.tensor_tensor(dxb[:, :], hsr[it][:, :],
                                                       dbc[:, :], ALU.mult), inc="ve")
                    VE_DX = ctx.v("ve")
                    for gg in range(G):
                        par = gg % 2
                        ctx.wait("sync", "te", bc_used[par])
                        vbc = DMA("sync", bchr[par][:, :],
                                  dram["bch"][128 * gg:128 * (gg + 1), t0:t0 + CS],
                                  ["bc0", "bc1"][par])
                        vbc = DMA("sync", cchr[par][:, :],
                                  dram["cch"][128 * gg:128 * (gg + 1), t0:t0 + CS],
                                  ["bc0", "bc1"][par])
                        ctx.wait("sync", gt_used[par][0], gt_used[par][1])
                        vgt = DMA("sync", gater[par][:, :],
                                  dram["gate"][t0 + 128 * it:t0 + 128 * (it + 1),
                                               1024 * gg:1024 * (gg + 1)],
                                  ["gt0", "gt1"][par])
                        bsc = pool.acquire("tensor", which=0)
                        ctx.wait("tensor", ["bc0", "bc1"][par], vbc)
                        njt = it + 1
                        MM(psum[0][:, 0:128], bchr[par][:, 0:128],
                           cchr[par][:, 128 * it:128 * (it + 1)], True, njt == 1,
                           inc="te" if njt == 1 else None)
                        if it == 1:
                            MM(psum[0][:, 128:256], bchr[par][:, 128:256],
                               cchr[par][:, 128:256], False, True, inc="te")
                        bc_used[par] = ctx.v("te")
                        TE_SC = ctx.v("te")
                        ctx.wait("vector", "te", TE_SC)
                        V(lambda e, it=it: e.tensor_copy(
                            scraw[:, 0:128 * (it + 1)],
                            psum[0][:, 0:128 * (it + 1)]), inc="ve")
                        pool.release(0, "ve", ctx.v("ve"))
                        VE_SCRAW = ctx.v("ve")
                        yb = [3, 4]
                        zb = [5, 6]
                        TE_DB = [0, 0]

                        def dec_stage1(hp):
                            dpar = hp % 2
                            hh = 2 * hp
                            if hh % 4 == 0:
                                ctx.wait("sync", "te", cumr_used[0])
                                ctx.wait("sync", "cmw", cmw_v)
                                sq = hh // 4
                                cum_flat = dram["cumhbm"].rearrange("h t -> (h t)")
                                cumr_v[0] = DMA("sync", cumr[0:1, :],
                                                cum_flat[256 * 4 * sq:256 * 4 * (sq + 1)].unsqueeze(0),
                                                "cmr")
                            pool.acquire("tensor", which=1 + dpar)
                            ctx.wait("tensor", "cmr", cumr_v[0])
                            base = (hh % 4) * 256 + 128 * it
                            crow2 = cumr[0:1, base:base + 256 + 128].rearrange(
                                "o (a b) -> o a b", a=3)[:, 0::2, :]
                            if it == 0:
                                MM(psum[1 + dpar][:, 0:256], onesrow[0:1, :], crow2,
                                   True, False)
                                MM(psum[1 + dpar][:, 0:256], ident32[:, :],
                                   mneg32[:, :].unsqueeze(1).broadcast_to([128, 2, 128]),
                                   False, True, inc="te")
                            else:
                                MM(psum[1 + dpar][:, 0:256], onesrow[0:1, :], crow2,
                                   True, False)
                                MM(psum[1 + dpar][:, 256:512], onesrow[0:1, :], crow2,
                                   False, False)
                                MM(psum[1 + dpar][:, 256:512], ident32[:, :],
                                   mneg32[:, :].unsqueeze(1).broadcast_to([128, 2, 128]),
                                   False, True, inc="te")
                            cumr_used[0] = ctx.v("te")
                            TE_DB[dpar] = ctx.v("te")

                        def dec_stage2(hh, hg):
                            dpar = (hh // 2) % 2
                            odd = hh % 2
                            hh8 = hh % 8
                            ctx.wait("scalar", "te", TE_DB[dpar])
                            ctx.wait("scalar", "ve", VE_DT)
                            ctx.wait("scalar", "ve", dec_v_used[dpar])
                            for jt in range(it + 1):
                                if it == 0:
                                    po = 128 * odd
                                else:
                                    po = 128 * odd + (256 if jt == 1 else 0)
                                SC(lambda e, dpar=dpar, jt=jt, hh=hh, po=po: e.activation(
                                    decb[dpar][:, 128 * jt:128 * (jt + 1)],
                                    psum[1 + dpar][:, po:po + 128], AF.Exp,
                                    bias=sbtok[:, jt, hh:hh + 1]), inc="se")
                            if odd == 1:
                                pool.release(1 + dpar, "se", ctx.v("se"))
                                dec_used[dpar] = 0
                            SE_DEC = ctx.v("se")
                            ctx.wait("vector", "se", SE_DEC)
                            ctx.wait("vector", "ve", VE_SCRAW)
                            ctx.wait("vector", "te", sch_used[hh % 2])
                            ctx.wait("vector", "ve", dec_v_used[dpar])
                            V(lambda e, dpar=dpar, it=it, hh=hh: e.tensor_tensor(
                                schb[hh % 2][:, 0:128 * (it + 1)],
                                scraw[:, 0:128 * (it + 1)],
                                decb[dpar][:, 0:128 * (it + 1)], ALU.mult),
                              inc="ve")
                            dec_v_used[dpar] = ctx.v("ve")
                            SCH_V = ("ve", ctx.v("ve"))
                            ctx.wait("tensor", SCH_V[0], SCH_V[1])
                            for jt in range(it + 1):
                                ctx.wait("tensor", ["hs0", "hs1"][jt], v_hs[jt])
                                MM(psum[yb[hg]][:, 64 * hh8:64 * (hh8 + 1)],
                                   schb[hh % 2][:, 128 * jt:128 * (jt + 1)],
                                   hsr[jt][:, 64 * hh:64 * (hh + 1)],
                                   False, False, inc="te")
                            sch_used[hh % 2] = ctx.v("te")
                        dec_stage1(8 * gg)
                        for hg in range(2):
                            pool.acquire("tensor", which=yb[hg])
                            pool.acquire("tensor", which=zb[hg])
                            h0 = 16 * gg + 8 * hg
                            ctx.wait("tensor", "ve", VE_DX)
                            MM(psum[yb[hg]][:, 0:512], ident16[:, :],
                               dxb[:, 64 * h0:64 * h0 + 512], True, False, inc="te")
                            ctx.wait("tensor", "ve", s_ready[0])
                            MM(psum[zb[hg]][:, 0:512],
                               cchr[par][:, 128 * it:128 * (it + 1)],
                               S[:, 64 * h0:64 * h0 + 512], True, True, inc="te")
                            bc_used[par] = ctx.v("te")
                            for hh8 in range(8):
                                hh = h0 + hh8
                                if hh % 2 == 1 and hh + 1 < 16 * (gg + 1):
                                    dec_stage1((hh + 1) // 2)
                                dec_stage2(hh, hg)
                            TE_YZ = ctx.v("te")
                            ctx.wait("vector", "te", TE_YZ)
                            ctx.wait("vector", "se", SE_DT)
                            ctx.wait("vector", "ve", ctx.v("ve"))
                            V(lambda e, hg=hg, it=it, h0=h0: e.tensor_tensor(
                                yzbuf[:, 0:512].rearrange("p (a b) -> p a b", a=8),
                                psum[zb[hg]][:, 0:512].rearrange("p (a b) -> p a b", a=8),
                                ectt[:, it, h0:h0 + 8].unsqueeze(2).broadcast_to([128, 8, 64]),
                                ALU.mult), inc="ve")
                            pool.release(zb[hg], "ve", ctx.v("ve"))
                            ctx.wait("vector", "ve", ctx.v("ve"))
                            V(lambda e, hg=hg, h0=h0: e.scalar_tensor_tensor(
                                g0b[:, 64 * h0:64 * h0 + 512],
                                psum[yb[hg]][:, 0:512], 1.0, yzbuf[:, 0:512],
                                ALU.mult, ALU.add), inc="ve")
                            pool.release(yb[hg], "ve", ctx.v("ve"))
                            G0_V = ctx.v("ve")
                            ctx.wait("vector", ["gt0", "gt1"][par], vgt)
                            ctx.wait("vector", "ve", G0_V)
                            V(lambda e, h0=h0, hg=hg, par=par: e.tensor_tensor(
                                g0b[:, 64 * h0:64 * h0 + 512],
                                g0b[:, 64 * h0:64 * h0 + 512],
                                gater[par][:, 512 * hg:512 * (hg + 1)], ALU.mult),
                              inc="ve")
                            gt_used[par] = ("ve", ctx.v("ve"))
                    ctx.wait("scalar", "ve", ctx.v("ve"))
                    for gg in range(G):
                        SC(lambda e, gg=gg: e.activation(
                            gater[0][:, :], g0b[:, 1024 * gg:1024 * (gg + 1)],
                            AF.Square, accum_out=colb[:, 16 + gg:17 + gg]), inc="se")
                        ctx.wait("scalar", "se", ctx.v("se"))
                    gt_used[0] = ("se", ctx.v("se"))
                    SC(lambda e: e.activation(rstd8[:, :], colb[:, 16:24], AF.Ln,
                                              scale=1.0 / 1024.0,
                                              bias=epscol[:, 0:1]), inc="se")
                    ctx.wait("scalar", "se", ctx.v("se"))
                    SC(lambda e: e.activation(rstd8[:, :], rstd8[:, :], AF.Exp,
                                              scale=-0.5), inc="se")
                    SE_RSTD = ctx.v("se")
                    ctx.wait("vector", "se", SE_RSTD)
                    ctx.wait("vector", "te", dxb_free[0])
                    for gg in range(G):
                        V(lambda e, gg=gg: e.tensor_scalar(
                            dxb[:, 1024 * gg:1024 * (gg + 1)],
                            g0b[:, 1024 * gg:1024 * (gg + 1)],
                            rstd8[:, gg:gg + 1], None, ALU.mult), inc="ve")
                    VE_G = ctx.v("ve")
                    if debug:
                        import os as _os
                        _what = _os.environ.get("DBG_G", "g")
                        ctx.wait("sync", "ve", VE_G)
                        if _what == "g":
                            DMA("sync", dram["gdbg"][t0 + 128 * it:t0 + 128 * (it + 1), :],
                                dxb[:, :], "dout")
                        elif _what == "g0":
                            DMA("sync", dram["gdbg"][t0 + 128 * it:t0 + 128 * (it + 1), :],
                                g0b[:, :], "dout")
                        if False:
                            pass
                    ctx.wait("tensor", "ve", VE_G)
                    tloc = (c % 2) * 2 + it
                    for rr in range(8):
                        bk2 = pool.acquire("tensor")
                        pb = psum[bk2][:, :].bitcast(BF16)
                        for kk in range(8):
                            kt = rr * 8 + kk
                            TR(pb[:, 128 * kk:128 * (kk + 1)],
                               dxb[:, 128 * kt:128 * (kt + 1)], ident16[:, :],
                               inc="te" if kk == 7 else None)
                        ctx.wait("vector", "te", ctx.v("te"))
                        for kk in range(8):
                            kt = rr * 8 + kk
                            V(lambda e, kt=kt, kk=kk, pb=pb, tloc=tloc: e.tensor_scalar(
                                gTacc[:, kt, 128 * tloc:128 * (tloc + 1)],
                                pb[:, 128 * kk:128 * (kk + 1)],
                                nws[:, kt:kt + 1], None, ALU.mult), inc="ve")
                        pool.release(bk2, "ve", ctx.v("ve"))
                    dxb_free[0] = ctx.v("te")
                ssd_states_and_update(c, v_hs, v_bt, is_main=True)
                s_ready[0] = ctx.v("ve")
                cum_rows_read[0] = ctx.v("cmr")
                if c % 2 == 1:
                    out_quarter(c // 2)
                    if c < NCH - 1:
                        barrier()
                        build_dbc()

        dxb_free = [0]
        bc_used = [0, 0]
        gt_used = [("ve", 0), ("ve", 0)]
        cumr_used = [0]
        cumr_v = [0]
        dec_used = [0, 0]
        dec_g_used = [0, 0]
        dec_v_used = [0, 0]
        sch_used = [0, 0]
        s_ready = [0]

        def ffn():
            barrier()
            build_fbb()
            DMA("sync", sczb[:, :, :], dram["scz"], "xc")
            DMA("sync", zrzb[:, :, :], dram["zrz"], "xc")
            XC_SZ = ctx.v("xc")
            qwsem = ["qw0", "qw1"]
            qw_ready = [0, 0]
            qw_used = [0, 0]
            x1sem = ["x1r", "fbr"]
            h2dram = dram["h2t"].rearrange("(kt kl) t -> kl kt t", kl=128)
            for half in range(2):
                ctx.wait("sync", "te", h2th_used[0])
                vh2 = DMA("sync", h2th[:, :, :],
                          h2dram[:, :, 1024 * half:1024 * (half + 1)], "h2r")
                for i in range(2):
                    ctx.wait("sync", qwsem[i], qw_ready[i])
                    ctx.wait("sync", "ve", qw_used[i])
                    ctx.wait("sync", "cv3", ctx.v("cv3"))
                    qw_ready[i] = DMA("sync", qwb[i][:, :],
                                      dram["qwb16"][128 * i:128 * (i + 1), :],
                                      qwsem[i])
                for ob in range(8):
                    for otl in range(4):
                        ot = 4 * ob + otl
                        slot = ot % 2
                        ctx.wait("vector", qwsem[slot], qw_ready[slot])
                        ctx.wait("vector", "xc", XC_SZ)
                        ctx.wait("vector", "te", wdq_used[0])
                        ctx.wait("vector", "ve", ctx.v("ve"))
                        V(lambda e, slot=slot, ot=ot: e.tensor_tensor(
                            wdq[:, :].rearrange("p (a b) -> p a b", a=32),
                            qwb[slot][:, :].rearrange("p (a b) -> p a b", a=32),
                            sczb[:, ot, :].unsqueeze(2).broadcast_to([128, 32, 128]),
                            ALU.mult), inc="ve")
                        ctx.wait("vector", "ve", ctx.v("ve"))
                        V(lambda e, ot=ot: e.tensor_tensor(
                            wdq[:, :].rearrange("p (a b) -> p a b", a=32),
                            wdq[:, :].rearrange("p (a b) -> p a b", a=32),
                            zrzb[:, ot, :].unsqueeze(2).broadcast_to([128, 32, 128]),
                            ALU.add), inc="ve")
                        qw_used[slot] = ctx.v("ve")
                        VE_DQ = ctx.v("ve")
                        if ot + 2 < 32 or half == 0:
                            nt = (ot + 2) % 32
                            ctx.wait("sync", "ve", qw_used[slot])
                            qw_ready[slot] = DMA("sync", qwb[slot][:, :],
                                                 dram["qwb16"][128 * nt:128 * (nt + 1), :],
                                                 qwsem[slot])
                        ctx.wait("tensor", "ve", VE_DQ)
                        ctx.wait("tensor", "te", wT_used[0])
                        for rr in range(4):
                            bk2 = pool.acquire("tensor")
                            pb = psum[bk2][:, :].bitcast(BF16)
                            for kk in range(8):
                                kt = rr * 8 + kk
                                TR(pb[:, 128 * kk:128 * (kk + 1)],
                                   wdq[:, 128 * kt:128 * (kt + 1)], ident16[:, :],
                                   inc="te" if kk == 7 else None)
                            ctx.wait("scalar", "te", ctx.v("te"))
                            SC(lambda e, rr=rr, otl=otl, pb=pb: e.activation(
                                wT[:, 8 * rr:8 * rr + 8, 128 * otl:128 * (otl + 1)],
                                pb[:, 0:1024].rearrange("p (a b) -> p a b", a=8),
                                AF.Identity), inc="se")
                            pool.release(bk2, "se", ctx.v("se"))
                        wdq_used[0] = ctx.v("te")
                    WT_V = ctx.v("se")
                    ctx.wait("tensor", "se", WT_V)
                    ctx.wait("tensor", "h2r", vh2)
                    for tt in range(8):
                        bk2 = pool.acquire("tensor")
                        for kt in range(NKT):
                            MM(psum[bk2][:, 0:512], h2th[:, kt, 128 * tt:128 * (tt + 1)],
                               wT[:, kt, :], kt == 0, kt == NKT - 1,
                               inc="te" if kt == NKT - 1 else None)
                        h2th_used[0] = ctx.v("te")
                        wT_used[0] = ctx.v("te")
                        TE_O = ctx.v("te")
                        par = tt % 2
                        ctx.wait("sync", "ve", x1p_used[par])
                        vx1 = DMA("sync", x1p[par][:, :],
                                  dram["x1s"][1024 * half + 128 * tt:
                                              1024 * half + 128 * (tt + 1),
                                              512 * ob:512 * (ob + 1)],
                                  ["x1r", "fbr"][par])
                        ctx.wait("vector", "te", TE_O)
                        ctx.wait("vector", ["dout", "dou2"][par], outst_used[par])
                        V(lambda e, bk2=bk2, ob=ob, par=par: e.tensor_tensor(
                            outst[par][:, :], psum[bk2][:, 0:512],
                            fbb[:, 512 * ob:512 * (ob + 1)], ALU.add), inc="ve")
                        pool.release(bk2, "ve", ctx.v("ve"))
                        ctx.wait("vector", "ve", ctx.v("ve"))
                        ctx.wait("vector", ["x1r", "fbr"][par], vx1)
                        V(lambda e, par=par: e.tensor_tensor(
                            outst[par][:, :], outst[par][:, :], x1p[par][:, :],
                            ALU.add), inc="ve")
                        x1p_used[par] = ctx.v("ve")
                        VE_OUT = ctx.v("ve")
                        ctx.wait("sync", "ve", VE_OUT)
                        dv = DMA("sync",
                                 dram["out"][1024 * half + 128 * tt:
                                             1024 * half + 128 * (tt + 1),
                                             512 * ob:512 * (ob + 1)],
                                 outst[par][:, :], ["dout", "dou2"][par])
                        outst_used[par] = dv

        h2th_used = [0]
        wdq_used = [0]
        wT_used = [0]
        x1p_used = [0, 0]
        outst_used = [0, 0]

        if stages >= 2:
            htr = stageA_ln(dram["x_pref"])
            stageB_proj(is_main=False, ht_ready=htr)
            dt_ready[0] = ctx.v("ve")
            ssd_prefix()
            if debug:
                barrier()
                DMA("sync", dram["sdbg"][:, :], S[:, :], "dout")
            barrier()

        htr = stageA_ln(dram["x_main"])
        stageB_proj(is_main=True, ht_ready=htr)
        dt_ready[0] = ctx.v("ve")
        if debug:
            V(lambda e: e.memset(dtbuf[:, 0:128], 0.0), inc="ve")
            ctx.wait("sync", "ve", ctx.v("ve"))
            ctx.wait("sync", "se", ctx.v("se"))
            DMA("sync", dram["dtdbg"][:, :], dtbuf[:, :], "dout")
        if stages >= 3:
            ssd_main()
        if stages >= 4:
            ffn()

        for s in sem_names:
            if s not in ("te", "ve", "se", "ge"):
                ctx.wait("sync", s, ctx.v(s))

    with nc.Block() as block:
        @block.sync
        def _(sync):
            program(Ctx(nc, "sync", sems))

        @block.tensor
        def _(tensor):
            program(Ctx(nc, "tensor", sems))

        @block.vector
        def _(vector):
            program(Ctx(nc, "vector", sems))

        @block.scalar
        def _(scalar):
            program(Ctx(nc, "scalar", sems))

        @block.gpsimd
        def _(gpsimd):
            program(Ctx(nc, "gpsimd", sems))

    es.close()
    return nc


def stage_inputs(inputs):
    x = np.ascontiguousarray(inputs["L0_attn_input_states"])
    w1t = np.ascontiguousarray(inputs["L0_attn_in_proj_w"].T)
    w2t = np.ascontiguousarray(inputs["L0_attn_out_w"].T)
    qw = np.ascontiguousarray(inputs["L0_ffn_qweight"].reshape(HID, HID))
    scz = np.ascontiguousarray(
        inputs["L0_ffn_scales"][:, :, 0].reshape(32, 128, 32).transpose(1, 0, 2))
    zrz = np.ascontiguousarray(
        inputs["L0_ffn_scaled_zeros"][:, :, 0].reshape(32, 128, 32).transpose(1, 0, 2))
    convw = np.ascontiguousarray(
        inputs["L0_attn_conv_w"][:, 0, :].reshape(80, 128, 4).transpose(1, 0, 2))
    convb = np.ascontiguousarray(
        inputs["L0_attn_conv_b"].reshape(80, 128).T)
    dtb = np.ascontiguousarray(inputs["L0_attn_dt_bias"].reshape(H, 1))
    alog = np.ascontiguousarray(inputs["L0_attn_A_log"].reshape(H, 1))
    dvec = np.ascontiguousarray(inputs["L0_attn_D"].reshape(H, 1))
    nw = np.ascontiguousarray(inputs["L0_attn_norm_w"].reshape(64, 128).T)
    fbias = np.ascontiguousarray(inputs["L0_ffn_bias"].reshape(1, HID))

    in_maps = []
    for core in range(8):
        bi, hi = core // 2, core % 2
        xm = np.zeros((TOKH, HID), np.float32)
        xm[HALO:] = x[bi, hi * TOK:(hi + 1) * TOK]
        if hi == 1:
            xm[:HALO] = x[bi, TOK - HALO:TOK]
        xp = np.zeros((TOKH, HID), np.float32)
        if hi == 1:
            xp[HALO:] = x[bi, 0:TOK]
        smask = np.full((128, 1), float(hi), np.float32)
        in_maps.append({
            "x_main": xm, "x_pref": xp, "w1t": w1t, "w2t": w2t, "qw": qw,
            "scz": scz, "zrz": zrz, "convw": convw, "convb": convb,
            "dtb": dtb, "alog": alog, "dvec": dvec, "nw": nw, "fbias": fbias,
            "smask": smask,
        })
    return in_maps


def kernel(**inputs):
    in_maps = stage_inputs(inputs)
    nc = build_program(debug=False, stages=4)
    res = run_bass_kernel_spmd(nc, in_maps, core_ids=list(range(8)))
    out = np.empty((BATCH, SEQ, HID), np.float32)
    for core in range(8):
        bi, hi = core // 2, core % 2
        out[bi, hi * TOK:(hi + 1) * TOK] = res.results[core]["out"]
    return out


# version 10
# speedup vs baseline: 1.0626x; 1.0626x over previous
import numpy as np
import ml_dtypes
import concourse.bass as bass
import concourse.mybir as mybir
from concourse.bass_utils import run_bass_kernel_spmd
from contextlib import ExitStack

F32 = mybir.dt.float32
BF16 = mybir.dt.bfloat16
I32 = mybir.dt.int32
AF = mybir.ActivationFunctionType
ALU = mybir.AluOpType

BATCH, SEQ, HID = 4, 4096, 4096
H, PDIM, NDIM, G, KCONV, CS = 128, 64, 128, 8, 4, 256
INTER = H * PDIM
CONVCH = INTER + 2 * G * NDIM
PROJ = INTER + CONVCH + H
EPS = 1e-5

TOK = 2048
HALO = 128
TOKH = TOK + HALO
NCH = TOK // CS
NKT = HID // 128
NITK = INTER // 128

GATE_T0 = 0
HS_T0 = INTER // 128
B_T0 = HS_T0 + 64
C_T0 = B_T0 + 8
DT_T0 = C_T0 + 8

DEBUG_TENSORS = {"hstok", "gate", "bch", "btok", "cch", "hstok_p", "btok_p",
                 "x1s", "h2t", "dtdbg", "sdbg", "gdbg", "dbg2"}


class Ctx:
    def __init__(self, nc, cur, sems):
        self.nc = nc
        self.cur = cur
        self.sems = sems
        self.cnt = {k: 0 for k in sems}

    def eng(self, name):
        return getattr(self.nc, name) if self.cur == name else None

    def inc(self, instr, sem, k):
        self.cnt[sem] += k
        if instr is not None:
            instr.then_inc(self.sems[sem], k)
        return self.cnt[sem]

    def wait(self, engname, sem, val):
        if val <= 0:
            return
        e = self.eng(engname)
        if e is not None:
            e.wait_ge(self.sems[sem], val)

    def v(self, sem):
        return self.cnt[sem]


class BankPool:
    def __init__(self, ctx, banks):
        self.ctx = ctx
        self.banks = banks
        self.free_at = [None] * len(banks)
        self.rr = 0

    def acquire(self, producer_eng, which=None):
        if which is None:
            which = self.rr % len(self.banks)
            self.rr += 1
        fa = self.free_at[which]
        if fa is not None:
            self.ctx.wait(producer_eng, fa[0], fa[1])
            self.free_at[which] = None
        return which

    def release(self, which, sem, val):
        self.free_at[which] = (sem, val)


def build_program(debug=False, stages=4):
    nc = bass.Bass("TRN2", target_bir_lowering=False, debug=False)

    dram = {}

    def din(name, shape, dtype):
        dram[name] = nc.dram_tensor(name, shape, dtype, kind="ExternalInput").ap()

    def dint(name, shape, dtype):
        if debug and name in DEBUG_TENSORS:
            dram[name] = nc.dram_tensor(name, shape, dtype, kind="ExternalOutput").ap()
        else:
            dram[name] = nc.dram_tensor(name, shape, dtype).ap()

    din("x_main", [TOKH, HID], F32)
    din("x_pref", [TOKH, HID], F32)
    din("w1t", [HID, PROJ], F32)
    din("w2t", [INTER, HID], F32)
    din("qw", [HID, HID], I32)
    din("scz", [128, 32, 32], F32)
    din("zrz", [128, 32, 32], F32)
    din("convw", [128, 80, 4], F32)
    din("convb", [128, 80], F32)
    din("dtb", [H, 1], F32)
    din("alog", [H, 1], F32)
    din("dvec", [H, 1], F32)
    din("nw", [128, 64], F32)
    din("fbias", [1, HID], F32)
    din("smask", [128, 1], F32)
    out_hbm = nc.dram_tensor("out", [TOK, HID], F32, kind="ExternalOutput").ap()
    dram["out"] = out_hbm

    dint("hstok", [TOK, INTER], BF16)
    dint("gate", [TOK, INTER], BF16)
    dint("bch", [G * NDIM, TOK], BF16)
    dint("btok", [TOK, G * NDIM], BF16)
    dint("cch", [G * NDIM, TOK], BF16)
    dint("hstok_p", [TOK, INTER], BF16)
    dint("btok_p", [TOK, G * NDIM], BF16)
    dint("cumhbm", [H, CS], F32)
    dint("x1s", [TOK, HID], F32)
    dint("h2t", [HID, TOK], BF16)
    dint("w2b16", [INTER, HID], BF16)
    dint("w1b16", [HID, PROJ], BF16)
    dint("qwb16", [HID, HID], BF16)
    if debug:
        dint("dtdbg", [H, TOKH], F32)
        dint("sdbg", [NDIM, INTER], BF16)
        dint("gdbg", [TOK, INTER], BF16)
        dint("dbg2", [768, INTER], F32)

    es = ExitStack()
    sb = lambda n, s, d: es.enter_context(nc.sbuf_tensor(n, s, d))
    psb = lambda n, s, d: es.enter_context(nc.psum_tensor(n, s, d))

    ident16 = sb("ident16", [128, 128], BF16)
    ident32 = sb("ident32", [128, 128], F32)
    tril16 = sb("tril16", [128, 128], BF16)
    mneg32 = sb("mneg32", [128, 128], F32)
    onesrow = sb("onesrow", [1, 128], F32)
    rowst = sb("rowst", [1, 512], F32)
    conww = sb("conww", [128, 80, 4], F32)
    conbb = sb("conbb", [128, 80], F32)
    acol = sb("acol", [128, 1], F32)
    dtbcol = sb("dtbcol", [128, 1], F32)
    dvcol = sb("dvcol", [128, 1], F32)
    smaskc = sb("smaskc", [128, 1], F32)
    epscol = sb("epscol", [128, 1], F32)
    onecol = sb("onecol", [128, 1], F32)
    drowp = sb("drowp", [1, 128], F32)
    nws = sb("nws", [128, 64], F32)
    colb = sb("colb", [128, 32], F32)
    dtbuf = sb("dtbuf", [128, TOKH], F32)
    S = sb("S", [128, INTER], BF16)


    ARENA_BYTES = 180224
    arena = sb("arena", [128, ARENA_BYTES], mybir.dt.uint8)
    abase = nc.lookup_mloc(arena).addr

    def at(name, shape, dtype, rel):
        t = nc.alloc_sbuf_tensor_at(name, shape, dtype, offset=abase + rel)
        return t

    biga = at("biga", [128, 69632], BF16, 0)
    gTacc = at("gTacc", [128, NITK, 512], BF16, 0)
    h2th = at("h2th", [128, NKT, 1024], BF16, 0)
    hsr = [at(f"hsr{i}", [128, INTER], BF16, 65536 + 16384 * i) for i in range(2)]
    x1q = [at(f"x1q{i}", [128, HID], F32, 65536 + 16384 * i) for i in range(2)]
    qwb = [at(f"qwb{i}", [128, HID], BF16, 65536 + 8192 * i) for i in range(2)]
    gater = [at(f"gater{i}", [128, 1024], BF16, 98304 + 2048 * i) for i in range(2)]
    g0b = at("g0b", [128, INTER], BF16, 102400)
    wT = at("wT", [128, NKT, 512], BF16, 98304)
    cumr = at("cumr", [1, 1024], F32, 118784)
    bchr = [at(f"bchr{i}", [128, CS], BF16, 122880 + 512 * i) for i in range(2)]
    cchr = [at(f"cchr{i}", [128, CS], BF16, 123904 + 512 * i) for i in range(2)]
    btokr = [at(f"btokr{i}", [128, G * NDIM], BF16, 124928 + 2048 * i) for i in range(2)]
    xpc = [at(f"xpc{i}", [128, 512], F32, 129024 + 2048 * i) for i in range(2)]
    wdq = at("wdq", [128, HID], BF16, 131072)
    xbuf = at("xbuf", [128, HID], F32, 139264)
    hbuf = at("hbuf", [128, HID], BF16, 155648)
    xbcst = at("xbcst", [128, TOKH], BF16, 139264)
    convo = at("convo", [128, TOK], F32, 143616)
    spst = at("spst", [128, TOK], BF16, 151808)
    spstB = at("spstB", [128, TOK], BF16, 155904)
    tst = at("tst", [128, 8, 128], BF16, 160000)
    dbc = at("dbc", [128, INTER], BF16, 139264)
    sczb = at("sczb", [128, 32, 32], F32, 139264)
    zrzb = at("zrzb", [128, 32, 32], F32, 143360)
    fbb = at("fbb", [128, HID], BF16, 147456)
    outst = [at(f"outst{i}", [128, 512], F32, 155648 + 2048 * i) for i in range(2)]
    x1p = [at(f"x1p{i}", [128, 512], F32, 159744 + 2048 * i) for i in range(2)]
    tst2 = at("tst2", [128, 16, 128], BF16, 147456)
    oq = [at(f"oq{i}", [128, 512], F32, 118784 + 2048 * i) for i in range(2)]
    w1b = [at(f"w1b{i}", [128, NKT, 128], BF16, 163840 + 8192 * i) for i in range(2)]
    dxb = at("dxb", [128, INTER], BF16, 163840)
    w2p = [at(f"w2p{i}", [128, 8, 512], BF16, 163840 + 8192 * i) for i in range(2)]
    dtsp = at("dtsp", [128, CS], F32, 155648)
    dtt1 = at("dtt1", [128, CS], F32, 156672)
    dtt2 = at("dtt2", [128, CS], F32, 157696)
    cumc = at("cumc", [128, CS], F32, 158720)
    wch = at("wch", [128, CS], F32, 159744)
    sbch = at("sbch", [128, CS], F32, 160768)
    cumt = at("cumt", [128, 2, 128], F32, 161792)
    ectt = at("ectt", [128, 2, 128], F32, 162816)
    wtok = at("wtok", [128, 2, 128], F32, 129024)
    sbtok = at("sbtok", [128, 2, 128], F32, 130048)
    scraw = at("scraw", [128, CS + 128], BF16, 131072)
    decb = [at(f"decb{i}", [128, CS + 128], BF16, 131840 + 768 * i) for i in range(3)]
    schb = [at(f"schb{i}", [128, CS + 128], BF16, 134144 + 768 * i) for i in range(3)]
    yzbuf = at("yzbuf", [128, 512], BF16, 136448)
    cdecc = at("cdecc", [128, 128], F32, 137472)
    cdrow = at("cdrow", [1, 128], F32, 137984)
    rstd8 = at("rstd8", [128, 8], F32, 138496)
    bwb = [at(f"bwb{i}", [128, 2, 16, 128], BF16, 98304 + 8192 * i) for i in range(2)]
    w2st = [at(f"w2st{i}", [128, 1024], F32, 131072 + 4096 * i) for i in range(2)]
    w2cv = [at(f"w2cv{i}", [128, 1024], BF16, 118784 + 2048 * i) for i in range(2)]

    psum = [psb(f"pp{i}", [128, 512], F32) for i in range(8)]

    sem_names = ["xc", "xl", "w1a", "w1b", "spt", "sps", "sp2", "hs0", "hs1",
                 "bt0", "bt1", "bc0", "bc1", "gt0", "gt1", "cmw", "cmr",
                 "w2a", "w2b", "xp0", "xp1", "x1w", "x1r", "h2w", "h2r",
                 "qw0", "qw1", "fbr", "oq0", "oq1", "x1q0", "x1q1", "dout", "dou2", "dbg", "wc0", "wc1", "wcw", "wcx", "cv1", "cv2", "cv3",
                 "te", "ve", "se", "ge"]
    sems = {n: es.enter_context(nc.semaphore(n)) for n in sem_names}

    def program(ctx: Ctx):
        te_ = ctx.eng("tensor")
        ve_ = ctx.eng("vector")
        se_ = ctx.eng("scalar")
        ge_ = ctx.eng("gpsimd")
        sy_ = ctx.eng("sync")
        pool = BankPool(ctx, psum)

        def MM(out, lhsT, rhs, start, stop, inc=None, k=1, tp=None):
            i = None
            if te_ is not None:
                i = te_.matmul(out, lhsT, rhs, start=start, stop=stop,
                               skip_group_check=True, tile_position=tp)
            if inc:
                ctx.inc(i, inc, k)

        def TR(out, in_, ident, inc=None, k=1):
            i = None
            if te_ is not None:
                i = te_.transpose(out, in_, ident)
            if inc:
                ctx.inc(i, inc, k)

        def V(fn, *a, inc=None, k=1, **kw):
            i = fn(ve_, *a, **kw) if ve_ is not None else None
            if inc:
                ctx.inc(i, inc, k)

        def SC(fn, *a, inc=None, k=1, **kw):
            i = fn(se_, *a, **kw) if se_ is not None else None
            if inc:
                ctx.inc(i, inc, k)

        def GP(fn, *a, inc=None, k=1, **kw):
            i = fn(ge_, *a, **kw) if ge_ is not None else None
            if inc:
                ctx.inc(i, inc, k)

        def DMA(eng, out, in_, sem, engobj=None):
            e = ctx.eng(eng)
            i = e.dma_start(out=out, in_=in_) if e is not None else None
            return ctx.inc(i, sem, 16)

        g = ge_
        i1 = g.memset(ident16[:, :], 0.0) if g else None
        ctx.inc(i1, "ge", 1)
        ctx.wait("gpsimd", "ge", ctx.v("ge"))
        i1 = g.affine_select(out=ident16[:, :], in_=ident16[:, :],
                             compare_op=ALU.not_equal, fill=1.0, base=0,
                             pattern=[[-1, 128]], channel_multiplier=1) if g else None
        ctx.inc(i1, "ge", 1)
        i1 = g.memset(ident32[:, :], 0.0) if g else None
        ctx.inc(i1, "ge", 1)
        ctx.wait("gpsimd", "ge", ctx.v("ge"))
        i1 = g.affine_select(out=ident32[:, :], in_=ident32[:, :],
                             compare_op=ALU.not_equal, fill=1.0, base=0,
                             pattern=[[-1, 128]], channel_multiplier=1) if g else None
        ctx.inc(i1, "ge", 1)
        i1 = g.memset(tril16[:, :], 1.0) if g else None
        ctx.inc(i1, "ge", 1)
        ctx.wait("gpsimd", "ge", ctx.v("ge"))
        i1 = g.affine_select(out=tril16[:, :], in_=tril16[:, :],
                             compare_op=ALU.is_ge, fill=0.0, base=0,
                             pattern=[[1, 128]], channel_multiplier=-1) if g else None
        ctx.inc(i1, "ge", 1)
        i1 = g.memset(mneg32[:, :], 0.0) if g else None
        ctx.inc(i1, "ge", 1)
        ctx.wait("gpsimd", "ge", ctx.v("ge"))
        i1 = g.affine_select(out=mneg32[:, :], in_=mneg32[:, :],
                             compare_op=ALU.is_ge, fill=-1e30, base=0,
                             pattern=[[1, 128]], channel_multiplier=-1) if g else None
        ctx.inc(i1, "ge", 1)
        GE_CONST = ctx.v("ge")

        V(lambda e: e.memset(onesrow[:, :], 1.0), inc="ve")
        V(lambda e: e.memset(epscol[:, :], EPS), inc="ve")
        V(lambda e: e.memset(onecol[:, :], 1.0), inc="ve")
        VE_ONES = ctx.v("ve")

        for (dst, src) in [
            (conww[:, :, :], dram["convw"]),
            (conbb[:, :], dram["convb"]),
            (dtbcol[:, :], dram["dtb"]),
            (acol[:, :], dram["alog"]),
            (dvcol[:, :], dram["dvec"]),
            (smaskc[:, :], dram["smask"]),
            (nws[:, :], dram["nw"]),
        ]:
            DMA("sync", dst, src, "xc")
        DX_CONST = ctx.v("xc")

        ctx.wait("scalar", "xc", DX_CONST)
        SC(lambda e: e.activation(acol[:, :], acol[:, :], AF.Exp), inc="se")
        ctx.wait("scalar", "se", ctx.v("se"))
        SC(lambda e: e.mul(acol[:, :], acol[:, :], -1.0), inc="se")
        SE_ACOL = ctx.v("se")

        ctx.wait("tensor", "xc", DX_CONST)
        ctx.wait("tensor", "ve", VE_ONES)
        ctx.wait("tensor", "ge", GE_CONST)
        bk = pool.acquire("tensor")
        TR(psum[bk][0:1, 0:128], dvcol[:, 0:1], ident32[:, :], inc="te")
        ctx.wait("vector", "te", ctx.v("te"))
        V(lambda e: e.tensor_copy(drowp[0:1, 0:128], psum[bk][0:1, 0:128]), inc="ve")
        pool.release(bk, "ve", ctx.v("ve"))
        VE_DROW = ctx.v("ve")

        CV1 = 0

        def emit_convert_piece(i):
            if i < 64:
                i1 = ge_.dma_start(out=dram["w2b16"][128 * i:128 * (i + 1), :],
                                   in_=dram["w2t"][128 * i:128 * (i + 1), :]) if ge_ else None
                ctx.inc(i1, "cv2", 16)
            else:
                j = i - 64
                i1 = ge_.dma_start(out=dram["qwb16"][128 * j:128 * (j + 1), :],
                                   in_=dram["qw"][128 * j:128 * (j + 1), :]) if ge_ else None
                ctx.inc(i1, "cv3", 16)

        def build_dbc():
            for jj in range(16):
                bkd = pool.acquire("tensor")
                ctx.wait("tensor", "ve", VE_DROW)
                rhs = drowp[0:1, 8 * jj:8 * jj + 8].unsqueeze(2).broadcast_to([1, 8, 64])
                MM(psum[bkd][:, 0:512].rearrange("p (a b) -> p a b", a=8),
                   onesrow[0:1, :], rhs, True, True, inc="te")
                ctx.wait("vector", "te", ctx.v("te"))
                V(lambda e, jj=jj, bkd=bkd: e.tensor_copy(
                    dbc[:, 512 * jj:512 * (jj + 1)], psum[bkd][:, 0:512]), inc="ve")
                pool.release(bkd, "ve", ctx.v("ve"))

        def build_fbb():
            for jj in range(8):
                ctx.wait("sync", "te", ctx.v("te"))
                DMA("sync", rowst[0:1, 0:512],
                    dram["fbias"][0:1, 512 * jj:512 * (jj + 1)], "fbr")
                bkd = pool.acquire("tensor")
                ctx.wait("tensor", "fbr", ctx.v("fbr"))
                MM(psum[bkd][:, 0:512], onesrow[0:1, :], rowst[0:1, 0:512],
                   True, True, inc="te")
                ctx.wait("vector", "te", ctx.v("te"))
                V(lambda e, jj=jj, bkd=bkd: e.tensor_copy(
                    fbb[:, 512 * jj:512 * (jj + 1)], psum[bkd][:, 0:512]), inc="ve")
                pool.release(bkd, "ve", ctx.v("ve"))

        def barrier():
            snap = {s: ctx.v(s) for s in sem_names}
            for en in ("sync", "tensor", "vector", "scalar", "gpsimd"):
                for s, v in snap.items():
                    ctx.wait(en, s, v)

        def stageA_ln(xdram):
            HT = biga[:, 0:NKT * TOKH].rearrange("p (kt t) -> p kt t", kt=NKT)
            xload_v = [None] * 17
            xln_done = [0] * 17
            hbuf_free = [0] * 17

            for tt in range(17):
                b = tt % 2
                if tt >= 1:
                    ctx.wait("sync", "se", xln_done[tt - 1])
                xload_v[tt] = DMA("sync", xbuf[:, :],
                                  xdram[128 * tt:128 * (tt + 1), :], "xl")
                ctx.wait("vector", "xl", xload_v[tt])
                ctx.wait("scalar", "xl", xload_v[tt])
                if tt >= 1:
                    ctx.wait("scalar", "te", hbuf_free[tt - 1])
                c0 = 4 * b
                V(lambda e: e.tensor_reduce(colb[:, c0:c0 + 1], xbuf[:, :],
                                            mybir.AxisListType.X, ALU.add), inc="ve")
                SC(lambda e: e.activation(hbuf[:, :], xbuf[:, :], AF.Square,
                                          accum_out=colb[:, c0 + 1:c0 + 2]), inc="se")
                ctx.wait("vector", "se", ctx.v("se"))
                ctx.wait("vector", "ve", ctx.v("ve"))
                V(lambda e: e.tensor_scalar(colb[:, c0:c0 + 1], colb[:, c0:c0 + 1],
                                            1.0 / HID, None, ALU.mult), inc="ve")
                ctx.wait("vector", "ve", ctx.v("ve"))
                V(lambda e: e.scalar_tensor_tensor(colb[:, c0 + 1:c0 + 2],
                                                   colb[:, c0 + 1:c0 + 2], 1.0 / HID,
                                                   colb[:, c0:c0 + 1], ALU.mult,
                                                   ALU.bypass), inc="ve")
                ctx.wait("vector", "ve", ctx.v("ve"))
                V(lambda e: e.tensor_tensor(colb[:, c0 + 2:c0 + 3], colb[:, c0:c0 + 1],
                                            colb[:, c0:c0 + 1], ALU.mult), inc="ve")
                ctx.wait("vector", "ve", ctx.v("ve"))
                V(lambda e: e.tensor_tensor(colb[:, c0 + 1:c0 + 2], colb[:, c0 + 1:c0 + 2],
                                            colb[:, c0 + 2:c0 + 3], ALU.subtract), inc="ve")
                VVAR = ctx.inc(None, "ve", 0)
                ctx.wait("scalar", "ve", ctx.v("ve"))
                SC(lambda e: e.activation(colb[:, c0 + 1:c0 + 2], colb[:, c0 + 1:c0 + 2],
                                          AF.Ln, bias=epscol[:, 0:1]), inc="se")
                ctx.wait("scalar", "se", ctx.v("se"))
                SC(lambda e: e.activation(colb[:, c0 + 1:c0 + 2], colb[:, c0 + 1:c0 + 2],
                                          AF.Exp, scale=-0.5), inc="se")
                ctx.wait("vector", "se", ctx.v("se"))
                V(lambda e: e.tensor_scalar(colb[:, c0 + 2:c0 + 3], colb[:, c0:c0 + 1],
                                            colb[:, c0 + 1:c0 + 2], -1.0,
                                            ALU.mult, ALU.mult), inc="ve")
                ctx.wait("scalar", "ve", ctx.v("ve"))
                SC(lambda e: e.activation(hbuf[:, :], xbuf[:, :], AF.Identity,
                                          bias=colb[:, c0 + 2:c0 + 3],
                                          scale=colb[:, c0 + 1:c0 + 2]), inc="se")
                xln_done[tt] = ctx.v("se")
                ctx.wait("tensor", "se", ctx.v("se"))
                for rr in range(4):
                    bk2 = pool.acquire("tensor")
                    pb = psum[bk2][:, :].bitcast(BF16)
                    for kk in range(8):
                        kt = rr * 8 + kk
                        TR(pb[:, 128 * kk:128 * (kk + 1)],
                           hbuf[:, 128 * kt:128 * (kt + 1)], ident16[:, :],
                           inc="te" if kk == 7 else None)
                    ctx.wait("vector", "te", ctx.v("te"))
                    V(lambda e: e.tensor_copy(
                        HT[:, 8 * rr:8 * rr + 8, 128 * tt:128 * (tt + 1)],
                        pb[:, 0:1024].rearrange("p (a b) -> p a b", a=8)), inc="ve")
                    pool.release(bk2, "ve", ctx.v("ve"))
                hbuf_free[tt] = ctx.v("te")
            return ctx.v("ve")

        def stageB_proj(is_main, ht_ready=0):
            HT = biga[:, 0:NKT * TOKH].rearrange("p (kt t) -> p kt t", kt=NKT)
            ctx.wait("tensor", "ve", ht_ready)
            w1dram = dram["w1t"].rearrange("(kt kl) c -> kl kt c", kl=128)
            hs_d = dram["hstok"] if is_main else dram["hstok_p"]
            bt_d = dram["btok"] if is_main else dram["btok_p"]

            if is_main:
                tiles = ([("hs", HS_T0 + i, i) for i in range(64)]
                         + [("b", B_T0 + i, i) for i in range(8)]
                         + [("c", C_T0 + i, i) for i in range(8)]
                         + [("dt", DT_T0, 0)]
                         + [("gate", GATE_T0 + i, i) for i in range(64)])
            else:
                tiles = ([("hs", HS_T0 + i, i) for i in range(64)]
                         + [("b", B_T0 + i, i) for i in range(8)]
                         + [("dt", DT_T0, 0)])

            w1sem = ["w1a", "w1b"]
            w1_used_at = [0, 0]
            w1_ready = [0, 0]
            for i in range(min(2, len(tiles))):
                kind, ct, rel = tiles[i]
                ctx.wait("gpsimd", "te", w1_used_at[i])
                w1_ready[i] = DMA("gpsimd", w1b[i][:, :, :],
                                  w1dram[:, :, 128 * ct:128 * (ct + 1)], w1sem[i])

            spstp = [spst, spstB]
            xbcst_free = [("ve", 0)]
            convo_free = [("se", 0)]
            spst_free = [[("te", 0), ("sps", 0)], [("te", 0), ("sps", 0)]]
            tstf = [0]
            pending = [None]

            def make_tail(kind, rel, par, silu_se):
                def tail():
                    sp = spstp[par]
                    ctx.wait("tensor", "se", silu_se)
                    if kind == "hs":
                        dstd = hs_d.rearrange("(tt tl) c -> tl tt c", tl=128)[:, :, 128 * rel:128 * (rel + 1)]
                    elif kind == "b":
                        dstd = bt_d.rearrange("(tt tl) c -> tl tt c", tl=128)[:, :, 128 * rel:128 * (rel + 1)]
                    else:
                        dstd = dram["gate"].rearrange("(tt tl) c -> tl tt c", tl=128)[:, :, 128 * rel:128 * (rel + 1)]
                    for rr in range(2):
                        bk3 = pool.acquire("tensor")
                        pb = psum[bk3][:, :].bitcast(BF16)
                        for kk in range(8):
                            ttt = rr * 8 + kk
                            TR(pb[:, 128 * kk:128 * (kk + 1)],
                               sp[:, 128 * ttt:128 * (ttt + 1)], ident16[:, :],
                               inc="te" if kk == 7 else None)
                        if rr == 1:
                            spst_free[par][0] = ("te", ctx.v("te"))
                        ctx.wait("vector", "te", ctx.v("te"))
                        ctx.wait("vector", "spt", tstf[0])
                        V(lambda e, pb=pb: e.tensor_copy(
                            tst[:, :, :],
                            pb[:, 0:1024].rearrange("p (a b) -> p a b", a=8)), inc="ve")
                        pool.release(bk3, "ve", ctx.v("ve"))
                        ctx.wait("sync", "ve", ctx.v("ve"))
                        tstf[0] = DMA("sync", dstd[:, 8 * rr:8 * rr + 8, :],
                                      tst[:, :, :], "spt")
                return tail

            cvn = [0]
            for idx, (kind, ct, rel) in enumerate(tiles):
                if not is_main:
                    for _ in range(2):
                        if cvn[0] < 96:
                            emit_convert_piece(cvn[0])
                            cvn[0] += 1
                wb = idx % 2
                par = idx % 2
                ctx.wait("tensor", w1sem[wb], w1_ready[wb])
                conv = kind in ("hs", "b", "c")
                if kind in ("gate", "dt"):
                    tbs = [(128 + 512 * i, 512) for i in range(4)]
                else:
                    tbs = [(0, 512), (512, 512), (1024, 512), (1536, 512), (2048, 128)]
                unit_banks = []
                for (t0, tw) in tbs:
                    bk2 = pool.acquire("tensor")
                    for kt in range(NKT):
                        MM(psum[bk2][:, 0:tw], w1b[wb][:, kt, :], HT[:, kt, t0:t0 + tw],
                           kt == 0, kt == NKT - 1,
                           inc="te" if kt == NKT - 1 else None)
                    unit_banks.append((bk2, t0, tw, ctx.v("te")))
                w1_used_at[wb] = ctx.v("te")
                if idx + 2 < len(tiles):
                    kind2, ct2, _ = tiles[idx + 2]
                    ctx.wait("gpsimd", "te", w1_used_at[wb])
                    w1_ready[wb] = DMA("gpsimd", w1b[wb][:, :, :],
                                       w1dram[:, :, 128 * ct2:128 * (ct2 + 1)], w1sem[wb])
                if pending[0] is not None:
                    pending[0]()
                    pending[0] = None

                if kind == "dt":
                    for (bk2, t0, tw, tev) in unit_banks:
                        ctx.wait("vector", "te", tev)
                        V(lambda e, bk2=bk2, t0=t0, tw=tw:
                          e.tensor_copy(dtbuf[:, t0:t0 + tw], psum[bk2][:, 0:tw]),
                          inc="ve")
                        pool.release(bk2, "ve", ctx.v("ve"))
                    continue

                if kind == "gate":
                    ctx.wait("scalar", spst_free[par][0][0], spst_free[par][0][1])
                    ctx.wait("scalar", spst_free[par][1][0], spst_free[par][1][1])
                else:
                    ctx.wait("scalar", xbcst_free[0][0], xbcst_free[0][1])
                for (bk2, t0, tw, tev) in unit_banks:
                    ctx.wait("scalar", "te", tev)
                    if kind == "gate":
                        SC(lambda e, bk2=bk2, t0=t0, tw=tw, par=par:
                           e.activation(spstp[par][:, t0 - 128:t0 - 128 + tw],
                                        psum[bk2][:, 0:tw], AF.Silu), inc="se")
                    else:
                        SC(lambda e, bk2=bk2, t0=t0, tw=tw:
                           e.activation(xbcst[:, t0:t0 + tw], psum[bk2][:, 0:tw],
                                        AF.Copy), inc="se")
                    pool.release(bk2, "se", ctx.v("se"))
                EVAC_SE = ctx.v("se")

                if conv:
                    ctx.wait("vector", "se", EVAC_SE)
                    ctx.wait("vector", convo_free[0][0], convo_free[0][1])
                    cti = (ct - HS_T0)
                    V(lambda e: e.tensor_scalar(convo[:, :], xbcst[:, 125:125 + TOK],
                                                conww[:, cti, 0:1], None, ALU.mult),
                      inc="ve")
                    for kk in range(1, 4):
                        ctx.wait("vector", "ve", ctx.v("ve"))
                        V(lambda e, kk=kk: e.scalar_tensor_tensor(
                            convo[:, :], xbcst[:, 125 + kk:125 + kk + TOK],
                            conww[:, cti, kk:kk + 1], convo[:, :], ALU.mult, ALU.add),
                          inc="ve")
                    xbcst_free[0] = ("ve", ctx.v("ve"))
                    ctx.wait("scalar", "ve", ctx.v("ve"))
                    ctx.wait("scalar", spst_free[par][0][0], spst_free[par][0][1])
                    ctx.wait("scalar", spst_free[par][1][0], spst_free[par][1][1])
                    SC(lambda e, par=par: e.activation(spstp[par][:, :], convo[:, :],
                                                       AF.Silu,
                                                       bias=conbb[:, cti:cti + 1]),
                       inc="se")
                    convo_free[0] = ("se", ctx.v("se"))
                SILU_SE = ctx.v("se")

                if kind in ("c", "b"):
                    ctx.wait("sync", "se", SILU_SE)
                    chd = dram["cch"] if kind == "c" else dram["bch"]
                    spsem = ["sps", "sp2"][par]
                    dv = DMA("sync", chd[128 * rel:128 * (rel + 1), :],
                             spstp[par][:, :], spsem)
                    spst_free[par][1] = (spsem, dv)
                    if kind == "c":
                        continue
                pending[0] = make_tail(kind, rel, par, SILU_SE)
            if pending[0] is not None:
                pending[0]()
                pending[0] = None
            return

        def dt_pipeline(c, is_main):
            t0 = 128 + c * CS
            src = dtbuf[:, t0:t0 + CS]
            ctx.wait("scalar", "ve", dt_ready[0])
            SC(lambda e: e.activation(dtt1[:, :], src, AF.Abs, bias=dtbcol[:, 0:1]), inc="se")
            ctx.wait("scalar", "se", ctx.v("se"))
            SC(lambda e: e.activation(dtt1[:, :], dtt1[:, :], AF.Exp, scale=-1.0), inc="se")
            ctx.wait("scalar", "se", ctx.v("se"))
            SC(lambda e: e.activation(dtt1[:, :], dtt1[:, :], AF.Ln, bias=onecol[:, 0:1]), inc="se")
            ctx.wait("scalar", "se", ctx.v("se"))
            SC(lambda e: e.activation(dtt2[:, :], src, AF.Relu, bias=dtbcol[:, 0:1]), inc="se")
            ctx.wait("vector", "se", ctx.v("se"))
            V(lambda e: e.tensor_tensor(dtsp[:, :], dtt1[:, :], dtt2[:, :], ALU.add), inc="ve")
            ctx.wait("vector", "ve", ctx.v("ve"))
            ctx.wait("vector", "se", SE_ACOL)
            V(lambda e: e.tensor_scalar(dtt1[:, :], dtsp[:, :], acol[:, 0:1], None,
                                        ALU.mult), inc="ve")
            ctx.wait("vector", "ve", ctx.v("ve"))
            V(lambda e: e.tensor_tensor_scan(cumc[:, :], dtt1[:, :], dtt1[:, :], 0.0,
                                             ALU.add, ALU.bypass), inc="ve")
            VE_CUM = ctx.v("ve")
            ctx.wait("scalar", "ve", VE_CUM)
            SC(lambda e: e.activation(dtt2[:, :], cumc[:, :], AF.Exp, scale=-1.0,
                                      bias=cumc[:, CS - 1:CS]), inc="se")
            ctx.wait("vector", "se", ctx.v("se"))
            V(lambda e: e.tensor_tensor(wch[:, :], dtsp[:, :], dtt2[:, :], ALU.mult), inc="ve")
            VE_W = ctx.v("ve")
            if is_main:
                SC(lambda e: e.activation(dtt1[:, :], dtsp[:, :], AF.Ln), inc="se")
                ctx.wait("vector", "se", ctx.v("se"))
                V(lambda e: e.tensor_tensor(sbch[:, :], dtt1[:, :], cumc[:, :],
                                            ALU.subtract), inc="ve")
            VE_SB = ctx.v("ve")
            ctx.wait("tensor", "ve", ctx.v("ve"))
            trs = [(cumc, cumt), (wch, wtok)] + ([(sbch, sbtok)] if is_main else [])
            for (srcb, dstb) in trs:
                bk = pool.acquire("tensor")
                for jt in range(2):
                    TR(psum[bk][:, 128 * jt:128 * (jt + 1)],
                       srcb[:, 128 * jt:128 * (jt + 1)], ident32[:, :],
                       inc="te" if jt == 1 else None)
                ctx.wait("vector", "te", ctx.v("te"))
                V(lambda e, dstb=dstb, bk=bk: e.tensor_copy(
                    dstb[:, :, :].rearrange("p a b -> p (a b)"), psum[bk][:, 0:256]),
                  inc="ve")
                pool.release(bk, "ve", ctx.v("ve"))
            VE_TOK = ctx.v("ve")
            if is_main:
                ctx.wait("scalar", "ve", VE_TOK)
                SC(lambda e: e.activation(
                    ectt[:, :, :].rearrange("p a b -> p (a b)"),
                    cumt[:, :, :].rearrange("p a b -> p (a b)"), AF.Exp), inc="se")
            bk = pool.acquire("tensor")
            ctx.wait("tensor", "ve", VE_CUM)
            TR(psum[bk][0:1, 0:128], cumc[:, CS - 1:CS], ident32[:, :], inc="te")
            ctx.wait("scalar", "te", ctx.v("te"))
            SC(lambda e, bk=bk: e.activation(cdrow[0:1, :], psum[bk][0:1, 0:128],
                                             AF.Exp), inc="se")
            pool.release(bk, "se", ctx.v("se"))
            bk = pool.acquire("tensor")
            ctx.wait("tensor", "se", ctx.v("se"))
            MM(psum[bk][:, 0:128], onesrow[0:1, :], cdrow[0:1, :], True, True, inc="te")
            ctx.wait("vector", "te", ctx.v("te"))
            V(lambda e, bk=bk: e.tensor_copy(cdecc[:, :], psum[bk][:, 0:128]), inc="ve")
            pool.release(bk, "ve", ctx.v("ve"))
            if is_main:
                ctx.wait("sync", "cmr", cum_rows_read[0])
                ctx.wait("sync", "ve", VE_CUM)
                DMA("sync", dram["cumhbm"][:, :], cumc[:, :], "cmw")
            return ctx.v("ve")

        dt_ready = [0]
        cum_rows_read = [0]

        w2c_state = {"ld": [0, 0], "cv": [0, 0], "st": [0, 0]}

        def w2conv_load(i):
            kt, cq = i // 4, i % 4
            par = i % 2
            ldsem = ["wc0", "wc1"][par]
            ctx.wait("sync", "se", w2c_state["cv"][par])
            w2c_state["ld"][par] = DMA(
                "sync", w2st[par][:, :],
                dram["w2t"][128 * kt:128 * (kt + 1), 1024 * cq:1024 * (cq + 1)], ldsem)

        def w2conv_piece(i):
            kt, cq = i // 4, i % 4
            par = i % 2
            stsem = ["wcw", "wcx"][par]
            ldsem = ["wc0", "wc1"][par]
            ctx.wait("scalar", ldsem, w2c_state["ld"][par])
            ctx.wait("scalar", stsem, w2c_state["st"][par])
            SC(lambda e, par=par: e.activation(w2cv[par][:, :], w2st[par][:, :],
                                               AF.Identity), inc="se")
            w2c_state["cv"][par] = ctx.v("se")
            ctx.wait("scalar", "se", ctx.v("se"))
            w2c_state["st"][par] = DMA(
                "scalar", dram["w2b16"][128 * kt:128 * (kt + 1), 1024 * cq:1024 * (cq + 1)],
                w2cv[par][:, :], stsem)

        def ssd_prefix():
            V(lambda e: e.memset(S[:, :], 0.0), inc="ve")
            for c in range(NCH):
                barrier()
                t0 = c * CS
                v_hs = [DMA("sync", hsr[it][:, :],
                            dram["hstok_p"][t0 + 128 * it:t0 + 128 * (it + 1), :],
                            ["hs0", "hs1"][it]) for it in range(2)]
                v_bt = [DMA("sync", btokr[jt][:, :],
                            dram["btok_p"][t0 + 128 * jt:t0 + 128 * (jt + 1), :],
                            ["bt0", "bt1"][jt]) for jt in range(2)]
                dt_pipeline(c, is_main=False)
                ssd_states_and_update(c, v_hs, v_bt, is_main=False)
            barrier()
            ctx.wait("vector", "xc", DX_CONST)
            V(lambda e: e.tensor_scalar(S[:, :], S[:, :], smaskc[:, 0:1], None,
                                        ALU.mult), inc="ve")

        def ssd_states_and_update(c, v_hs, v_bt, is_main):
            VE_PRE = ctx.v("ve")
            bw_used = [0, 0]
            bw_ready = [0, 0]

            def emit_bw(gg):
                par = gg % 2
                for jt in range(2):
                    ctx.wait("vector", "ve", VE_PRE)
                    ctx.wait("vector", ["bt0", "bt1"][jt], v_bt[jt])
                    ctx.wait("vector", "te", bw_used[par])
                    ctx.wait("vector", "ve", ctx.v("ve"))
                    V(lambda e, par=par, gg=gg, jt=jt: e.tensor_tensor(
                        bwb[par][:, jt, :, :],
                        btokr[jt][:, 128 * gg:128 * (gg + 1)].unsqueeze(1)
                        .broadcast_to([128, 16, 128]),
                        wtok[:, jt, 16 * gg:16 * (gg + 1)].unsqueeze(2)
                        .broadcast_to([128, 16, 128]), ALU.mult), inc="ve")
                bw_ready[par] = ctx.v("ve")

            emit_bw(0)
            for gg in range(G):
                par = gg % 2
                st_banks = [pool.acquire("tensor") for _ in range(2)]
                ctx.wait("tensor", "ve", bw_ready[par])
                if gg == 0:
                    ctx.wait("tensor", "hs0", v_hs[0])
                    ctx.wait("tensor", "hs1", v_hs[1])
                for jt in range(2):
                    for hslot in range(16):
                        hh = 16 * gg + hslot
                        stb = st_banks[hslot // 8]
                        MM(psum[stb][:, 64 * (hslot % 8):64 * (hslot % 8 + 1)],
                           bwb[par][:, jt, hslot, :],
                           hsr[jt][:, 64 * hh:64 * (hh + 1)],
                           (jt == 0 and hslot % 8 == 0), (jt == 1 and hslot % 8 == 7),
                           inc="te" if (hslot % 8 == 7) else None)
                bw_used[par] = ctx.v("te")
                TE_ST = ctx.v("te")
                if gg + 1 < G:
                    emit_bw(gg + 1)
                ctx.wait("vector", "te", TE_ST)
                ctx.wait("vector", "ve", ctx.v("ve"))
                V(lambda e, gg=gg: e.tensor_tensor(
                    S[:, 1024 * gg:1024 * (gg + 1)].rearrange("p (a b) -> p a b", a=16),
                    S[:, 1024 * gg:1024 * (gg + 1)].rearrange("p (a b) -> p a b", a=16),
                    cdecc[:, 16 * gg:16 * (gg + 1)].unsqueeze(2)
                    .broadcast_to([128, 16, 64]), ALU.mult), inc="ve")
                for hg in range(2):
                    stb = st_banks[hg]
                    ctx.wait("vector", "ve", ctx.v("ve"))
                    V(lambda e, gg=gg, hg=hg, stb=stb: e.tensor_tensor(
                        S[:, 1024 * gg + 512 * hg:1024 * gg + 512 * (hg + 1)],
                        S[:, 1024 * gg + 512 * hg:1024 * gg + 512 * (hg + 1)],
                        psum[stb][:, 0:512], ALU.add), inc="ve")
                    pool.release(stb, "ve", ctx.v("ve"))

        def ln2_tile(src_f32, dst_bf16, c0):
            V(lambda e: e.tensor_reduce(colb[:, c0:c0 + 1], src_f32,
                                        mybir.AxisListType.X, ALU.add), inc="ve")
            SC(lambda e: e.activation(dst_bf16, src_f32, AF.Square,
                                      accum_out=colb[:, c0 + 1:c0 + 2]), inc="se")
            ctx.wait("vector", "se", ctx.v("se"))
            ctx.wait("vector", "ve", ctx.v("ve"))
            V(lambda e: e.tensor_scalar(colb[:, c0:c0 + 1], colb[:, c0:c0 + 1],
                                        1.0 / HID, None, ALU.mult), inc="ve")
            ctx.wait("vector", "ve", ctx.v("ve"))
            V(lambda e: e.scalar_tensor_tensor(colb[:, c0 + 1:c0 + 2],
                                               colb[:, c0 + 1:c0 + 2], 1.0 / HID,
                                               colb[:, c0:c0 + 1], ALU.mult,
                                               ALU.bypass), inc="ve")
            ctx.wait("vector", "ve", ctx.v("ve"))
            V(lambda e: e.tensor_tensor(colb[:, c0 + 2:c0 + 3], colb[:, c0:c0 + 1],
                                        colb[:, c0:c0 + 1], ALU.mult), inc="ve")
            ctx.wait("vector", "ve", ctx.v("ve"))
            V(lambda e: e.tensor_tensor(colb[:, c0 + 1:c0 + 2], colb[:, c0 + 1:c0 + 2],
                                        colb[:, c0 + 2:c0 + 3], ALU.subtract), inc="ve")
            ctx.wait("scalar", "ve", ctx.v("ve"))
            SC(lambda e: e.activation(colb[:, c0 + 1:c0 + 2], colb[:, c0 + 1:c0 + 2],
                                      AF.Ln, bias=epscol[:, 0:1]), inc="se")
            ctx.wait("scalar", "se", ctx.v("se"))
            SC(lambda e: e.activation(colb[:, c0 + 1:c0 + 2], colb[:, c0 + 1:c0 + 2],
                                      AF.Exp, scale=-0.5), inc="se")
            ctx.wait("vector", "se", ctx.v("se"))
            V(lambda e: e.tensor_scalar(colb[:, c0 + 2:c0 + 3], colb[:, c0:c0 + 1],
                                        colb[:, c0 + 1:c0 + 2], -1.0,
                                        ALU.mult, ALU.mult), inc="ve")
            ctx.wait("scalar", "ve", ctx.v("ve"))
            SC(lambda e: e.activation(dst_bf16, src_f32, AF.Identity,
                                      bias=colb[:, c0 + 2:c0 + 3],
                                      scale=colb[:, c0 + 1:c0 + 2]), inc="se")
            return ctx.v("se")

        def out_quarter(q):
            barrier()
            w2dram = dram["w2b16"].rearrange("(kt kl) c -> kl kt c", kl=128)
            w2sem = ["w2a", "w2b"]
            w2_ready = [0, 0]
            w2_used = [0, 0]
            xpsem = ["xp0", "xp1"]
            xp_ready = [0, 0]
            xp_used = [0, 0]
            oqsem = ["oq0", "oq1"]
            oq_st = [0, 0]
            oq_used = [0, 0]

            def load_w2(slot, hb, kt8):
                ctx.wait("sync", "te", w2_used[slot])
                w2_ready[slot] = DMA(
                    "sync", w2p[slot][:, :, :],
                    w2dram[:, 8 * kt8:8 * kt8 + 8, 512 * hb:512 * (hb + 1)],
                    w2sem[slot])

            xunits = [(hb, tq) for hb in range(8) for tq in range(4)]

            def load_xp(slot, hb, tq):
                ctx.wait("sync", "ve", xp_used[slot])
                xp_ready[slot] = DMA(
                    "sync", xpc[slot][:, :],
                    dram["x_main"][HALO + 512 * q + 128 * tq:
                                   HALO + 512 * q + 128 * (tq + 1),
                                   512 * hb:512 * (hb + 1)], xpsem[slot])

            units = [(hb, kt8) for hb in range(8) for kt8 in range(8)]
            load_w2(0, 0, 0)
            load_w2(1, 0, 1)
            for i in range(2):
                load_xp(i, *xunits[i])
            for ui, (hb, kt8) in enumerate(units):
                slot = ui % 2
                ctx.wait("tensor", w2sem[slot], w2_ready[slot])
                if kt8 == 0:
                    hb_banks = [pool.acquire("tensor") for _ in range(4)]
                for tq in range(4):
                    for kl in range(8):
                        kt = 8 * kt8 + kl
                        MM(psum[hb_banks[tq]][:, 0:512],
                           gTacc[:, kt, 128 * tq:128 * (tq + 1)],
                           w2p[slot][:, kl, :], kt == 0, kt == 63,
                           inc="te" if (kl == 7 and tq == 3) else None)
                w2_used[slot] = ctx.v("te")
                if ui + 2 < len(units):
                    load_w2(slot, *units[ui + 2])
                if kt8 == 7:
                    TE_HB = ctx.v("te")
                    for tq in range(4):
                        ci = hb * 4 + tq
                        xslot = ci % 2
                        opar = ci % 2
                        ctx.wait("vector", "te", TE_HB)
                        ctx.wait("vector", xpsem[xslot], xp_ready[xslot])
                        ctx.wait("vector", oqsem[opar], oq_used[opar])
                        V(lambda e, tq=tq, xslot=xslot, opar=opar: e.tensor_tensor(
                            oq[opar][:, :], psum[hb_banks[tq]][:, 0:512],
                            xpc[xslot][:, :], ALU.add), inc="ve")
                        xp_used[xslot] = ctx.v("ve")
                        pool.release(hb_banks[tq], "ve", ctx.v("ve"))
                        if ci + 2 < len(xunits):
                            load_xp(xslot, *xunits[ci + 2])
                        ctx.wait("sync", "ve", ctx.v("ve"))
                        oq_used[opar] = DMA(
                            "sync",
                            dram["x1s"][512 * q + 128 * tq:512 * q + 128 * (tq + 1),
                                        512 * hb:512 * (hb + 1)],
                            oq[opar][:, :], oqsem[opar])
            ctx.wait("sync", oqsem[0], oq_used[0])
            ctx.wait("sync", oqsem[1], oq_used[1])
            x1qsem = ["x1q0", "x1q1"]
            x1q_ready = [0, 0]
            for tq in range(4):
                tok0 = 512 * q + 128 * tq
                par = tq % 2
                if tq >= 2:
                    ctx.wait("sync", "se", x1q_done[par])
                x1q_ready[par] = DMA("sync", x1q[par][:, :],
                                     dram["x1s"][tok0:tok0 + 128, :], x1qsem[par])
                ctx.wait("vector", x1qsem[par], x1q_ready[par])
                ctx.wait("scalar", x1qsem[par], x1q_ready[par])
                if tq >= 1:
                    ctx.wait("scalar", "te", h2_tpose_done[0])
                se_h2 = ln2_tile(x1q[par][:, :], hbuf[:, :], 8)
                x1q_done[par] = se_h2
                ctx.wait("tensor", "se", se_h2)
                for half in range(2):
                    for rr in range(2):
                        bk2 = pool.acquire("tensor")
                        pb = psum[bk2][:, :].bitcast(BF16)
                        for kk in range(8):
                            kt = 16 * half + rr * 8 + kk
                            TR(pb[:, 128 * kk:128 * (kk + 1)],
                               hbuf[:, 128 * kt:128 * (kt + 1)], ident16[:, :],
                               inc="te" if kk == 7 else None)
                        ctx.wait("vector", "te", ctx.v("te"))
                        ctx.wait("vector", "h2w", h2w_done[0])
                        V(lambda e, rr=rr, half=half, pb=pb: e.tensor_copy(
                            tst2[:, 8 * rr:8 * rr + 8, :],
                            pb[:, 0:1024].rearrange("p (a b) -> p a b", a=8)), inc="ve")
                        pool.release(bk2, "ve", ctx.v("ve"))
                        if rr == 1:
                            ctx.wait("sync", "ve", ctx.v("ve"))
                            h2w_done[0] = DMA(
                                "sync",
                                dram["h2t"].rearrange("(kh kl) t -> kl kh t", kl=128)
                                [:, 16 * half:16 * half + 16, tok0:tok0 + 128],
                                tst2[:, :, :], "h2w")
                h2_tpose_done[0] = ctx.v("te")

        x1q_done = [0, 0]
        h2_tpose_done = [0]
        h2w_done = [0]

        def ssd_main():
            barrier()
            build_dbc()
            if debug:
                import os as _os
                if _os.environ.get("DBG_STOP") == "dbc":
                    ctx.wait("gpsimd", "ve", ctx.v("ve"))
                    DMA("gpsimd", dram["dbg2"][0:128, :], dbc[:, :], "dbg")
                    return
            for c in range(NCH):
                barrier()
                t0 = c * CS
                v_hs = [DMA("sync", hsr[it][:, :],
                            dram["hstok"][t0 + 128 * it:t0 + 128 * (it + 1), :],
                            ["hs0", "hs1"][it]) for it in range(2)]
                v_bt = [DMA("sync", btokr[jt][:, :],
                            dram["btok"][t0 + 128 * jt:t0 + 128 * (jt + 1), :],
                            ["bt0", "bt1"][jt]) for jt in range(2)]
                dt_pipeline(c, is_main=True)
                VE_DT = ctx.v("ve")
                SE_DT = ctx.v("se")
                cmw_v = ctx.v("cmw")
                for it in range(2):
                    ctx.wait("vector", ["hs0", "hs1"][it], v_hs[it])
                    ctx.wait("vector", "te", dxb_free[0])
                    V(lambda e, it=it: e.tensor_tensor(dxb[:, :], hsr[it][:, :],
                                                       dbc[:, :], ALU.mult), inc="ve")
                    VE_DX = ctx.v("ve")
                    for gg in range(G):
                        par = gg % 2
                        ctx.wait("sync", "te", bc_used[par])
                        vbc = DMA("sync", bchr[par][:, :],
                                  dram["bch"][128 * gg:128 * (gg + 1), t0:t0 + CS],
                                  ["bc0", "bc1"][par])
                        vbc = DMA("sync", cchr[par][:, :],
                                  dram["cch"][128 * gg:128 * (gg + 1), t0:t0 + CS],
                                  ["bc0", "bc1"][par])
                        ctx.wait("sync", gt_used[par][0], gt_used[par][1])
                        vgt = DMA("sync", gater[par][:, :],
                                  dram["gate"][t0 + 128 * it:t0 + 128 * (it + 1),
                                               1024 * gg:1024 * (gg + 1)],
                                  ["gt0", "gt1"][par])
                        bsc = pool.acquire("tensor", which=0)
                        ctx.wait("tensor", ["bc0", "bc1"][par], vbc)
                        njt = it + 1
                        MM(psum[0][:, 0:128], bchr[par][:, 0:128],
                           cchr[par][:, 128 * it:128 * (it + 1)], True, njt == 1,
                           inc="te" if njt == 1 else None)
                        if it == 1:
                            MM(psum[0][:, 128:256], bchr[par][:, 128:256],
                               cchr[par][:, 128:256], False, True, inc="te")
                        bc_used[par] = ctx.v("te")
                        TE_SC = ctx.v("te")
                        ctx.wait("vector", "te", TE_SC)
                        V(lambda e, it=it: e.tensor_copy(
                            scraw[:, 0:128 * (it + 1)],
                            psum[0][:, 0:128 * (it + 1)]), inc="ve")
                        pool.release(0, "ve", ctx.v("ve"))
                        VE_SCRAW = ctx.v("ve")
                        yb = [4, 5]
                        zb = [6, 7]
                        TE_DB = [0, 0, 0]

                        def dec_stage1(hp):
                            dpar = hp % 3
                            hh = 2 * hp
                            if hh % 4 == 0:
                                ctx.wait("sync", "te", cumr_used[0])
                                ctx.wait("sync", "cmw", cmw_v)
                                sq = hh // 4
                                cum_flat = dram["cumhbm"].rearrange("h t -> (h t)")
                                cumr_v[0] = DMA("sync", cumr[0:1, :],
                                                cum_flat[256 * 4 * sq:256 * 4 * (sq + 1)].unsqueeze(0),
                                                "cmr")
                            pool.acquire("tensor", which=1 + dpar)
                            ctx.wait("tensor", "cmr", cumr_v[0])
                            base = (hh % 4) * 256 + 128 * it
                            crow2 = cumr[0:1, base:base + 256 + 128].rearrange(
                                "o (a b) -> o a b", a=3)[:, 0::2, :]
                            if it == 0:
                                MM(psum[1 + dpar][:, 0:256], onesrow[0:1, :], crow2,
                                   True, False)
                                MM(psum[1 + dpar][:, 0:256], ident32[:, :],
                                   mneg32[:, :].unsqueeze(1).broadcast_to([128, 2, 128]),
                                   False, True, inc="te")
                            else:
                                MM(psum[1 + dpar][:, 0:256], onesrow[0:1, :], crow2,
                                   True, False)
                                MM(psum[1 + dpar][:, 256:512], onesrow[0:1, :], crow2,
                                   False, False)
                                MM(psum[1 + dpar][:, 256:512], ident32[:, :],
                                   mneg32[:, :].unsqueeze(1).broadcast_to([128, 2, 128]),
                                   False, True, inc="te")
                            cumr_used[0] = ctx.v("te")
                            TE_DB[dpar] = ctx.v("te")

                        def dec_stage2(hh, hg):
                            dpar = (hh // 2) % 3
                            odd = hh % 2
                            hh8 = hh % 8
                            ctx.wait("scalar", "te", TE_DB[dpar])
                            ctx.wait("scalar", "ve", VE_DT)
                            ctx.wait("scalar", "ve", dec_v_used[dpar])
                            for jt in range(it + 1):
                                if it == 0:
                                    po = 128 * odd
                                else:
                                    po = 128 * odd + (256 if jt == 1 else 0)
                                SC(lambda e, dpar=dpar, jt=jt, hh=hh, po=po: e.activation(
                                    decb[dpar][:, 128 * jt:128 * (jt + 1)],
                                    psum[1 + dpar][:, po:po + 128], AF.Exp,
                                    bias=sbtok[:, jt, hh:hh + 1]), inc="se")
                            if odd == 1:
                                pool.release(1 + dpar, "se", ctx.v("se"))
                                dec_used[dpar] = 0
                            SE_DEC = ctx.v("se")
                            ctx.wait("vector", "se", SE_DEC)
                            ctx.wait("vector", "ve", VE_SCRAW)
                            ctx.wait("vector", "te", sch_used[hh % 3])
                            ctx.wait("vector", "ve", dec_v_used[dpar])
                            V(lambda e, dpar=dpar, it=it, hh=hh: e.tensor_tensor(
                                schb[hh % 3][:, 0:128 * (it + 1)],
                                scraw[:, 0:128 * (it + 1)],
                                decb[dpar][:, 0:128 * (it + 1)], ALU.mult),
                              inc="ve")
                            dec_v_used[dpar] = ctx.v("ve")
                            SCH_V = ("ve", ctx.v("ve"))
                            ctx.wait("tensor", SCH_V[0], SCH_V[1])
                            for jt in range(it + 1):
                                ctx.wait("tensor", ["hs0", "hs1"][jt], v_hs[jt])
                                MM(psum[yb[hg]][:, 64 * hh8:64 * (hh8 + 1)],
                                   schb[hh % 3][:, 128 * jt:128 * (jt + 1)],
                                   hsr[jt][:, 64 * hh:64 * (hh + 1)],
                                   False, False, inc="te")
                            sch_used[hh % 3] = ctx.v("te")
                        dec_stage1(8 * gg)
                        dec_stage1(8 * gg + 1)
                        for hg in range(2):
                            pool.acquire("tensor", which=yb[hg])
                            pool.acquire("tensor", which=zb[hg])
                            h0 = 16 * gg + 8 * hg
                            ctx.wait("tensor", "ve", VE_DX)
                            MM(psum[yb[hg]][:, 0:512], ident16[:, :],
                               dxb[:, 64 * h0:64 * h0 + 512], True, False, inc="te")
                            ctx.wait("tensor", "ve", s_ready[0])
                            MM(psum[zb[hg]][:, 0:512],
                               cchr[par][:, 128 * it:128 * (it + 1)],
                               S[:, 64 * h0:64 * h0 + 512], True, True, inc="te")
                            bc_used[par] = ctx.v("te")
                            for hh8 in range(8):
                                hh = h0 + hh8
                                if hh % 2 == 1 and hh + 3 < 16 * (gg + 1):
                                    dec_stage1((hh + 3) // 2)
                                dec_stage2(hh, hg)
                            TE_YZ = ctx.v("te")
                            ctx.wait("vector", "te", TE_YZ)
                            ctx.wait("vector", "se", SE_DT)
                            ctx.wait("vector", "ve", ctx.v("ve"))
                            V(lambda e, hg=hg, it=it, h0=h0: e.tensor_tensor(
                                yzbuf[:, 0:512].rearrange("p (a b) -> p a b", a=8),
                                psum[zb[hg]][:, 0:512].rearrange("p (a b) -> p a b", a=8),
                                ectt[:, it, h0:h0 + 8].unsqueeze(2).broadcast_to([128, 8, 64]),
                                ALU.mult), inc="ve")
                            pool.release(zb[hg], "ve", ctx.v("ve"))
                            ctx.wait("vector", "ve", ctx.v("ve"))
                            V(lambda e, hg=hg, h0=h0: e.scalar_tensor_tensor(
                                g0b[:, 64 * h0:64 * h0 + 512],
                                psum[yb[hg]][:, 0:512], 1.0, yzbuf[:, 0:512],
                                ALU.mult, ALU.add), inc="ve")
                            pool.release(yb[hg], "ve", ctx.v("ve"))
                            G0_V = ctx.v("ve")
                            ctx.wait("vector", ["gt0", "gt1"][par], vgt)
                            ctx.wait("vector", "ve", G0_V)
                            V(lambda e, h0=h0, hg=hg, par=par: e.tensor_tensor(
                                g0b[:, 64 * h0:64 * h0 + 512],
                                g0b[:, 64 * h0:64 * h0 + 512],
                                gater[par][:, 512 * hg:512 * (hg + 1)], ALU.mult),
                              inc="ve")
                            gt_used[par] = ("ve", ctx.v("ve"))
                    ctx.wait("scalar", "ve", ctx.v("ve"))
                    for gg in range(G):
                        SC(lambda e, gg=gg: e.activation(
                            gater[0][:, :], g0b[:, 1024 * gg:1024 * (gg + 1)],
                            AF.Square, accum_out=colb[:, 16 + gg:17 + gg]), inc="se")
                        ctx.wait("scalar", "se", ctx.v("se"))
                    gt_used[0] = ("se", ctx.v("se"))
                    SC(lambda e: e.activation(rstd8[:, :], colb[:, 16:24], AF.Ln,
                                              scale=1.0 / 1024.0,
                                              bias=epscol[:, 0:1]), inc="se")
                    ctx.wait("scalar", "se", ctx.v("se"))
                    SC(lambda e: e.activation(rstd8[:, :], rstd8[:, :], AF.Exp,
                                              scale=-0.5), inc="se")
                    SE_RSTD = ctx.v("se")
                    ctx.wait("vector", "se", SE_RSTD)
                    ctx.wait("vector", "te", dxb_free[0])
                    for gg in range(G):
                        V(lambda e, gg=gg: e.tensor_scalar(
                            dxb[:, 1024 * gg:1024 * (gg + 1)],
                            g0b[:, 1024 * gg:1024 * (gg + 1)],
                            rstd8[:, gg:gg + 1], None, ALU.mult), inc="ve")
                    VE_G = ctx.v("ve")
                    if debug:
                        import os as _os
                        _what = _os.environ.get("DBG_G", "g")
                        ctx.wait("sync", "ve", VE_G)
                        if _what == "g":
                            DMA("sync", dram["gdbg"][t0 + 128 * it:t0 + 128 * (it + 1), :],
                                dxb[:, :], "dout")
                        elif _what == "g0":
                            DMA("sync", dram["gdbg"][t0 + 128 * it:t0 + 128 * (it + 1), :],
                                g0b[:, :], "dout")
                        if False:
                            pass
                    ctx.wait("tensor", "ve", VE_G)
                    tloc = (c % 2) * 2 + it
                    for rr in range(8):
                        bk2 = pool.acquire("tensor")
                        pb = psum[bk2][:, :].bitcast(BF16)
                        for kk in range(8):
                            kt = rr * 8 + kk
                            TR(pb[:, 128 * kk:128 * (kk + 1)],
                               dxb[:, 128 * kt:128 * (kt + 1)], ident16[:, :],
                               inc="te" if kk == 7 else None)
                        ctx.wait("vector", "te", ctx.v("te"))
                        for kk in range(8):
                            kt = rr * 8 + kk
                            V(lambda e, kt=kt, kk=kk, pb=pb, tloc=tloc: e.tensor_scalar(
                                gTacc[:, kt, 128 * tloc:128 * (tloc + 1)],
                                pb[:, 128 * kk:128 * (kk + 1)],
                                nws[:, kt:kt + 1], None, ALU.mult), inc="ve")
                        pool.release(bk2, "ve", ctx.v("ve"))
                    dxb_free[0] = ctx.v("te")
                ssd_states_and_update(c, v_hs, v_bt, is_main=True)
                s_ready[0] = ctx.v("ve")
                cum_rows_read[0] = ctx.v("cmr")
                if c % 2 == 1:
                    out_quarter(c // 2)
                    if c < NCH - 1:
                        barrier()
                        build_dbc()

        dxb_free = [0]
        bc_used = [0, 0]
        gt_used = [("ve", 0), ("ve", 0)]
        cumr_used = [0]
        cumr_v = [0]
        dec_used = [0, 0, 0]
        dec_g_used = [0, 0, 0]
        dec_v_used = [0, 0, 0]
        sch_used = [0, 0, 0]
        s_ready = [0]

        def ffn():
            barrier()
            build_fbb()
            DMA("sync", sczb[:, :, :], dram["scz"], "xc")
            DMA("sync", zrzb[:, :, :], dram["zrz"], "xc")
            XC_SZ = ctx.v("xc")
            qwsem = ["qw0", "qw1"]
            qw_ready = [0, 0]
            qw_used = [0, 0]
            x1sem = ["x1r", "fbr"]
            h2dram = dram["h2t"].rearrange("(kt kl) t -> kl kt t", kl=128)
            for half in range(2):
                ctx.wait("sync", "te", h2th_used[0])
                vh2 = DMA("sync", h2th[:, :, :],
                          h2dram[:, :, 1024 * half:1024 * (half + 1)], "h2r")
                for i in range(2):
                    ctx.wait("sync", qwsem[i], qw_ready[i])
                    ctx.wait("sync", "ve", qw_used[i])
                    ctx.wait("sync", "cv3", ctx.v("cv3"))
                    qw_ready[i] = DMA("sync", qwb[i][:, :],
                                      dram["qwb16"][128 * i:128 * (i + 1), :],
                                      qwsem[i])
                for ob in range(8):
                    for otl in range(4):
                        ot = 4 * ob + otl
                        slot = ot % 2
                        ctx.wait("vector", qwsem[slot], qw_ready[slot])
                        ctx.wait("vector", "xc", XC_SZ)
                        ctx.wait("vector", "te", wdq_used[0])
                        ctx.wait("vector", "ve", ctx.v("ve"))
                        V(lambda e, slot=slot, ot=ot: e.tensor_tensor(
                            wdq[:, :].rearrange("p (a b) -> p a b", a=32),
                            qwb[slot][:, :].rearrange("p (a b) -> p a b", a=32),
                            sczb[:, ot, :].unsqueeze(2).broadcast_to([128, 32, 128]),
                            ALU.mult), inc="ve")
                        ctx.wait("vector", "ve", ctx.v("ve"))
                        V(lambda e, ot=ot: e.tensor_tensor(
                            wdq[:, :].rearrange("p (a b) -> p a b", a=32),
                            wdq[:, :].rearrange("p (a b) -> p a b", a=32),
                            zrzb[:, ot, :].unsqueeze(2).broadcast_to([128, 32, 128]),
                            ALU.add), inc="ve")
                        qw_used[slot] = ctx.v("ve")
                        VE_DQ = ctx.v("ve")
                        if ot + 2 < 32 or half == 0:
                            nt = (ot + 2) % 32
                            ctx.wait("sync", "ve", qw_used[slot])
                            qw_ready[slot] = DMA("sync", qwb[slot][:, :],
                                                 dram["qwb16"][128 * nt:128 * (nt + 1), :],
                                                 qwsem[slot])
                        ctx.wait("tensor", "ve", VE_DQ)
                        ctx.wait("tensor", "te", wT_used[0])
                        for rr in range(4):
                            bk2 = pool.acquire("tensor")
                            pb = psum[bk2][:, :].bitcast(BF16)
                            for kk in range(8):
                                kt = rr * 8 + kk
                                TR(pb[:, 128 * kk:128 * (kk + 1)],
                                   wdq[:, 128 * kt:128 * (kt + 1)], ident16[:, :],
                                   inc="te" if kk == 7 else None)
                            ctx.wait("scalar", "te", ctx.v("te"))
                            SC(lambda e, rr=rr, otl=otl, pb=pb: e.activation(
                                wT[:, 8 * rr:8 * rr + 8, 128 * otl:128 * (otl + 1)],
                                pb[:, 0:1024].rearrange("p (a b) -> p a b", a=8),
                                AF.Identity), inc="se")
                            pool.release(bk2, "se", ctx.v("se"))
                        wdq_used[0] = ctx.v("te")
                    WT_V = ctx.v("se")
                    ctx.wait("tensor", "se", WT_V)
                    ctx.wait("tensor", "h2r", vh2)
                    for tt in range(8):
                        bk2 = pool.acquire("tensor")
                        for kt in range(NKT):
                            MM(psum[bk2][:, 0:512], h2th[:, kt, 128 * tt:128 * (tt + 1)],
                               wT[:, kt, :], kt == 0, kt == NKT - 1,
                               inc="te" if kt == NKT - 1 else None)
                        h2th_used[0] = ctx.v("te")
                        wT_used[0] = ctx.v("te")
                        TE_O = ctx.v("te")
                        par = tt % 2
                        ctx.wait("sync", "ve", x1p_used[par])
                        vx1 = DMA("sync", x1p[par][:, :],
                                  dram["x1s"][1024 * half + 128 * tt:
                                              1024 * half + 128 * (tt + 1),
                                              512 * ob:512 * (ob + 1)],
                                  ["x1r", "fbr"][par])
                        ctx.wait("vector", "te", TE_O)
                        ctx.wait("vector", ["dout", "dou2"][par], outst_used[par])
                        V(lambda e, bk2=bk2, ob=ob, par=par: e.tensor_tensor(
                            outst[par][:, :], psum[bk2][:, 0:512],
                            fbb[:, 512 * ob:512 * (ob + 1)], ALU.add), inc="ve")
                        pool.release(bk2, "ve", ctx.v("ve"))
                        ctx.wait("vector", "ve", ctx.v("ve"))
                        ctx.wait("vector", ["x1r", "fbr"][par], vx1)
                        V(lambda e, par=par: e.tensor_tensor(
                            outst[par][:, :], outst[par][:, :], x1p[par][:, :],
                            ALU.add), inc="ve")
                        x1p_used[par] = ctx.v("ve")
                        VE_OUT = ctx.v("ve")
                        ctx.wait("sync", "ve", VE_OUT)
                        dv = DMA("sync",
                                 dram["out"][1024 * half + 128 * tt:
                                             1024 * half + 128 * (tt + 1),
                                             512 * ob:512 * (ob + 1)],
                                 outst[par][:, :], ["dout", "dou2"][par])
                        outst_used[par] = dv

        h2th_used = [0]
        wdq_used = [0]
        wT_used = [0]
        x1p_used = [0, 0]
        outst_used = [0, 0]

        if stages >= 2:
            htr = stageA_ln(dram["x_pref"])
            stageB_proj(is_main=False, ht_ready=htr)
            dt_ready[0] = ctx.v("ve")
            ssd_prefix()
            if debug:
                barrier()
                DMA("sync", dram["sdbg"][:, :], S[:, :], "dout")
            barrier()

        htr = stageA_ln(dram["x_main"])
        stageB_proj(is_main=True, ht_ready=htr)
        dt_ready[0] = ctx.v("ve")
        if debug:
            V(lambda e: e.memset(dtbuf[:, 0:128], 0.0), inc="ve")
            ctx.wait("sync", "ve", ctx.v("ve"))
            ctx.wait("sync", "se", ctx.v("se"))
            DMA("sync", dram["dtdbg"][:, :], dtbuf[:, :], "dout")
        if stages >= 3:
            ssd_main()
        if stages >= 4:
            ffn()

        for s in sem_names:
            if s not in ("te", "ve", "se", "ge"):
                ctx.wait("sync", s, ctx.v(s))

    with nc.Block() as block:
        @block.sync
        def _(sync):
            program(Ctx(nc, "sync", sems))

        @block.tensor
        def _(tensor):
            program(Ctx(nc, "tensor", sems))

        @block.vector
        def _(vector):
            program(Ctx(nc, "vector", sems))

        @block.scalar
        def _(scalar):
            program(Ctx(nc, "scalar", sems))

        @block.gpsimd
        def _(gpsimd):
            program(Ctx(nc, "gpsimd", sems))

    es.close()
    return nc


def stage_inputs(inputs):
    x = np.ascontiguousarray(inputs["L0_attn_input_states"])
    w1t = np.ascontiguousarray(inputs["L0_attn_in_proj_w"].T)
    w2t = np.ascontiguousarray(inputs["L0_attn_out_w"].T)
    qw = np.ascontiguousarray(inputs["L0_ffn_qweight"].reshape(HID, HID))
    scz = np.ascontiguousarray(
        inputs["L0_ffn_scales"][:, :, 0].reshape(32, 128, 32).transpose(1, 0, 2))
    zrz = np.ascontiguousarray(
        inputs["L0_ffn_scaled_zeros"][:, :, 0].reshape(32, 128, 32).transpose(1, 0, 2))
    convw = np.ascontiguousarray(
        inputs["L0_attn_conv_w"][:, 0, :].reshape(80, 128, 4).transpose(1, 0, 2))
    convb = np.ascontiguousarray(
        inputs["L0_attn_conv_b"].reshape(80, 128).T)
    dtb = np.ascontiguousarray(inputs["L0_attn_dt_bias"].reshape(H, 1))
    alog = np.ascontiguousarray(inputs["L0_attn_A_log"].reshape(H, 1))
    dvec = np.ascontiguousarray(inputs["L0_attn_D"].reshape(H, 1))
    nw = np.ascontiguousarray(inputs["L0_attn_norm_w"].reshape(64, 128).T)
    fbias = np.ascontiguousarray(inputs["L0_ffn_bias"].reshape(1, HID))

    in_maps = []
    for core in range(8):
        bi, hi = core // 2, core % 2
        xm = np.zeros((TOKH, HID), np.float32)
        xm[HALO:] = x[bi, hi * TOK:(hi + 1) * TOK]
        if hi == 1:
            xm[:HALO] = x[bi, TOK - HALO:TOK]
        xp = np.zeros((TOKH, HID), np.float32)
        if hi == 1:
            xp[HALO:] = x[bi, 0:TOK]
        smask = np.full((128, 1), float(hi), np.float32)
        in_maps.append({
            "x_main": xm, "x_pref": xp, "w1t": w1t, "w2t": w2t, "qw": qw,
            "scz": scz, "zrz": zrz, "convw": convw, "convb": convb,
            "dtb": dtb, "alog": alog, "dvec": dvec, "nw": nw, "fbias": fbias,
            "smask": smask,
        })
    return in_maps


def kernel(**inputs):
    in_maps = stage_inputs(inputs)
    nc = build_program(debug=False, stages=4)
    res = run_bass_kernel_spmd(nc, in_maps, core_ids=list(range(8)))
    out = np.empty((BATCH, SEQ, HID), np.float32)
    for core in range(8):
        bi, hi = core // 2, core % 2
        out[bi, hi * TOK:(hi + 1) * TOK] = res.results[core]["out"]
    return out


# version 11 (final)
# speedup vs baseline: 1.2040x; 1.1331x over previous
import numpy as np
import ml_dtypes
import concourse.bass as bass
import concourse.mybir as mybir
from concourse.bass_utils import run_bass_kernel_spmd
from contextlib import ExitStack

F32 = mybir.dt.float32
BF16 = mybir.dt.bfloat16
I32 = mybir.dt.int32
AF = mybir.ActivationFunctionType
ALU = mybir.AluOpType

BATCH, SEQ, HID = 4, 4096, 4096
H, PDIM, NDIM, G, KCONV, CS = 128, 64, 128, 8, 4, 256
INTER = H * PDIM
CONVCH = INTER + 2 * G * NDIM
PROJ = INTER + CONVCH + H
EPS = 1e-5

TOK = 2048
HALO = 128
TOKH = TOK + HALO
NCH = TOK // CS
NKT = HID // 128
NITK = INTER // 128

GATE_T0 = 0
HS_T0 = INTER // 128
B_T0 = HS_T0 + 64
C_T0 = B_T0 + 8
DT_T0 = C_T0 + 8

DEBUG_TENSORS = {"hstok", "gate", "bch", "btok", "cch", "hstok_p", "btok_p",
                 "x1s", "h2t", "dtdbg", "sdbg", "gdbg", "dbg2"}


class Ctx:
    def __init__(self, nc, cur, sems):
        self.nc = nc
        self.cur = cur
        self.sems = sems
        self.cnt = {k: 0 for k in sems}

    def eng(self, name):
        return getattr(self.nc, name) if self.cur == name else None

    def inc(self, instr, sem, k):
        self.cnt[sem] += k
        if instr is not None:
            instr.then_inc(self.sems[sem], k)
        return self.cnt[sem]

    def wait(self, engname, sem, val):
        if val <= 0:
            return
        e = self.eng(engname)
        if e is not None:
            e.wait_ge(self.sems[sem], val)

    def v(self, sem):
        return self.cnt[sem]


class BankPool:
    def __init__(self, ctx, banks):
        self.ctx = ctx
        self.banks = banks
        self.free_at = [None] * len(banks)
        self.rr = 0

    def acquire(self, producer_eng, which=None):
        if which is None:
            which = self.rr % len(self.banks)
            self.rr += 1
        fa = self.free_at[which]
        if fa is not None:
            self.ctx.wait(producer_eng, fa[0], fa[1])
            self.free_at[which] = None
        return which

    def release(self, which, sem, val):
        self.free_at[which] = (sem, val)


def build_program(debug=False, stages=4):
    nc = bass.Bass("TRN2", target_bir_lowering=False, debug=False)

    dram = {}

    def din(name, shape, dtype):
        dram[name] = nc.dram_tensor(name, shape, dtype, kind="ExternalInput").ap()

    def dint(name, shape, dtype):
        if debug and name in DEBUG_TENSORS:
            dram[name] = nc.dram_tensor(name, shape, dtype, kind="ExternalOutput").ap()
        else:
            dram[name] = nc.dram_tensor(name, shape, dtype).ap()

    din("x_main", [TOKH, HID], F32)
    din("x_pref", [TOKH, HID], F32)
    din("w1t", [HID, PROJ], F32)
    din("w2t", [INTER, HID], F32)
    din("qw", [HID, HID], I32)
    din("scz", [128, 32, 32], F32)
    din("zrz", [128, 32, 32], F32)
    din("convw", [128, 80, 4], F32)
    din("convb", [128, 80], F32)
    din("dtb", [H, 1], F32)
    din("alog", [H, 1], F32)
    din("dvec", [H, 1], F32)
    din("nw", [128, 64], F32)
    din("fbias", [1, HID], F32)
    din("smask", [128, 1], F32)
    out_hbm = nc.dram_tensor("out", [TOK, HID], F32, kind="ExternalOutput").ap()
    dram["out"] = out_hbm

    dint("hstok", [TOK, INTER], BF16)
    dint("gate", [TOK, INTER], BF16)
    dint("bch", [G * NDIM, TOK], BF16)
    dint("btok", [TOK, G * NDIM], BF16)
    dint("cch", [G * NDIM, TOK], BF16)
    dint("hstok_p", [TOK, INTER], BF16)
    dint("btok_p", [TOK, G * NDIM], BF16)
    dint("cumhbm", [H, CS], F32)
    dint("x1s", [TOK, HID], F32)
    dint("h2t", [HID, TOK], BF16)
    dint("w2b16", [INTER, HID], BF16)
    dint("w1b16", [HID, PROJ], BF16)
    dint("qwb16", [HID, HID], BF16)
    if debug:
        dint("dtdbg", [H, TOKH], F32)
        dint("sdbg", [NDIM, INTER], BF16)
        dint("gdbg", [TOK, INTER], BF16)
        dint("dbg2", [768, INTER], F32)

    es = ExitStack()
    sb = lambda n, s, d: es.enter_context(nc.sbuf_tensor(n, s, d))
    psb = lambda n, s, d: es.enter_context(nc.psum_tensor(n, s, d))

    ident16 = sb("ident16", [128, 128], BF16)
    ident32 = sb("ident32", [128, 128], F32)
    tril16 = sb("tril16", [128, 128], BF16)
    mneg32 = sb("mneg32", [128, 128], F32)
    onesrow = sb("onesrow", [1, 128], F32)
    rowst = sb("rowst", [1, 512], F32)
    conww = sb("conww", [128, 80, 4], F32)
    conbb = sb("conbb", [128, 80], F32)
    acol = sb("acol", [128, 1], F32)
    dtbcol = sb("dtbcol", [128, 1], F32)
    dvcol = sb("dvcol", [128, 1], F32)
    smaskc = sb("smaskc", [128, 1], F32)
    epscol = sb("epscol", [128, 1], F32)
    onecol = sb("onecol", [128, 1], F32)
    drowp = sb("drowp", [1, 128], F32)
    nws = sb("nws", [128, 64], F32)
    colb = sb("colb", [128, 32], F32)
    dtbuf = sb("dtbuf", [128, TOKH], F32)
    S = sb("S", [128, INTER], BF16)


    ARENA_BYTES = 180224
    arena = sb("arena", [128, ARENA_BYTES], mybir.dt.uint8)
    abase = nc.lookup_mloc(arena).addr

    def at(name, shape, dtype, rel):
        t = nc.alloc_sbuf_tensor_at(name, shape, dtype, offset=abase + rel)
        return t

    biga = at("biga", [128, 69632], BF16, 0)
    gTacc = at("gTacc", [128, NITK, 512], BF16, 0)
    h2th = at("h2th", [128, NKT, 1024], BF16, 0)
    hsr = [at(f"hsr{i}", [128, INTER], BF16, 65536 + 16384 * i) for i in range(2)]
    x1q = [at(f"x1q{i}", [128, HID], F32, 65536 + 16384 * i) for i in range(2)]
    qwb = [at(f"qwb{i}", [128, HID], BF16, 65536 + 8192 * i) for i in range(2)]
    gater = [at(f"gater{i}", [128, 1024], BF16, 98304 + 2048 * i) for i in range(2)]
    g0b = at("g0b", [128, INTER], BF16, 102400)
    wT = at("wT", [128, NKT, 512], BF16, 98304)
    cumr = at("cumr", [1, 1024], F32, 118784)
    bchr = [at(f"bchr{i}", [128, CS], BF16, 122880 + 512 * i) for i in range(2)]
    cchr = [at(f"cchr{i}", [128, CS], BF16, 123904 + 512 * i) for i in range(2)]
    btokr = [at(f"btokr{i}", [128, G * NDIM], BF16, 124928 + 2048 * i) for i in range(2)]
    xpc = [at(f"xpc{i}", [128, 512], F32, 124928 + 2048 * i) for i in range(2)]
    wdq = at("wdq", [128, HID], BF16, 131072)
    xbuf = at("xbuf", [128, HID], F32, 139264)
    hbuf = at("hbuf", [128, HID], BF16, 155648)
    xbcst = at("xbcst", [128, TOKH], BF16, 139264)
    convo = at("convo", [128, TOK], F32, 143616)
    spst = at("spst", [128, TOK], BF16, 151808)
    spstB = at("spstB", [128, TOK], BF16, 155904)
    tst = at("tst", [128, 8, 128], BF16, 160000)
    dbc = at("dbc", [128, INTER], BF16, 139264)
    sczb = at("sczb", [128, 32, 32], F32, 139264)
    zrzb = at("zrzb", [128, 32, 32], F32, 143360)
    fbb = at("fbb", [128, HID], BF16, 147456)
    outst = [at(f"outst{i}", [128, 512], F32, 155648 + 2048 * i) for i in range(2)]
    x1p = [at(f"x1p{i}", [128, 512], F32, 159744 + 2048 * i) for i in range(2)]
    tst2 = at("tst2", [128, 16, 128], BF16, 147456)
    oq = [at(f"oq{i}", [128, 512], F32, 118784 + 2048 * i) for i in range(2)]
    w1b = [at(f"w1b{i}", [128, NKT, 128], BF16, 163840 + 8192 * i) for i in range(2)]
    dxb = at("dxb", [128, INTER], BF16, 163840)
    w2p = [at(f"w2p{i}", [128, 8, 512], BF16, 163840 + 8192 * i) for i in range(2)]
    dtsp = at("dtsp", [128, CS], F32, 155648)
    dtt1 = at("dtt1", [128, CS], F32, 156672)
    dtt2 = at("dtt2", [128, CS], F32, 157696)
    cumc = at("cumc", [128, CS], F32, 158720)
    wch = at("wch", [128, CS], F32, 159744)
    wtok2 = at("wtok2", [128, 2, 128], F32, 160768)
    cumt = at("cumt", [128, 2, 128], F32, 161792)
    ectt = at("ectt", [128, 2, 128], F32, 162816)
    wtok = at("wtok", [128, 2, 128], F32, 129024)
    sbtok = at("sbtok", [128, 2, 128], F32, 130048)
    scraw = at("scraw", [128, CS + 128], BF16, 131072)
    decb = [at(f"decb{i}", [128, CS + 128], BF16, 131840 + 768 * i) for i in range(3)]
    schb = [at(f"schb{i}", [128, CS + 128], BF16, 134144 + 768 * i) for i in range(3)]
    yzbuf = at("yzbuf", [128, 512], BF16, 136448)
    cdecc = at("cdecc", [128, 128], F32, 137472)
    cdrow = at("cdrow", [1, 128], F32, 137984)
    rstd8 = at("rstd8", [128, 8], F32, 138496)
    cdecc2 = at("cdecc2", [128, 128], F32, 138528)
    wtokP = [wtok2, wtok]
    cdeccP = [cdecc, cdecc2]
    bwb = [at(f"bwb{i}", [128, 2, 16, 128], BF16, 98304 + 8192 * i) for i in range(2)]
    w2st = [at(f"w2st{i}", [128, 1024], F32, 131072 + 4096 * i) for i in range(2)]
    w2cv = [at(f"w2cv{i}", [128, 1024], BF16, 118784 + 2048 * i) for i in range(2)]

    psum = [psb(f"pp{i}", [128, 512], F32) for i in range(8)]

    sem_names = ["xc", "xl", "w1a", "w1b", "spt", "sps", "sp2", "hs0", "hs1",
                 "bt0", "bt1", "bc0", "bc1", "gt0", "gt1", "cmw", "cmr",
                 "w2a", "w2b", "xp0", "xp1", "x1w", "x1r", "h2w", "h2r",
                 "qw0", "qw1", "fbr", "oq0", "oq1", "x1q0", "x1q1", "dout", "dou2", "dbg", "wc0", "wc1", "wcw", "wcx", "cv1", "cv2", "cv3",
                 "te", "ve", "se", "ge"]
    sems = {n: es.enter_context(nc.semaphore(n)) for n in sem_names}

    def program(ctx: Ctx):
        te_ = ctx.eng("tensor")
        ve_ = ctx.eng("vector")
        se_ = ctx.eng("scalar")
        ge_ = ctx.eng("gpsimd")
        sy_ = ctx.eng("sync")
        pool = BankPool(ctx, psum)

        def MM(out, lhsT, rhs, start, stop, inc=None, k=1, tp=None):
            i = None
            if te_ is not None:
                i = te_.matmul(out, lhsT, rhs, start=start, stop=stop,
                               skip_group_check=True, tile_position=tp)
            if inc:
                ctx.inc(i, inc, k)

        def TR(out, in_, ident, inc=None, k=1):
            i = None
            if te_ is not None:
                i = te_.transpose(out, in_, ident)
            if inc:
                ctx.inc(i, inc, k)

        def V(fn, *a, inc=None, k=1, **kw):
            i = fn(ve_, *a, **kw) if ve_ is not None else None
            if inc:
                ctx.inc(i, inc, k)

        def SC(fn, *a, inc=None, k=1, **kw):
            i = fn(se_, *a, **kw) if se_ is not None else None
            if inc:
                ctx.inc(i, inc, k)

        def GP(fn, *a, inc=None, k=1, **kw):
            i = fn(ge_, *a, **kw) if ge_ is not None else None
            if inc:
                ctx.inc(i, inc, k)

        def DMA(eng, out, in_, sem, engobj=None):
            e = ctx.eng(eng)
            i = e.dma_start(out=out, in_=in_) if e is not None else None
            return ctx.inc(i, sem, 16)

        g = ge_
        i1 = g.memset(ident16[:, :], 0.0) if g else None
        ctx.inc(i1, "ge", 1)
        ctx.wait("gpsimd", "ge", ctx.v("ge"))
        i1 = g.affine_select(out=ident16[:, :], in_=ident16[:, :],
                             compare_op=ALU.not_equal, fill=1.0, base=0,
                             pattern=[[-1, 128]], channel_multiplier=1) if g else None
        ctx.inc(i1, "ge", 1)
        i1 = g.memset(ident32[:, :], 0.0) if g else None
        ctx.inc(i1, "ge", 1)
        ctx.wait("gpsimd", "ge", ctx.v("ge"))
        i1 = g.affine_select(out=ident32[:, :], in_=ident32[:, :],
                             compare_op=ALU.not_equal, fill=1.0, base=0,
                             pattern=[[-1, 128]], channel_multiplier=1) if g else None
        ctx.inc(i1, "ge", 1)
        i1 = g.memset(tril16[:, :], 1.0) if g else None
        ctx.inc(i1, "ge", 1)
        ctx.wait("gpsimd", "ge", ctx.v("ge"))
        i1 = g.affine_select(out=tril16[:, :], in_=tril16[:, :],
                             compare_op=ALU.is_ge, fill=0.0, base=0,
                             pattern=[[1, 128]], channel_multiplier=-1) if g else None
        ctx.inc(i1, "ge", 1)
        i1 = g.memset(mneg32[:, :], 0.0) if g else None
        ctx.inc(i1, "ge", 1)
        ctx.wait("gpsimd", "ge", ctx.v("ge"))
        i1 = g.affine_select(out=mneg32[:, :], in_=mneg32[:, :],
                             compare_op=ALU.is_ge, fill=-1e30, base=0,
                             pattern=[[1, 128]], channel_multiplier=-1) if g else None
        ctx.inc(i1, "ge", 1)
        GE_CONST = ctx.v("ge")

        V(lambda e: e.memset(onesrow[:, :], 1.0), inc="ve")
        V(lambda e: e.memset(epscol[:, :], EPS), inc="ve")
        V(lambda e: e.memset(onecol[:, :], 1.0), inc="ve")
        VE_ONES = ctx.v("ve")

        for (dst, src) in [
            (conww[:, :, :], dram["convw"]),
            (conbb[:, :], dram["convb"]),
            (dtbcol[:, :], dram["dtb"]),
            (acol[:, :], dram["alog"]),
            (dvcol[:, :], dram["dvec"]),
            (smaskc[:, :], dram["smask"]),
            (nws[:, :], dram["nw"]),
        ]:
            DMA("sync", dst, src, "xc")
        DX_CONST = ctx.v("xc")

        ctx.wait("scalar", "xc", DX_CONST)
        SC(lambda e: e.activation(acol[:, :], acol[:, :], AF.Exp), inc="se")
        ctx.wait("scalar", "se", ctx.v("se"))
        SC(lambda e: e.mul(acol[:, :], acol[:, :], -1.0), inc="se")
        SE_ACOL = ctx.v("se")

        ctx.wait("tensor", "xc", DX_CONST)
        ctx.wait("tensor", "ve", VE_ONES)
        ctx.wait("tensor", "ge", GE_CONST)
        bk = pool.acquire("tensor")
        TR(psum[bk][0:1, 0:128], dvcol[:, 0:1], ident32[:, :], inc="te")
        ctx.wait("vector", "te", ctx.v("te"))
        V(lambda e: e.tensor_copy(drowp[0:1, 0:128], psum[bk][0:1, 0:128]), inc="ve")
        pool.release(bk, "ve", ctx.v("ve"))
        VE_DROW = ctx.v("ve")

        CV1 = 0

        def emit_convert_piece(i):
            if i < 64:
                i1 = ge_.dma_start(out=dram["w2b16"][128 * i:128 * (i + 1), :],
                                   in_=dram["w2t"][128 * i:128 * (i + 1), :]) if ge_ else None
                ctx.inc(i1, "cv2", 16)
            else:
                j = i - 64
                i1 = ge_.dma_start(out=dram["qwb16"][128 * j:128 * (j + 1), :],
                                   in_=dram["qw"][128 * j:128 * (j + 1), :]) if ge_ else None
                ctx.inc(i1, "cv3", 16)

        def build_dbc():
            for jj in range(16):
                bkd = pool.acquire("tensor")
                ctx.wait("tensor", "ve", VE_DROW)
                rhs = drowp[0:1, 8 * jj:8 * jj + 8].unsqueeze(2).broadcast_to([1, 8, 64])
                MM(psum[bkd][:, 0:512].rearrange("p (a b) -> p a b", a=8),
                   onesrow[0:1, :], rhs, True, True, inc="te")
                ctx.wait("vector", "te", ctx.v("te"))
                V(lambda e, jj=jj, bkd=bkd: e.tensor_copy(
                    dbc[:, 512 * jj:512 * (jj + 1)], psum[bkd][:, 0:512]), inc="ve")
                pool.release(bkd, "ve", ctx.v("ve"))

        def build_fbb():
            for jj in range(8):
                ctx.wait("sync", "te", ctx.v("te"))
                DMA("sync", rowst[0:1, 0:512],
                    dram["fbias"][0:1, 512 * jj:512 * (jj + 1)], "fbr")
                bkd = pool.acquire("tensor")
                ctx.wait("tensor", "fbr", ctx.v("fbr"))
                MM(psum[bkd][:, 0:512], onesrow[0:1, :], rowst[0:1, 0:512],
                   True, True, inc="te")
                ctx.wait("vector", "te", ctx.v("te"))
                V(lambda e, jj=jj, bkd=bkd: e.tensor_copy(
                    fbb[:, 512 * jj:512 * (jj + 1)], psum[bkd][:, 0:512]), inc="ve")
                pool.release(bkd, "ve", ctx.v("ve"))

        def barrier():
            snap = {s: ctx.v(s) for s in sem_names}
            for en in ("sync", "tensor", "vector", "scalar", "gpsimd"):
                for s, v in snap.items():
                    ctx.wait(en, s, v)

        def stageA_ln(xdram):
            HT = biga[:, 0:NKT * TOKH].rearrange("p (kt t) -> p kt t", kt=NKT)
            xload_v = [None] * 17
            xln_done = [0] * 17
            hbuf_free = [0] * 17

            for tt in range(17):
                b = tt % 2
                if tt >= 1:
                    ctx.wait("sync", "se", xln_done[tt - 1])
                xload_v[tt] = DMA("sync", xbuf[:, :],
                                  xdram[128 * tt:128 * (tt + 1), :], "xl")
                ctx.wait("vector", "xl", xload_v[tt])
                ctx.wait("scalar", "xl", xload_v[tt])
                if tt >= 1:
                    ctx.wait("scalar", "te", hbuf_free[tt - 1])
                c0 = 4 * b
                V(lambda e: e.tensor_reduce(colb[:, c0:c0 + 1], xbuf[:, :],
                                            mybir.AxisListType.X, ALU.add), inc="ve")
                SC(lambda e: e.activation(hbuf[:, :], xbuf[:, :], AF.Square,
                                          accum_out=colb[:, c0 + 1:c0 + 2]), inc="se")
                ctx.wait("vector", "se", ctx.v("se"))
                ctx.wait("vector", "ve", ctx.v("ve"))
                V(lambda e: e.tensor_scalar(colb[:, c0:c0 + 1], colb[:, c0:c0 + 1],
                                            1.0 / HID, None, ALU.mult), inc="ve")
                ctx.wait("vector", "ve", ctx.v("ve"))
                V(lambda e: e.scalar_tensor_tensor(colb[:, c0 + 1:c0 + 2],
                                                   colb[:, c0 + 1:c0 + 2], 1.0 / HID,
                                                   colb[:, c0:c0 + 1], ALU.mult,
                                                   ALU.bypass), inc="ve")
                ctx.wait("vector", "ve", ctx.v("ve"))
                V(lambda e: e.tensor_tensor(colb[:, c0 + 2:c0 + 3], colb[:, c0:c0 + 1],
                                            colb[:, c0:c0 + 1], ALU.mult), inc="ve")
                ctx.wait("vector", "ve", ctx.v("ve"))
                V(lambda e: e.tensor_tensor(colb[:, c0 + 1:c0 + 2], colb[:, c0 + 1:c0 + 2],
                                            colb[:, c0 + 2:c0 + 3], ALU.subtract), inc="ve")
                VVAR = ctx.inc(None, "ve", 0)
                ctx.wait("scalar", "ve", ctx.v("ve"))
                SC(lambda e: e.activation(colb[:, c0 + 1:c0 + 2], colb[:, c0 + 1:c0 + 2],
                                          AF.Ln, bias=epscol[:, 0:1]), inc="se")
                ctx.wait("scalar", "se", ctx.v("se"))
                SC(lambda e: e.activation(colb[:, c0 + 1:c0 + 2], colb[:, c0 + 1:c0 + 2],
                                          AF.Exp, scale=-0.5), inc="se")
                ctx.wait("vector", "se", ctx.v("se"))
                V(lambda e: e.tensor_scalar(colb[:, c0 + 2:c0 + 3], colb[:, c0:c0 + 1],
                                            colb[:, c0 + 1:c0 + 2], -1.0,
                                            ALU.mult, ALU.mult), inc="ve")
                ctx.wait("scalar", "ve", ctx.v("ve"))
                SC(lambda e: e.activation(hbuf[:, :], xbuf[:, :], AF.Identity,
                                          bias=colb[:, c0 + 2:c0 + 3],
                                          scale=colb[:, c0 + 1:c0 + 2]), inc="se")
                xln_done[tt] = ctx.v("se")
                ctx.wait("tensor", "se", ctx.v("se"))
                for rr in range(4):
                    bk2 = pool.acquire("tensor")
                    pb = psum[bk2][:, :].bitcast(BF16)
                    for kk in range(8):
                        kt = rr * 8 + kk
                        TR(pb[:, 128 * kk:128 * (kk + 1)],
                           hbuf[:, 128 * kt:128 * (kt + 1)], ident16[:, :],
                           inc="te" if kk == 7 else None)
                    ctx.wait("vector", "te", ctx.v("te"))
                    V(lambda e: e.tensor_copy(
                        HT[:, 8 * rr:8 * rr + 8, 128 * tt:128 * (tt + 1)],
                        pb[:, 0:1024].rearrange("p (a b) -> p a b", a=8)), inc="ve")
                    pool.release(bk2, "ve", ctx.v("ve"))
                hbuf_free[tt] = ctx.v("te")
            return ctx.v("ve")

        def stageB_proj(is_main, ht_ready=0):
            HT = biga[:, 0:NKT * TOKH].rearrange("p (kt t) -> p kt t", kt=NKT)
            ctx.wait("tensor", "ve", ht_ready)
            w1dram = dram["w1t"].rearrange("(kt kl) c -> kl kt c", kl=128)
            hs_d = dram["hstok"] if is_main else dram["hstok_p"]
            bt_d = dram["btok"] if is_main else dram["btok_p"]

            if is_main:
                tiles = ([("hs", HS_T0 + i, i) for i in range(64)]
                         + [("b", B_T0 + i, i) for i in range(8)]
                         + [("c", C_T0 + i, i) for i in range(8)]
                         + [("dt", DT_T0, 0)]
                         + [("gate", GATE_T0 + i, i) for i in range(64)])
            else:
                tiles = ([("hs", HS_T0 + i, i) for i in range(64)]
                         + [("b", B_T0 + i, i) for i in range(8)]
                         + [("dt", DT_T0, 0)])

            w1sem = ["w1a", "w1b"]
            w1_used_at = [0, 0]
            w1_ready = [0, 0]
            for i in range(min(2, len(tiles))):
                kind, ct, rel = tiles[i]
                ctx.wait("gpsimd", "te", w1_used_at[i])
                w1_ready[i] = DMA("gpsimd", w1b[i][:, :, :],
                                  w1dram[:, :, 128 * ct:128 * (ct + 1)], w1sem[i])

            spstp = [spst, spstB]
            xbcst_free = [("ve", 0)]
            convo_free = [("se", 0)]
            spst_free = [[("te", 0), ("sps", 0)], [("te", 0), ("sps", 0)]]
            tstf = [0]
            pending = [None]

            def make_tail(kind, rel, par, silu_se):
                def tail():
                    sp = spstp[par]
                    ctx.wait("tensor", "se", silu_se)
                    if kind == "hs":
                        dstd = hs_d.rearrange("(tt tl) c -> tl tt c", tl=128)[:, :, 128 * rel:128 * (rel + 1)]
                    elif kind == "b":
                        dstd = bt_d.rearrange("(tt tl) c -> tl tt c", tl=128)[:, :, 128 * rel:128 * (rel + 1)]
                    else:
                        dstd = dram["gate"].rearrange("(tt tl) c -> tl tt c", tl=128)[:, :, 128 * rel:128 * (rel + 1)]
                    for rr in range(2):
                        bk3 = pool.acquire("tensor")
                        pb = psum[bk3][:, :].bitcast(BF16)
                        for kk in range(8):
                            ttt = rr * 8 + kk
                            TR(pb[:, 128 * kk:128 * (kk + 1)],
                               sp[:, 128 * ttt:128 * (ttt + 1)], ident16[:, :],
                               inc="te" if kk == 7 else None)
                        if rr == 1:
                            spst_free[par][0] = ("te", ctx.v("te"))
                        ctx.wait("vector", "te", ctx.v("te"))
                        ctx.wait("vector", "spt", tstf[0])
                        V(lambda e, pb=pb: e.tensor_copy(
                            tst[:, :, :],
                            pb[:, 0:1024].rearrange("p (a b) -> p a b", a=8)), inc="ve")
                        pool.release(bk3, "ve", ctx.v("ve"))
                        ctx.wait("sync", "ve", ctx.v("ve"))
                        tstf[0] = DMA("sync", dstd[:, 8 * rr:8 * rr + 8, :],
                                      tst[:, :, :], "spt")
                return tail

            cvn = [0]
            for idx, (kind, ct, rel) in enumerate(tiles):
                if not is_main:
                    for _ in range(2):
                        if cvn[0] < 96:
                            emit_convert_piece(cvn[0])
                            cvn[0] += 1
                wb = idx % 2
                par = idx % 2
                ctx.wait("tensor", w1sem[wb], w1_ready[wb])
                conv = kind in ("hs", "b", "c")
                if kind in ("gate", "dt"):
                    tbs = [(128 + 512 * i, 512) for i in range(4)]
                else:
                    tbs = [(0, 512), (512, 512), (1024, 512), (1536, 512), (2048, 128)]
                unit_banks = []
                for (t0, tw) in tbs:
                    bk2 = pool.acquire("tensor")
                    for kt in range(NKT):
                        MM(psum[bk2][:, 0:tw], w1b[wb][:, kt, :], HT[:, kt, t0:t0 + tw],
                           kt == 0, kt == NKT - 1,
                           inc="te" if kt == NKT - 1 else None)
                    unit_banks.append((bk2, t0, tw, ctx.v("te")))
                w1_used_at[wb] = ctx.v("te")
                if idx + 2 < len(tiles):
                    kind2, ct2, _ = tiles[idx + 2]
                    ctx.wait("gpsimd", "te", w1_used_at[wb])
                    w1_ready[wb] = DMA("gpsimd", w1b[wb][:, :, :],
                                       w1dram[:, :, 128 * ct2:128 * (ct2 + 1)], w1sem[wb])
                if pending[0] is not None:
                    pending[0]()
                    pending[0] = None

                if kind == "dt":
                    for (bk2, t0, tw, tev) in unit_banks:
                        ctx.wait("vector", "te", tev)
                        V(lambda e, bk2=bk2, t0=t0, tw=tw:
                          e.tensor_copy(dtbuf[:, t0:t0 + tw], psum[bk2][:, 0:tw]),
                          inc="ve")
                        pool.release(bk2, "ve", ctx.v("ve"))
                    continue

                if kind == "gate":
                    ctx.wait("scalar", spst_free[par][0][0], spst_free[par][0][1])
                    ctx.wait("scalar", spst_free[par][1][0], spst_free[par][1][1])
                else:
                    ctx.wait("scalar", xbcst_free[0][0], xbcst_free[0][1])
                for (bk2, t0, tw, tev) in unit_banks:
                    ctx.wait("scalar", "te", tev)
                    if kind == "gate":
                        SC(lambda e, bk2=bk2, t0=t0, tw=tw, par=par:
                           e.activation(spstp[par][:, t0 - 128:t0 - 128 + tw],
                                        psum[bk2][:, 0:tw], AF.Silu), inc="se")
                    else:
                        SC(lambda e, bk2=bk2, t0=t0, tw=tw:
                           e.activation(xbcst[:, t0:t0 + tw], psum[bk2][:, 0:tw],
                                        AF.Copy), inc="se")
                    pool.release(bk2, "se", ctx.v("se"))
                EVAC_SE = ctx.v("se")

                if conv:
                    ctx.wait("vector", "se", EVAC_SE)
                    ctx.wait("vector", convo_free[0][0], convo_free[0][1])
                    cti = (ct - HS_T0)
                    V(lambda e: e.tensor_scalar(convo[:, :], xbcst[:, 125:125 + TOK],
                                                conww[:, cti, 0:1], None, ALU.mult),
                      inc="ve")
                    for kk in range(1, 4):
                        ctx.wait("vector", "ve", ctx.v("ve"))
                        V(lambda e, kk=kk: e.scalar_tensor_tensor(
                            convo[:, :], xbcst[:, 125 + kk:125 + kk + TOK],
                            conww[:, cti, kk:kk + 1], convo[:, :], ALU.mult, ALU.add),
                          inc="ve")
                    xbcst_free[0] = ("ve", ctx.v("ve"))
                    ctx.wait("scalar", "ve", ctx.v("ve"))
                    ctx.wait("scalar", spst_free[par][0][0], spst_free[par][0][1])
                    ctx.wait("scalar", spst_free[par][1][0], spst_free[par][1][1])
                    SC(lambda e, par=par: e.activation(spstp[par][:, :], convo[:, :],
                                                       AF.Silu,
                                                       bias=conbb[:, cti:cti + 1]),
                       inc="se")
                    convo_free[0] = ("se", ctx.v("se"))
                SILU_SE = ctx.v("se")

                if kind in ("c", "b"):
                    ctx.wait("sync", "se", SILU_SE)
                    chd = dram["cch"] if kind == "c" else dram["bch"]
                    spsem = ["sps", "sp2"][par]
                    dv = DMA("sync", chd[128 * rel:128 * (rel + 1), :],
                             spstp[par][:, :], spsem)
                    spst_free[par][1] = (spsem, dv)
                    if kind == "c":
                        continue
                pending[0] = make_tail(kind, rel, par, SILU_SE)
            if pending[0] is not None:
                pending[0]()
                pending[0] = None
            return

        def dt_pipeline(c, is_main):
            wtokc = wtokP[c % 2]
            cdc = cdeccP[c % 2]
            ctx.wait("scalar", "ve", ctx.v("ve"))
            ctx.wait("scalar", "te", ctx.v("te"))
            ctx.wait("vector", "se", ctx.v("se"))
            ctx.wait("vector", "te", ctx.v("te"))
            ctx.wait("vector", "cmw", ctx.v("cmw"))
            t0 = 128 + c * CS
            src = dtbuf[:, t0:t0 + CS]
            ctx.wait("scalar", "ve", dt_ready[0])
            SC(lambda e: e.activation(dtt1[:, :], src, AF.Abs, bias=dtbcol[:, 0:1]), inc="se")
            ctx.wait("scalar", "se", ctx.v("se"))
            SC(lambda e: e.activation(dtt1[:, :], dtt1[:, :], AF.Exp, scale=-1.0), inc="se")
            ctx.wait("scalar", "se", ctx.v("se"))
            SC(lambda e: e.activation(dtt1[:, :], dtt1[:, :], AF.Ln, bias=onecol[:, 0:1]), inc="se")
            ctx.wait("scalar", "se", ctx.v("se"))
            SC(lambda e: e.activation(dtt2[:, :], src, AF.Relu, bias=dtbcol[:, 0:1]), inc="se")
            ctx.wait("vector", "se", ctx.v("se"))
            V(lambda e: e.tensor_tensor(dtsp[:, :], dtt1[:, :], dtt2[:, :], ALU.add), inc="ve")
            ctx.wait("vector", "ve", ctx.v("ve"))
            ctx.wait("vector", "se", SE_ACOL)
            V(lambda e: e.tensor_scalar(dtt1[:, :], dtsp[:, :], acol[:, 0:1], None,
                                        ALU.mult), inc="ve")
            ctx.wait("vector", "ve", ctx.v("ve"))
            V(lambda e: e.tensor_tensor_scan(cumc[:, :], dtt1[:, :], dtt1[:, :], 0.0,
                                             ALU.add, ALU.bypass), inc="ve")
            VE_CUM = ctx.v("ve")
            ctx.wait("scalar", "ve", VE_CUM)
            SC(lambda e: e.activation(dtt2[:, :], cumc[:, :], AF.Exp, scale=-1.0,
                                      bias=cumc[:, CS - 1:CS]), inc="se")
            ctx.wait("vector", "se", ctx.v("se"))
            V(lambda e: e.tensor_tensor(wch[:, :], dtsp[:, :], dtt2[:, :], ALU.mult), inc="ve")
            VE_W = ctx.v("ve")
            if is_main:
                SC(lambda e: e.activation(dtt1[:, :], dtsp[:, :], AF.Ln), inc="se")
                ctx.wait("vector", "se", ctx.v("se"))
                ctx.wait("vector", "ve", ctx.v("ve"))
                V(lambda e: e.tensor_tensor(dtt2[:, :], dtt1[:, :], cumc[:, :],
                                            ALU.subtract), inc="ve")
            VE_SB = ctx.v("ve")
            ctx.wait("tensor", "ve", ctx.v("ve"))
            trs = [(cumc, cumt), (wch, wtokc)] + ([(dtt2, sbtok)] if is_main else [])
            for (srcb, dstb) in trs:
                bk = pool.acquire("tensor")
                for jt in range(2):
                    TR(psum[bk][:, 128 * jt:128 * (jt + 1)],
                       srcb[:, 128 * jt:128 * (jt + 1)], ident32[:, :],
                       inc="te" if jt == 1 else None)
                ctx.wait("vector", "te", ctx.v("te"))
                V(lambda e, dstb=dstb, bk=bk: e.tensor_copy(
                    dstb[:, :, :].rearrange("p a b -> p (a b)"), psum[bk][:, 0:256]),
                  inc="ve")
                pool.release(bk, "ve", ctx.v("ve"))
            VE_TOK = ctx.v("ve")
            if is_main:
                ctx.wait("scalar", "ve", VE_TOK)
                SC(lambda e: e.activation(
                    ectt[:, :, :].rearrange("p a b -> p (a b)"),
                    cumt[:, :, :].rearrange("p a b -> p (a b)"), AF.Exp), inc="se")
            bk = pool.acquire("tensor")
            ctx.wait("tensor", "ve", VE_CUM)
            TR(psum[bk][0:1, 0:128], cumc[:, CS - 1:CS], ident32[:, :], inc="te")
            ctx.wait("scalar", "te", ctx.v("te"))
            SC(lambda e, bk=bk: e.activation(cdrow[0:1, :], psum[bk][0:1, 0:128],
                                             AF.Exp), inc="se")
            pool.release(bk, "se", ctx.v("se"))
            bk = pool.acquire("tensor")
            ctx.wait("tensor", "se", ctx.v("se"))
            MM(psum[bk][:, 0:128], onesrow[0:1, :], cdrow[0:1, :], True, True, inc="te")
            ctx.wait("vector", "te", ctx.v("te"))
            V(lambda e, bk=bk, cdc=cdc: e.tensor_copy(cdc[:, :], psum[bk][:, 0:128]),
              inc="ve")
            pool.release(bk, "ve", ctx.v("ve"))
            if is_main:
                ctx.wait("sync", "cmr", ctx.v("cmr"))
                ctx.wait("sync", "ve", VE_CUM)
                DMA("sync", dram["cumhbm"][:, :], cumc[:, :], "cmw")
            return {"ve": ctx.v("ve"), "se": ctx.v("se"), "cmw": ctx.v("cmw")}

        dt_ready = [0]

        w2c_state = {"ld": [0, 0], "cv": [0, 0], "st": [0, 0]}

        def w2conv_load(i):
            kt, cq = i // 4, i % 4
            par = i % 2
            ldsem = ["wc0", "wc1"][par]
            ctx.wait("sync", "se", w2c_state["cv"][par])
            w2c_state["ld"][par] = DMA(
                "sync", w2st[par][:, :],
                dram["w2t"][128 * kt:128 * (kt + 1), 1024 * cq:1024 * (cq + 1)], ldsem)

        def w2conv_piece(i):
            kt, cq = i // 4, i % 4
            par = i % 2
            stsem = ["wcw", "wcx"][par]
            ldsem = ["wc0", "wc1"][par]
            ctx.wait("scalar", ldsem, w2c_state["ld"][par])
            ctx.wait("scalar", stsem, w2c_state["st"][par])
            SC(lambda e, par=par: e.activation(w2cv[par][:, :], w2st[par][:, :],
                                               AF.Identity), inc="se")
            w2c_state["cv"][par] = ctx.v("se")
            ctx.wait("scalar", "se", ctx.v("se"))
            w2c_state["st"][par] = DMA(
                "scalar", dram["w2b16"][128 * kt:128 * (kt + 1), 1024 * cq:1024 * (cq + 1)],
                w2cv[par][:, :], stsem)

        def ssd_prefix():
            V(lambda e: e.memset(S[:, :], 0.0), inc="ve")
            barrier()
            dt_pipeline(0, is_main=False)
            for c in range(NCH):
                barrier()
                t0 = c * CS
                v_hs = [DMA("sync", hsr[it][:, :],
                            dram["hstok_p"][t0 + 128 * it:t0 + 128 * (it + 1), :],
                            ["hs0", "hs1"][it]) for it in range(2)]
                v_bt = [DMA("sync", btokr[jt][:, :],
                            dram["btok_p"][t0 + 128 * jt:t0 + 128 * (jt + 1), :],
                            ["bt0", "bt1"][jt]) for jt in range(2)]
                if c + 1 < NCH:
                    dt_pipeline(c + 1, is_main=False)
                ssd_states_and_update(c, v_hs, v_bt, is_main=False)
            barrier()
            ctx.wait("vector", "xc", DX_CONST)
            V(lambda e: e.tensor_scalar(S[:, :], S[:, :], smaskc[:, 0:1], None,
                                        ALU.mult), inc="ve")

        def ssd_states_and_update(c, v_hs, v_bt, is_main):
            wtokc = wtokP[c % 2]
            cdc = cdeccP[c % 2]
            """states matmuls for all 128 heads + S update (batched Bw, pipelined)."""
            VE_PRE = ctx.v("ve")
            bw_used = [0, 0]
            bw_ready = [0, 0]

            def emit_bw(gg):
                par = gg % 2
                for jt in range(2):
                    ctx.wait("vector", "ve", VE_PRE)
                    ctx.wait("vector", ["bt0", "bt1"][jt], v_bt[jt])
                    ctx.wait("vector", "te", bw_used[par])
                    ctx.wait("vector", "ve", ctx.v("ve"))
                    V(lambda e, par=par, gg=gg, jt=jt: e.tensor_tensor(
                        bwb[par][:, jt, :, :],
                        btokr[jt][:, 128 * gg:128 * (gg + 1)].unsqueeze(1)
                        .broadcast_to([128, 16, 128]),
                        wtokc[:, jt, 16 * gg:16 * (gg + 1)].unsqueeze(2)
                        .broadcast_to([128, 16, 128]), ALU.mult), inc="ve")
                bw_ready[par] = ctx.v("ve")

            emit_bw(0)
            for gg in range(G):
                par = gg % 2
                st_banks = [pool.acquire("tensor") for _ in range(2)]
                ctx.wait("tensor", "ve", bw_ready[par])
                if gg == 0:
                    ctx.wait("tensor", "hs0", v_hs[0])
                    ctx.wait("tensor", "hs1", v_hs[1])
                for jt in range(2):
                    for hslot in range(16):
                        hh = 16 * gg + hslot
                        stb = st_banks[hslot // 8]
                        MM(psum[stb][:, 64 * (hslot % 8):64 * (hslot % 8 + 1)],
                           bwb[par][:, jt, hslot, :],
                           hsr[jt][:, 64 * hh:64 * (hh + 1)],
                           (jt == 0 and hslot % 8 == 0), (jt == 1 and hslot % 8 == 7),
                           inc="te" if (hslot % 8 == 7) else None)
                bw_used[par] = ctx.v("te")
                TE_ST = ctx.v("te")
                if gg + 1 < G:
                    emit_bw(gg + 1)
                ctx.wait("vector", "te", TE_ST)
                ctx.wait("vector", "ve", ctx.v("ve"))
                V(lambda e, gg=gg: e.tensor_tensor(
                    S[:, 1024 * gg:1024 * (gg + 1)].rearrange("p (a b) -> p a b", a=16),
                    S[:, 1024 * gg:1024 * (gg + 1)].rearrange("p (a b) -> p a b", a=16),
                    cdc[:, 16 * gg:16 * (gg + 1)].unsqueeze(2)
                    .broadcast_to([128, 16, 64]), ALU.mult), inc="ve")
                for hg in range(2):
                    stb = st_banks[hg]
                    ctx.wait("vector", "ve", ctx.v("ve"))
                    V(lambda e, gg=gg, hg=hg, stb=stb: e.tensor_tensor(
                        S[:, 1024 * gg + 512 * hg:1024 * gg + 512 * (hg + 1)],
                        S[:, 1024 * gg + 512 * hg:1024 * gg + 512 * (hg + 1)],
                        psum[stb][:, 0:512], ALU.add), inc="ve")
                    pool.release(stb, "ve", ctx.v("ve"))

        def ln2_tile(src_f32, dst_bf16, c0):
            V(lambda e: e.tensor_reduce(colb[:, c0:c0 + 1], src_f32,
                                        mybir.AxisListType.X, ALU.add), inc="ve")
            SC(lambda e: e.activation(dst_bf16, src_f32, AF.Square,
                                      accum_out=colb[:, c0 + 1:c0 + 2]), inc="se")
            ctx.wait("vector", "se", ctx.v("se"))
            ctx.wait("vector", "ve", ctx.v("ve"))
            V(lambda e: e.tensor_scalar(colb[:, c0:c0 + 1], colb[:, c0:c0 + 1],
                                        1.0 / HID, None, ALU.mult), inc="ve")
            ctx.wait("vector", "ve", ctx.v("ve"))
            V(lambda e: e.scalar_tensor_tensor(colb[:, c0 + 1:c0 + 2],
                                               colb[:, c0 + 1:c0 + 2], 1.0 / HID,
                                               colb[:, c0:c0 + 1], ALU.mult,
                                               ALU.bypass), inc="ve")
            ctx.wait("vector", "ve", ctx.v("ve"))
            V(lambda e: e.tensor_tensor(colb[:, c0 + 2:c0 + 3], colb[:, c0:c0 + 1],
                                        colb[:, c0:c0 + 1], ALU.mult), inc="ve")
            ctx.wait("vector", "ve", ctx.v("ve"))
            V(lambda e: e.tensor_tensor(colb[:, c0 + 1:c0 + 2], colb[:, c0 + 1:c0 + 2],
                                        colb[:, c0 + 2:c0 + 3], ALU.subtract), inc="ve")
            ctx.wait("scalar", "ve", ctx.v("ve"))
            SC(lambda e: e.activation(colb[:, c0 + 1:c0 + 2], colb[:, c0 + 1:c0 + 2],
                                      AF.Ln, bias=epscol[:, 0:1]), inc="se")
            ctx.wait("scalar", "se", ctx.v("se"))
            SC(lambda e: e.activation(colb[:, c0 + 1:c0 + 2], colb[:, c0 + 1:c0 + 2],
                                      AF.Exp, scale=-0.5), inc="se")
            ctx.wait("vector", "se", ctx.v("se"))
            V(lambda e: e.tensor_scalar(colb[:, c0 + 2:c0 + 3], colb[:, c0:c0 + 1],
                                        colb[:, c0 + 1:c0 + 2], -1.0,
                                        ALU.mult, ALU.mult), inc="ve")
            ctx.wait("scalar", "ve", ctx.v("ve"))
            SC(lambda e: e.activation(dst_bf16, src_f32, AF.Identity,
                                      bias=colb[:, c0 + 2:c0 + 3],
                                      scale=colb[:, c0 + 1:c0 + 2]), inc="se")
            return ctx.v("se")

        def out_quarter(q):
            barrier()
            w2dram = dram["w2b16"].rearrange("(kt kl) c -> kl kt c", kl=128)
            w2sem = ["w2a", "w2b"]
            w2_ready = [0, 0]
            w2_used = [0, 0]
            xpsem = ["xp0", "xp1"]
            xp_ready = [0, 0]
            xp_used = [0, 0]
            oqsem = ["oq0", "oq1"]
            oq_st = [0, 0]
            oq_used = [0, 0]

            def load_w2(slot, hb, kt8):
                ctx.wait("sync", "te", w2_used[slot])
                w2_ready[slot] = DMA(
                    "sync", w2p[slot][:, :, :],
                    w2dram[:, 8 * kt8:8 * kt8 + 8, 512 * hb:512 * (hb + 1)],
                    w2sem[slot])

            xunits = [(hb, tq) for hb in range(8) for tq in range(4)]

            def load_xp(slot, hb, tq):
                ctx.wait("sync", "ve", xp_used[slot])
                xp_ready[slot] = DMA(
                    "sync", xpc[slot][:, :],
                    dram["x_main"][HALO + 512 * q + 128 * tq:
                                   HALO + 512 * q + 128 * (tq + 1),
                                   512 * hb:512 * (hb + 1)], xpsem[slot])

            units = [(hb, kt8) for hb in range(8) for kt8 in range(8)]
            load_w2(0, 0, 0)
            load_w2(1, 0, 1)
            for i in range(2):
                load_xp(i, *xunits[i])
            for ui, (hb, kt8) in enumerate(units):
                slot = ui % 2
                ctx.wait("tensor", w2sem[slot], w2_ready[slot])
                if kt8 == 0:
                    hb_banks = [pool.acquire("tensor") for _ in range(4)]
                for tq in range(4):
                    for kl in range(8):
                        kt = 8 * kt8 + kl
                        MM(psum[hb_banks[tq]][:, 0:512],
                           gTacc[:, kt, 128 * tq:128 * (tq + 1)],
                           w2p[slot][:, kl, :], kt == 0, kt == 63,
                           inc="te" if (kl == 7 and tq == 3) else None)
                w2_used[slot] = ctx.v("te")
                if ui + 2 < len(units):
                    load_w2(slot, *units[ui + 2])
                if kt8 == 7:
                    TE_HB = ctx.v("te")
                    for tq in range(4):
                        ci = hb * 4 + tq
                        xslot = ci % 2
                        opar = ci % 2
                        ctx.wait("vector", "te", TE_HB)
                        ctx.wait("vector", xpsem[xslot], xp_ready[xslot])
                        ctx.wait("vector", oqsem[opar], oq_used[opar])
                        V(lambda e, tq=tq, xslot=xslot, opar=opar: e.tensor_tensor(
                            oq[opar][:, :], psum[hb_banks[tq]][:, 0:512],
                            xpc[xslot][:, :], ALU.add), inc="ve")
                        xp_used[xslot] = ctx.v("ve")
                        pool.release(hb_banks[tq], "ve", ctx.v("ve"))
                        if ci + 2 < len(xunits):
                            load_xp(xslot, *xunits[ci + 2])
                        ctx.wait("sync", "ve", ctx.v("ve"))
                        oq_used[opar] = DMA(
                            "sync",
                            dram["x1s"][512 * q + 128 * tq:512 * q + 128 * (tq + 1),
                                        512 * hb:512 * (hb + 1)],
                            oq[opar][:, :], oqsem[opar])
            ctx.wait("sync", oqsem[0], oq_used[0])
            ctx.wait("sync", oqsem[1], oq_used[1])
            x1qsem = ["x1q0", "x1q1"]
            x1q_ready = [0, 0]
            for tq in range(4):
                tok0 = 512 * q + 128 * tq
                par = tq % 2
                if tq >= 2:
                    ctx.wait("sync", "se", x1q_done[par])
                x1q_ready[par] = DMA("sync", x1q[par][:, :],
                                     dram["x1s"][tok0:tok0 + 128, :], x1qsem[par])
                ctx.wait("vector", x1qsem[par], x1q_ready[par])
                ctx.wait("scalar", x1qsem[par], x1q_ready[par])
                if tq >= 1:
                    ctx.wait("scalar", "te", h2_tpose_done[0])
                se_h2 = ln2_tile(x1q[par][:, :], hbuf[:, :], 8)
                x1q_done[par] = se_h2
                ctx.wait("tensor", "se", se_h2)
                for half in range(2):
                    for rr in range(2):
                        bk2 = pool.acquire("tensor")
                        pb = psum[bk2][:, :].bitcast(BF16)
                        for kk in range(8):
                            kt = 16 * half + rr * 8 + kk
                            TR(pb[:, 128 * kk:128 * (kk + 1)],
                               hbuf[:, 128 * kt:128 * (kt + 1)], ident16[:, :],
                               inc="te" if kk == 7 else None)
                        ctx.wait("vector", "te", ctx.v("te"))
                        ctx.wait("vector", "h2w", h2w_done[0])
                        V(lambda e, rr=rr, half=half, pb=pb: e.tensor_copy(
                            tst2[:, 8 * rr:8 * rr + 8, :],
                            pb[:, 0:1024].rearrange("p (a b) -> p a b", a=8)), inc="ve")
                        pool.release(bk2, "ve", ctx.v("ve"))
                        if rr == 1:
                            ctx.wait("sync", "ve", ctx.v("ve"))
                            h2w_done[0] = DMA(
                                "sync",
                                dram["h2t"].rearrange("(kh kl) t -> kl kh t", kl=128)
                                [:, 16 * half:16 * half + 16, tok0:tok0 + 128],
                                tst2[:, :, :], "h2w")
                h2_tpose_done[0] = ctx.v("te")

        x1q_done = [0, 0]
        h2_tpose_done = [0]
        h2w_done = [0]

        def ssd_main():
            barrier()
            build_dbc()
            if debug:
                import os as _os
                if _os.environ.get("DBG_STOP") == "dbc":
                    ctx.wait("gpsimd", "ve", ctx.v("ve"))
                    DMA("gpsimd", dram["dbg2"][0:128, :], dbc[:, :], "dbg")
                    return
            chain_out = {0: dt_pipeline(0, is_main=True)}
            for c in range(NCH):
                barrier()
                t0 = c * CS
                v_hs = [DMA("sync", hsr[it][:, :],
                            dram["hstok"][t0 + 128 * it:t0 + 128 * (it + 1), :],
                            ["hs0", "hs1"][it]) for it in range(2)]
                v_bt = [DMA("sync", btokr[jt][:, :],
                            dram["btok"][t0 + 128 * jt:t0 + 128 * (jt + 1), :],
                            ["bt0", "bt1"][jt]) for jt in range(2)]
                VE_DT = chain_out[c]["ve"]
                SE_DT = chain_out[c]["se"]
                cmw_v = chain_out[c]["cmw"]
                for it in range(2):
                    ctx.wait("vector", ["hs0", "hs1"][it], v_hs[it])
                    ctx.wait("vector", "te", dxb_free[0])
                    V(lambda e, it=it: e.tensor_tensor(dxb[:, :], hsr[it][:, :],
                                                       dbc[:, :], ALU.mult), inc="ve")
                    VE_DX = ctx.v("ve")
                    for gg in range(G):
                        par = gg % 2
                        ctx.wait("sync", "te", bc_used[par])
                        vbc = DMA("sync", bchr[par][:, :],
                                  dram["bch"][128 * gg:128 * (gg + 1), t0:t0 + CS],
                                  ["bc0", "bc1"][par])
                        vbc = DMA("sync", cchr[par][:, :],
                                  dram["cch"][128 * gg:128 * (gg + 1), t0:t0 + CS],
                                  ["bc0", "bc1"][par])
                        ctx.wait("sync", gt_used[par][0], gt_used[par][1])
                        vgt = DMA("sync", gater[par][:, :],
                                  dram["gate"][t0 + 128 * it:t0 + 128 * (it + 1),
                                               1024 * gg:1024 * (gg + 1)],
                                  ["gt0", "gt1"][par])
                        bsc = pool.acquire("tensor", which=0)
                        ctx.wait("tensor", ["bc0", "bc1"][par], vbc)
                        njt = it + 1
                        MM(psum[0][:, 0:128], bchr[par][:, 0:128],
                           cchr[par][:, 128 * it:128 * (it + 1)], True, njt == 1,
                           inc="te" if njt == 1 else None)
                        if it == 1:
                            MM(psum[0][:, 128:256], bchr[par][:, 128:256],
                               cchr[par][:, 128:256], False, True, inc="te")
                        bc_used[par] = ctx.v("te")
                        TE_SC = ctx.v("te")
                        ctx.wait("vector", "te", TE_SC)
                        V(lambda e, it=it: e.tensor_copy(
                            scraw[:, 0:128 * (it + 1)],
                            psum[0][:, 0:128 * (it + 1)]), inc="ve")
                        pool.release(0, "ve", ctx.v("ve"))
                        VE_SCRAW = ctx.v("ve")
                        yb = [4, 5]
                        zb = [6, 7]
                        TE_DB = [0, 0, 0]

                        def dec_stage1(hp):
                            dpar = hp % 3
                            hh = 2 * hp
                            if hh % 4 == 0:
                                ctx.wait("sync", "te", cumr_used[0])
                                ctx.wait("sync", "cmw", cmw_v)
                                sq = hh // 4
                                cum_flat = dram["cumhbm"].rearrange("h t -> (h t)")
                                cumr_v[0] = DMA("sync", cumr[0:1, :],
                                                cum_flat[256 * 4 * sq:256 * 4 * (sq + 1)].unsqueeze(0),
                                                "cmr")
                            pool.acquire("tensor", which=1 + dpar)
                            ctx.wait("tensor", "cmr", cumr_v[0])
                            base = (hh % 4) * 256 + 128 * it
                            crow2 = cumr[0:1, base:base + 256 + 128].rearrange(
                                "o (a b) -> o a b", a=3)[:, 0::2, :]
                            if it == 0:
                                MM(psum[1 + dpar][:, 0:256], onesrow[0:1, :], crow2,
                                   True, False)
                                MM(psum[1 + dpar][:, 0:256], ident32[:, :],
                                   mneg32[:, :].unsqueeze(1).broadcast_to([128, 2, 128]),
                                   False, True, inc="te")
                            else:
                                MM(psum[1 + dpar][:, 0:256], onesrow[0:1, :], crow2,
                                   True, False)
                                MM(psum[1 + dpar][:, 256:512], onesrow[0:1, :], crow2,
                                   False, False)
                                MM(psum[1 + dpar][:, 256:512], ident32[:, :],
                                   mneg32[:, :].unsqueeze(1).broadcast_to([128, 2, 128]),
                                   False, True, inc="te")
                            cumr_used[0] = ctx.v("te")
                            TE_DB[dpar] = ctx.v("te")

                        def dec_stage2(hh, hg):
                            dpar = (hh // 2) % 3
                            odd = hh % 2
                            hh8 = hh % 8
                            ctx.wait("scalar", "te", TE_DB[dpar])
                            ctx.wait("scalar", "ve", VE_DT)
                            ctx.wait("scalar", "ve", dec_v_used[dpar])
                            for jt in range(it + 1):
                                if it == 0:
                                    po = 128 * odd
                                else:
                                    po = 128 * odd + (256 if jt == 1 else 0)
                                SC(lambda e, dpar=dpar, jt=jt, hh=hh, po=po: e.activation(
                                    decb[dpar][:, 128 * jt:128 * (jt + 1)],
                                    psum[1 + dpar][:, po:po + 128], AF.Exp,
                                    bias=sbtok[:, jt, hh:hh + 1]), inc="se")
                            if odd == 1:
                                pool.release(1 + dpar, "se", ctx.v("se"))
                                dec_used[dpar] = 0
                            SE_DEC = ctx.v("se")
                            ctx.wait("vector", "se", SE_DEC)
                            ctx.wait("vector", "ve", VE_SCRAW)
                            ctx.wait("vector", "te", sch_used[hh % 3])
                            ctx.wait("vector", "ve", dec_v_used[dpar])
                            V(lambda e, dpar=dpar, it=it, hh=hh: e.tensor_tensor(
                                schb[hh % 3][:, 0:128 * (it + 1)],
                                scraw[:, 0:128 * (it + 1)],
                                decb[dpar][:, 0:128 * (it + 1)], ALU.mult),
                              inc="ve")
                            dec_v_used[dpar] = ctx.v("ve")
                            SCH_V = ("ve", ctx.v("ve"))
                            ctx.wait("tensor", SCH_V[0], SCH_V[1])
                            for jt in range(it + 1):
                                ctx.wait("tensor", ["hs0", "hs1"][jt], v_hs[jt])
                                MM(psum[yb[hg]][:, 64 * hh8:64 * (hh8 + 1)],
                                   schb[hh % 3][:, 128 * jt:128 * (jt + 1)],
                                   hsr[jt][:, 64 * hh:64 * (hh + 1)],
                                   False, False, inc="te")
                            sch_used[hh % 3] = ctx.v("te")
                        dec_stage1(8 * gg)
                        dec_stage1(8 * gg + 1)
                        for hg in range(2):
                            pool.acquire("tensor", which=yb[hg])
                            pool.acquire("tensor", which=zb[hg])
                            h0 = 16 * gg + 8 * hg
                            ctx.wait("tensor", "ve", VE_DX)
                            MM(psum[yb[hg]][:, 0:512], ident16[:, :],
                               dxb[:, 64 * h0:64 * h0 + 512], True, False, inc="te")
                            ctx.wait("tensor", "ve", s_ready[0])
                            MM(psum[zb[hg]][:, 0:512],
                               cchr[par][:, 128 * it:128 * (it + 1)],
                               S[:, 64 * h0:64 * h0 + 512], True, True, inc="te")
                            bc_used[par] = ctx.v("te")
                            for hh8 in range(8):
                                hh = h0 + hh8
                                if hh % 2 == 1 and hh + 3 < 16 * (gg + 1):
                                    dec_stage1((hh + 3) // 2)
                                dec_stage2(hh, hg)
                            TE_YZ = ctx.v("te")
                            ctx.wait("vector", "te", TE_YZ)
                            ctx.wait("vector", "se", SE_DT)
                            ctx.wait("vector", "ve", ctx.v("ve"))
                            V(lambda e, hg=hg, it=it, h0=h0: e.tensor_tensor(
                                yzbuf[:, 0:512].rearrange("p (a b) -> p a b", a=8),
                                psum[zb[hg]][:, 0:512].rearrange("p (a b) -> p a b", a=8),
                                ectt[:, it, h0:h0 + 8].unsqueeze(2).broadcast_to([128, 8, 64]),
                                ALU.mult), inc="ve")
                            pool.release(zb[hg], "ve", ctx.v("ve"))
                            ctx.wait("vector", "ve", ctx.v("ve"))
                            V(lambda e, hg=hg, h0=h0: e.scalar_tensor_tensor(
                                g0b[:, 64 * h0:64 * h0 + 512],
                                psum[yb[hg]][:, 0:512], 1.0, yzbuf[:, 0:512],
                                ALU.mult, ALU.add), inc="ve")
                            pool.release(yb[hg], "ve", ctx.v("ve"))
                            G0_V = ctx.v("ve")
                            ctx.wait("vector", ["gt0", "gt1"][par], vgt)
                            ctx.wait("vector", "ve", G0_V)
                            V(lambda e, h0=h0, hg=hg, par=par: e.tensor_tensor(
                                g0b[:, 64 * h0:64 * h0 + 512],
                                g0b[:, 64 * h0:64 * h0 + 512],
                                gater[par][:, 512 * hg:512 * (hg + 1)], ALU.mult),
                              inc="ve")
                            gt_used[par] = ("ve", ctx.v("ve"))
                    ctx.wait("scalar", "ve", ctx.v("ve"))
                    for gg in range(G):
                        SC(lambda e, gg=gg: e.activation(
                            gater[0][:, :], g0b[:, 1024 * gg:1024 * (gg + 1)],
                            AF.Square, accum_out=colb[:, 16 + gg:17 + gg]), inc="se")
                        ctx.wait("scalar", "se", ctx.v("se"))
                    gt_used[0] = ("se", ctx.v("se"))
                    SC(lambda e: e.activation(rstd8[:, :], colb[:, 16:24], AF.Ln,
                                              scale=1.0 / 1024.0,
                                              bias=epscol[:, 0:1]), inc="se")
                    ctx.wait("scalar", "se", ctx.v("se"))
                    SC(lambda e: e.activation(rstd8[:, :], rstd8[:, :], AF.Exp,
                                              scale=-0.5), inc="se")
                    SE_RSTD = ctx.v("se")
                    ctx.wait("vector", "se", SE_RSTD)
                    ctx.wait("vector", "te", dxb_free[0])
                    for gg in range(G):
                        V(lambda e, gg=gg: e.tensor_scalar(
                            dxb[:, 1024 * gg:1024 * (gg + 1)],
                            g0b[:, 1024 * gg:1024 * (gg + 1)],
                            rstd8[:, gg:gg + 1], None, ALU.mult), inc="ve")
                    VE_G = ctx.v("ve")
                    if debug:
                        import os as _os
                        _what = _os.environ.get("DBG_G", "g")
                        ctx.wait("sync", "ve", VE_G)
                        if _what == "g":
                            DMA("sync", dram["gdbg"][t0 + 128 * it:t0 + 128 * (it + 1), :],
                                dxb[:, :], "dout")
                        elif _what == "g0":
                            DMA("sync", dram["gdbg"][t0 + 128 * it:t0 + 128 * (it + 1), :],
                                g0b[:, :], "dout")
                        if False:
                            pass
                    ctx.wait("tensor", "ve", VE_G)
                    tloc = (c % 2) * 2 + it
                    for rr in range(8):
                        bk2 = pool.acquire("tensor")
                        pb = psum[bk2][:, :].bitcast(BF16)
                        for kk in range(8):
                            kt = rr * 8 + kk
                            TR(pb[:, 128 * kk:128 * (kk + 1)],
                               dxb[:, 128 * kt:128 * (kt + 1)], ident16[:, :],
                               inc="te" if kk == 7 else None)
                        ctx.wait("vector", "te", ctx.v("te"))
                        for kk in range(8):
                            kt = rr * 8 + kk
                            V(lambda e, kt=kt, kk=kk, pb=pb, tloc=tloc: e.tensor_scalar(
                                gTacc[:, kt, 128 * tloc:128 * (tloc + 1)],
                                pb[:, 128 * kk:128 * (kk + 1)],
                                nws[:, kt:kt + 1], None, ALU.mult), inc="ve")
                        pool.release(bk2, "ve", ctx.v("ve"))
                    dxb_free[0] = ctx.v("te")
                if c % 2 == 0 and c + 1 < NCH:
                    chain_out[c + 1] = dt_pipeline(c + 1, is_main=True)
                ssd_states_and_update(c, v_hs, v_bt, is_main=True)
                s_ready[0] = ctx.v("ve")
                if c % 2 == 1:
                    out_quarter(c // 2)
                    if c < NCH - 1:
                        barrier()
                        build_dbc()
                        chain_out[c + 1] = dt_pipeline(c + 1, is_main=True)

        dxb_free = [0]
        bc_used = [0, 0]
        gt_used = [("ve", 0), ("ve", 0)]
        cumr_used = [0]
        cumr_v = [0]
        dec_used = [0, 0, 0]
        dec_g_used = [0, 0, 0]
        dec_v_used = [0, 0, 0]
        sch_used = [0, 0, 0]
        s_ready = [0]

        def ffn():
            barrier()
            build_fbb()
            DMA("sync", sczb[:, :, :], dram["scz"], "xc")
            DMA("sync", zrzb[:, :, :], dram["zrz"], "xc")
            XC_SZ = ctx.v("xc")
            qwsem = ["qw0", "qw1"]
            qw_ready = [0, 0]
            qw_used = [0, 0]
            x1sem = ["x1r", "fbr"]
            h2dram = dram["h2t"].rearrange("(kt kl) t -> kl kt t", kl=128)
            for half in range(2):
                ctx.wait("sync", "te", h2th_used[0])
                vh2 = DMA("sync", h2th[:, :, :],
                          h2dram[:, :, 1024 * half:1024 * (half + 1)], "h2r")
                for i in range(2):
                    ctx.wait("sync", qwsem[i], qw_ready[i])
                    ctx.wait("sync", "ve", qw_used[i])
                    ctx.wait("sync", "cv3", ctx.v("cv3"))
                    qw_ready[i] = DMA("sync", qwb[i][:, :],
                                      dram["qwb16"][128 * i:128 * (i + 1), :],
                                      qwsem[i])
                for ob in range(8):
                    for otl in range(4):
                        ot = 4 * ob + otl
                        slot = ot % 2
                        ctx.wait("vector", qwsem[slot], qw_ready[slot])
                        ctx.wait("vector", "xc", XC_SZ)
                        ctx.wait("vector", "te", wdq_used[0])
                        ctx.wait("vector", "ve", ctx.v("ve"))
                        V(lambda e, slot=slot, ot=ot: e.tensor_tensor(
                            wdq[:, :].rearrange("p (a b) -> p a b", a=32),
                            qwb[slot][:, :].rearrange("p (a b) -> p a b", a=32),
                            sczb[:, ot, :].unsqueeze(2).broadcast_to([128, 32, 128]),
                            ALU.mult), inc="ve")
                        ctx.wait("vector", "ve", ctx.v("ve"))
                        V(lambda e, ot=ot: e.tensor_tensor(
                            wdq[:, :].rearrange("p (a b) -> p a b", a=32),
                            wdq[:, :].rearrange("p (a b) -> p a b", a=32),
                            zrzb[:, ot, :].unsqueeze(2).broadcast_to([128, 32, 128]),
                            ALU.add), inc="ve")
                        qw_used[slot] = ctx.v("ve")
                        VE_DQ = ctx.v("ve")
                        if ot + 2 < 32 or half == 0:
                            nt = (ot + 2) % 32
                            ctx.wait("sync", "ve", qw_used[slot])
                            qw_ready[slot] = DMA("sync", qwb[slot][:, :],
                                                 dram["qwb16"][128 * nt:128 * (nt + 1), :],
                                                 qwsem[slot])
                        ctx.wait("tensor", "ve", VE_DQ)
                        ctx.wait("tensor", "te", wT_used[0])
                        for rr in range(4):
                            bk2 = pool.acquire("tensor")
                            pb = psum[bk2][:, :].bitcast(BF16)
                            for kk in range(8):
                                kt = rr * 8 + kk
                                TR(pb[:, 128 * kk:128 * (kk + 1)],
                                   wdq[:, 128 * kt:128 * (kt + 1)], ident16[:, :],
                                   inc="te" if kk == 7 else None)
                            ctx.wait("scalar", "te", ctx.v("te"))
                            SC(lambda e, rr=rr, otl=otl, pb=pb: e.activation(
                                wT[:, 8 * rr:8 * rr + 8, 128 * otl:128 * (otl + 1)],
                                pb[:, 0:1024].rearrange("p (a b) -> p a b", a=8),
                                AF.Identity), inc="se")
                            pool.release(bk2, "se", ctx.v("se"))
                        wdq_used[0] = ctx.v("te")
                    WT_V = ctx.v("se")
                    ctx.wait("tensor", "se", WT_V)
                    ctx.wait("tensor", "h2r", vh2)
                    for tt in range(8):
                        bk2 = pool.acquire("tensor")
                        for kt in range(NKT):
                            MM(psum[bk2][:, 0:512], h2th[:, kt, 128 * tt:128 * (tt + 1)],
                               wT[:, kt, :], kt == 0, kt == NKT - 1,
                               inc="te" if kt == NKT - 1 else None)
                        h2th_used[0] = ctx.v("te")
                        wT_used[0] = ctx.v("te")
                        TE_O = ctx.v("te")
                        par = tt % 2
                        ctx.wait("sync", "ve", x1p_used[par])
                        vx1 = DMA("sync", x1p[par][:, :],
                                  dram["x1s"][1024 * half + 128 * tt:
                                              1024 * half + 128 * (tt + 1),
                                              512 * ob:512 * (ob + 1)],
                                  ["x1r", "fbr"][par])
                        ctx.wait("vector", "te", TE_O)
                        ctx.wait("vector", ["dout", "dou2"][par], outst_used[par])
                        V(lambda e, bk2=bk2, ob=ob, par=par: e.tensor_tensor(
                            outst[par][:, :], psum[bk2][:, 0:512],
                            fbb[:, 512 * ob:512 * (ob + 1)], ALU.add), inc="ve")
                        pool.release(bk2, "ve", ctx.v("ve"))
                        ctx.wait("vector", "ve", ctx.v("ve"))
                        ctx.wait("vector", ["x1r", "fbr"][par], vx1)
                        V(lambda e, par=par: e.tensor_tensor(
                            outst[par][:, :], outst[par][:, :], x1p[par][:, :],
                            ALU.add), inc="ve")
                        x1p_used[par] = ctx.v("ve")
                        VE_OUT = ctx.v("ve")
                        ctx.wait("sync", "ve", VE_OUT)
                        dv = DMA("sync",
                                 dram["out"][1024 * half + 128 * tt:
                                             1024 * half + 128 * (tt + 1),
                                             512 * ob:512 * (ob + 1)],
                                 outst[par][:, :], ["dout", "dou2"][par])
                        outst_used[par] = dv

        h2th_used = [0]
        wdq_used = [0]
        wT_used = [0]
        x1p_used = [0, 0]
        outst_used = [0, 0]

        if stages >= 2:
            htr = stageA_ln(dram["x_pref"])
            stageB_proj(is_main=False, ht_ready=htr)
            dt_ready[0] = ctx.v("ve")
            ssd_prefix()
            if debug:
                barrier()
                DMA("sync", dram["sdbg"][:, :], S[:, :], "dout")
            barrier()

        htr = stageA_ln(dram["x_main"])
        stageB_proj(is_main=True, ht_ready=htr)
        dt_ready[0] = ctx.v("ve")
        if debug:
            V(lambda e: e.memset(dtbuf[:, 0:128], 0.0), inc="ve")
            ctx.wait("sync", "ve", ctx.v("ve"))
            ctx.wait("sync", "se", ctx.v("se"))
            DMA("sync", dram["dtdbg"][:, :], dtbuf[:, :], "dout")
        if stages >= 3:
            ssd_main()
        if stages >= 4:
            ffn()

        for s in sem_names:
            if s not in ("te", "ve", "se", "ge"):
                ctx.wait("sync", s, ctx.v(s))

    with nc.Block() as block:
        @block.sync
        def _(sync):
            program(Ctx(nc, "sync", sems))

        @block.tensor
        def _(tensor):
            program(Ctx(nc, "tensor", sems))

        @block.vector
        def _(vector):
            program(Ctx(nc, "vector", sems))

        @block.scalar
        def _(scalar):
            program(Ctx(nc, "scalar", sems))

        @block.gpsimd
        def _(gpsimd):
            program(Ctx(nc, "gpsimd", sems))

    es.close()
    return nc


def stage_inputs(inputs):
    x = np.ascontiguousarray(inputs["L0_attn_input_states"])
    w1t = np.ascontiguousarray(inputs["L0_attn_in_proj_w"].T)
    w2t = np.ascontiguousarray(inputs["L0_attn_out_w"].T)
    qw = np.ascontiguousarray(inputs["L0_ffn_qweight"].reshape(HID, HID))
    scz = np.ascontiguousarray(
        inputs["L0_ffn_scales"][:, :, 0].reshape(32, 128, 32).transpose(1, 0, 2))
    zrz = np.ascontiguousarray(
        inputs["L0_ffn_scaled_zeros"][:, :, 0].reshape(32, 128, 32).transpose(1, 0, 2))
    convw = np.ascontiguousarray(
        inputs["L0_attn_conv_w"][:, 0, :].reshape(80, 128, 4).transpose(1, 0, 2))
    convb = np.ascontiguousarray(
        inputs["L0_attn_conv_b"].reshape(80, 128).T)
    dtb = np.ascontiguousarray(inputs["L0_attn_dt_bias"].reshape(H, 1))
    alog = np.ascontiguousarray(inputs["L0_attn_A_log"].reshape(H, 1))
    dvec = np.ascontiguousarray(inputs["L0_attn_D"].reshape(H, 1))
    nw = np.ascontiguousarray(inputs["L0_attn_norm_w"].reshape(64, 128).T)
    fbias = np.ascontiguousarray(inputs["L0_ffn_bias"].reshape(1, HID))

    in_maps = []
    for core in range(8):
        bi, hi = core // 2, core % 2
        xm = np.zeros((TOKH, HID), np.float32)
        xm[HALO:] = x[bi, hi * TOK:(hi + 1) * TOK]
        if hi == 1:
            xm[:HALO] = x[bi, TOK - HALO:TOK]
        xp = np.zeros((TOKH, HID), np.float32)
        if hi == 1:
            xp[HALO:] = x[bi, 0:TOK]
        smask = np.full((128, 1), float(hi), np.float32)
        in_maps.append({
            "x_main": xm, "x_pref": xp, "w1t": w1t, "w2t": w2t, "qw": qw,
            "scz": scz, "zrz": zrz, "convw": convw, "convb": convb,
            "dtb": dtb, "alog": alog, "dvec": dvec, "nw": nw, "fbias": fbias,
            "smask": smask,
        })
    return in_maps


def kernel(**inputs):
    in_maps = stage_inputs(inputs)
    nc = build_program(debug=False, stages=4)
    res = run_bass_kernel_spmd(nc, in_maps, core_ids=list(range(8)))
    out = np.empty((BATCH, SEQ, HID), np.float32)
    for core in range(8):
        bi, hi = core // 2, core % 2
        out[bi, hi * TOK:(hi + 1) * TOK] = res.results[core]["out"]
    return out
